# version 09
# baseline (speedup 1.0000x reference)
import contextlib
import math
import numpy as np
import concourse.bass as bass
import concourse.mybir as mybir
from concourse.bass_utils import run_bass_kernel_spmd

F32 = mybir.dt.float32
BF16 = mybir.dt.bfloat16
FP8 = mybir.dt.float8e4
AF = mybir.ActivationFunctionType
ALU = mybir.AluOpType

B, T, D = 4, 4096, 1024
NCORE = 8
ROWS = T // 2


def _p2c():
    p = np.arange(1024)
    hp = p // 512
    j = (p % 512) // 64
    q = p % 64
    return q * 16 + (2 * j + hp)

P2C = _p2c()
C2P = np.argsort(P2C)


def _l1_qk_cols():
    cols = []
    for h in range(16):
        cols += [h * 64 + 2 * j for j in range(16)]
    for h in range(16):
        cols += [h * 64 + 2 * j + 1 for j in range(16)]
    for h in range(16):
        cols += [h * 64 + d for d in range(32, 64)]
    return np.array(cols)

L1_QK_COLS = _l1_qk_cols()


def _inner_qk_cols():
    cols = []
    for h in range(16):
        cols += [h * 64 + d for d in range(32)]
    for h in range(16):
        cols += [h * 64 + d for d in range(32, 64)]
    return np.array(cols)

INNER_QK_COLS = _inner_qk_cols()


def rope_l0attn(pos):
    p = np.arange(512)
    i = 8 * (p % 64) + p // 64
    inv = (1.0 / 10000.0) ** (i.astype(np.float64) / 512.0)
    ang = np.asarray(pos, dtype=np.float64)[None, :] * inv[:, None]
    return np.cos(ang).astype(np.float32), np.sin(ang).astype(np.float32)


def rope_inner(pos):
    p = np.arange(512)
    i = (p % 32).astype(np.float64)
    inv = (1.0 / 10000.0) ** (2.0 * i / 64.0)
    bs = (2.0 * i + 0.4 * 64.0) / (1.4 * 64.0)
    t = np.asarray(pos, dtype=np.float64)
    ang = t[None, :] * inv[:, None]
    power = (t[None, :] - T // 2) / 128.0
    scale = bs[:, None] ** power
    c, s = np.cos(ang), np.sin(ang)
    return ((8.0 * c * scale).astype(np.float32), (8.0 * s * scale).astype(np.float32),
            (c / scale).astype(np.float32), (s / scale).astype(np.float32))


def rope_l1(pos):
    p = np.arange(256)
    j = (p % 16).astype(np.float64)
    inv = (1.0 / 10000.0) ** (2.0 * j / 32.0)
    ang = np.asarray(pos, dtype=np.float64)[None, :] * inv[:, None]
    return np.cos(ang).astype(np.float32), np.sin(ang).astype(np.float32)


class Sems:
    def __init__(self, nc, ctx, names):
        self.h = {}
        self.n = {}
        for nm in names:
            self.h[nm] = ctx.enter_context(nc.semaphore(nm))
            self.n[nm] = 0

    def inc(self, name, instr, k=1):
        instr.then_inc(self.h[name], k)
        self.n[name] += k
        return self.n[name]

    def chain(self, eng, name, instr):
        c = self.inc(name, instr)
        eng.wait_ge(self.h[name], c)


def chunked(total, step):
    out = []
    o = 0
    while o < total:
        out.append((o, min(step, total - o)))
        o += step
    return out



def emit_row_bcast(nc, sems, onesr, jobs, pb2, tag):
    base_ev = sems.n["bc_ev"]
    with nc.Block(tag) as block:
        @block.tensor
        def _(tensor):
            for i, (src, dst) in enumerate(jobs):
                if i >= 2:
                    tensor.wait_ge(sems.h["bc_ev"], base_ev + i - 1)
                L = src.shape[-1]
                sems.inc("bc_mm", tensor.matmul(
                    pb2[:, (i % 2) * 512:(i % 2) * 512 + L], onesr[:, :], src,
                    start=True, stop=True))

        @block.scalar
        def _(scalar):
            base = sems.n["bc_mm"] - len(jobs)
            for i, (src, dst) in enumerate(jobs):
                L = src.shape[-1]
                scalar.wait_ge(sems.h["bc_mm"], base + i + 1)
                sems.inc("bc_ev", scalar.copy(
                    dst, pb2[:, (i % 2) * 512:(i % 2) * 512 + L]))


def emit_ln(nc, sems, xs, n, sq, stat, ones_col, onesr, ps, pb2, Ab, Bb,
            tag, x_off=0, chunk_stride=None, eps_col=None, chunk_ready=None):
    cs = chunk_stride if chunk_stride is not None else n
    offs = x_off if callable(x_off) else (lambda c: c * cs + x_off)
    M = stat[:, 0:n]
    W = stat[:, n:2 * n]

    with nc.Block(tag + "_s") as block:
        mm_base = sems.n["mm"]
        sch_base = sems.n["sch"]
        vch_base = sems.n["vch"]
        jobs = [(c, t0, tl) for c in range(8) for (t0, tl) in chunked(n, 512)]

        @block.scalar
        def _(scalar):
            for j, (c, t0, tl) in enumerate(jobs):
                if chunk_ready is not None and t0 == 0:
                    scalar.wait_ge(*chunk_ready[c])
                if j >= 2:
                    scalar.wait_ge(sems.h["mm"], mm_base + j)
                sems.inc("sch", scalar.activation(
                    sq[:, (j % 2) * 512:(j % 2) * 512 + tl],
                    xs[:, offs(c) + t0: offs(c) + t0 + tl], AF.Square))

        @block.tensor
        def _(tensor):
            last = None
            for c in range(8):
                if chunk_ready is not None:
                    tensor.wait_ge(*chunk_ready[c])
                for t0, tl in chunked(n, 512):
                    last = tensor.matmul(ps[:, t0:t0 + tl], ones_col[:, :],
                                         xs[:, offs(c) + t0: offs(c) + t0 + tl],
                                         start=(c == 0), stop=(c == 7))
            sems.inc("mm", last)
            for j, (c, t0, tl) in enumerate(jobs):
                tensor.wait_ge(sems.h["sch"], sch_base + j + 1)
                if j == 0:
                    tensor.wait_ge(sems.h["vch"], vch_base + 1)
                sems.inc("mm", tensor.matmul(
                    ps[:, t0:t0 + tl], ones_col[:, :],
                    sq[:, (j % 2) * 512:(j % 2) * 512 + tl],
                    start=(c == 0), stop=(c == 7)))

        @block.vector
        def _(vector):
            vector.wait_ge(sems.h["mm"], mm_base + 1)
            sems.inc("vch", vector.tensor_scalar_mul(M, ps[:, 0:n], 1.0 / 1024.0))

    with nc.Block(tag + "_statmath") as block:
        @block.vector
        def _(vector):
            sems.chain(vector, "vch", vector.tensor_scalar_mul(W, ps[:, 0:n], 1.0 / 1024.0))
            sems.chain(vector, "vch", vector.tensor_tensor(Ab[0:1, 0:n], M, M, ALU.mult))
            sems.inc("ev", vector.tensor_tensor(W, W, Ab[0:1, 0:n], ALU.subtract))

        @block.scalar
        def _(scalar):
            scalar.wait_ge(sems.h["ev"], sems.n["ev"])
            sems.chain(scalar, "sch", scalar.activation(W, W, AF.Ln, bias=eps_col[:1, :]))
            sems.inc("sch", scalar.activation(W, W, AF.Exp, scale=-0.5))

        @block.vector
        def _(vector):
            vector.wait_ge(sems.h["sch"], sems.n["sch"])
            sems.chain(vector, "vch", vector.tensor_tensor(M, M, W, ALU.mult))
            vector.tensor_scalar_mul(M, M, -1.0)

    jobs = []
    for t0, tl in chunked(n, 512):
        jobs.append((W[:, t0:t0 + tl], Ab[:, t0:t0 + tl]))
        jobs.append((M[:, t0:t0 + tl], Bb[:, t0:t0 + tl]))
    emit_row_bcast(nc, sems, onesr, jobs, pb2, tag + "_bc")


def build_ln0():
    nc = bass.Bass("TRN2", target_bir_lowering=False)
    x_h = nc.dram_tensor("x", [1024, ROWS], F32, kind="ExternalInput")
    xn_h = nc.dram_tensor("xn", [1024, ROWS], F32, kind="ExternalOutput")
    n = ROWS
    with contextlib.ExitStack() as ctx:
        e = ctx.enter_context
        xs = e(nc.sbuf_tensor("xs", [128, 8 * n], F32))
        sq = e(nc.sbuf_tensor("sq", [128, 1024], F32))
        stat = e(nc.sbuf_tensor("stat", [1, 2 * n], F32))
        ones_col = e(nc.sbuf_tensor("ones_col", [128, 1], F32))
        onesr = e(nc.sbuf_tensor("onesr", [1, 128], F32))
        eps_col = e(nc.sbuf_tensor("eps_col", [1, 1], F32))
        Ab = e(nc.sbuf_tensor("Ab", [128, n], F32))
        Bb = e(nc.sbuf_tensor("Bb", [128, n], F32))
        xn = xs
        ps = e(nc.psum_tensor("ps", [1, n], F32))
        pb2 = e(nc.psum_tensor("pb2", [128, 1024], F32))
        sems = Sems(nc, ctx, ["ld", "ev", "st", "bc_mm", "bc_ev", "vch", "sch", "mm"])

        with nc.Block("ld") as block:
            @block.sync
            def _(sync):
                sems.inc("ld", sync.dma_start(
                    out=xs[:, :],
                    in_=x_h.ap().rearrange("(c p) n -> p c n", p=128)), 16)
                sync.wait_ge(sems.h["ld"], sems.n["ld"])

            @block.vector
            def _(vector):
                vector.memset(ones_col[:, :], 1.0)
                vector.memset(onesr[:, :], 1.0)
                vector.memset(eps_col[:, :], 1e-5)

        emit_ln(nc, sems, xs, n, sq, stat, ones_col, onesr, ps, pb2, Ab, Bb,
                "ln0", eps_col=eps_col)

        with nc.Block("apply") as block:
            @block.vector
            def _(vector):
                for c in range(8):
                    sems.chain(vector, "vch", vector.tensor_tensor(
                        xn[:, c * n:(c + 1) * n], xs[:, c * n:(c + 1) * n],
                        Ab[:, :], ALU.mult))
                    vector.tensor_tensor(xn[:, c * n:(c + 1) * n],
                                         xn[:, c * n:(c + 1) * n], Bb[:, :], ALU.add)

        with nc.Block("store") as block:
            @block.sync
            def _(sync):
                sems.inc("st", sync.dma_start(
                    out=xn_h.ap().rearrange("(c p) n -> p c n", p=128),
                    in_=xn[:, :]), 16)
                sync.wait_ge(sems.h["st"], sems.n["st"])
    return nc



def shard_rows(x_btd, perm=None):
    shards = []
    for core in range(NCORE):
        b, h = core // 2, core % 2
        rows = x_btd[b, h * ROWS:(h + 1) * ROWS, :]
        if perm is not None:
            rows = rows[:, perm]
        shards.append(np.ascontiguousarray(rows.T))
    return shards


def unshard_rows(shards, perm=None):
    out = np.empty((B, T, D), dtype=shards[0].dtype)
    for core in range(NCORE):
        b, h = core // 2, core % 2
        rows = np.asarray(shards[core]).T
        if perm is not None:
            inv = np.argsort(perm)
            rows = rows[:, inv]
        out[b, h * ROWS:(h + 1) * ROWS, :] = rows
    return out


def kernel(**inputs):
    raise NotImplementedError("assembled after stages are unit-tested")

CS = 257 + ROWS
NKV = 18
NIB = 16


def _qoff(c, r):
    return c * CS + (257 if c < 4 else 256) + r


def _kvoff(c, r):
    return c * CS + (1 if c < 4 else 0) + r


def _rope_pos_q():
    return None


def dram_ffn0_attn(nc, sfx="", ext=True):
    k = lambda io: ({"kind": "External" + io} if ext else {})
    return dict(
        xf=nc.dram_tensor("xf" + sfx, [1024, CS], F32, **k("Input")),
        wq=nc.dram_tensor("wq" + sfx, [1024, 1024], F32, **k("Input")),
        wk=nc.dram_tensor("wk" + sfx, [1024, 1024], F32, **k("Input")),
        wv=nc.dram_tensor("wv" + sfx, [1024, 1024], F32, **k("Input")),
        wo=nc.dram_tensor("wo" + sfx, [1024, 1024], F32, **k("Input")),
        qsc=nc.dram_tensor("qsc" + sfx, [128, 4], F32, **k("Input")),
        qsr=nc.dram_tensor("qsr" + sfx, [1, 128], F32, **k("Input")),
        rcq=nc.dram_tensor("rcq" + sfx, [512, ROWS], F32, **k("Input")),
        rsq=nc.dram_tensor("rsq" + sfx, [512, ROWS], F32, **k("Input")),
        rck=nc.dram_tensor("rck" + sfx, [512, 2304], F32, **k("Input")),
        rsk=nc.dram_tensor("rsk" + sfx, [512, 2304], F32, **k("Input")),
        tri=nc.dram_tensor("tri" + sfx, [128, 384], F32, **k("Input")),
        hpo=nc.dram_tensor("hpo" + sfx, [1, 128], F32, **k("Input")),
        ind=nc.dram_tensor("ind" + sfx, [128, 64], F32, **k("Input")),
        indt=nc.dram_tensor("indt" + sfx, [16, 512], F32, **k("Input")),
        xa=nc.dram_tensor("xa" + sfx, [1024, ROWS], F32, **k("Output")),
        hflag=nc.dram_tensor("hflag" + sfx, [1, 1], F32, **k("Input")),
        xl=nc.dram_tensor("xl" + sfx, [1024, 1], F32, **k("Output")),
    )


def emit_ffn0_attn(nc, sems, H):
    xf_h = H["xf"]
    wq_h = H["wq"]
    wk_h = H["wk"]
    wv_h = H["wv"]
    wo_h = H["wo"]
    qsc_h = H["qsc"]
    qsr_h = H["qsr"]
    rcq_h = H["rcq"]
    rsq_h = H["rsq"]
    rck_h = H["rck"]
    rsk_h = H["rsk"]
    tri_h = H["tri"]
    hpo_h = H["hpo"]
    ind_h = H["ind"]
    indt_h = H["indt"]
    xa_h = H["xa"]
    xl_h = H["xl"]
    xsrc_h = H.get("xsrc")
    xh_h = H.get("xh257")
    hflag_h = H["hflag"]

    with contextlib.ExitStack() as ctx:
        e = ctx.enter_context
        xb = e(nc.sbuf_tensor("xb_s", [128, 8 * CS], BF16))
        wsb = e(nc.sbuf_tensor("wsb_s", [128, 64 * 128], BF16))
        qtil = e(nc.sbuf_tensor("qtil_s", [128, 8 * ROWS], BF16))
        ktil = e(nc.sbuf_tensor("ktil_s", [128, 8 * 2304], BF16))
        vx = e(nc.sbuf_tensor("vx_s", [128, NKV * 1040], BF16))
        sqs = e(nc.sbuf_tensor("sqs_s", [128, 512], BF16))
        rstd = e(nc.sbuf_tensor("rstd_s", [16, 512], F32))
        rsthi = e(nc.sbuf_tensor("rsthi_s", [16, 512], BF16))
        rstlo = e(nc.sbuf_tensor("rstlo_s", [16, 512], BF16))
        bsc = e(nc.sbuf_tensor("bsc_s", [128, 1024], F32))
        outb = bsc
        rta = e(nc.sbuf_tensor("rta_s", [128, 2 * 512], F32))
        xff = rta
        rtb = e(nc.sbuf_tensor("rtb_s", [128, 2 * 512], F32))
        colbuf = e(nc.sbuf_tensor("colbuf_s", [128, 8], F32))
        srow = e(nc.sbuf_tensor("srow_s", [1, 160], F32))
        qsc = e(nc.sbuf_tensor("qsc_s", [128, 4], F32))
        tri = e(nc.sbuf_tensor("tri_s", [128, 384], BF16))
        hpo = e(nc.sbuf_tensor("hpo_s", [1, 128], BF16))
        ind = e(nc.sbuf_tensor("ind_s", [128, 64], BF16))
        indt = e(nc.sbuf_tensor("indt_s", [16, 512], BF16))
        onesr = e(nc.sbuf_tensor("onesr_s", [1, 128], F32))
        onesrb = e(nc.sbuf_tensor("onesrb_s", [1, 128], BF16))
        ones_col = e(nc.sbuf_tensor("ones_col_s", [128, 1], F32))
        ptil = e(nc.sbuf_tensor("ptil_s", [128, 4 * 384], BF16))
        attn = xb
        rdc = e(nc.sbuf_tensor("rdc_s", [128, 4], F32))
        attnT = qtil
        PA = e(nc.psum_tensor("PA", [128, 1024], F32))
        PB = e(nc.psum_tensor("PB", [128, 1024], F32))
        PC = e(nc.psum_tensor("PC", [128, 1024], F32))
        PT1 = e(nc.psum_tensor("PT1", [128, 1024], BF16))
        PT2 = e(nc.psum_tensor("PT2", [128, 1024], BF16))

        with nc.Block("a_ld") as block:
            @block.sync
            def _(sync):
                sems.inc("ld", sync.dma_start(out=srow[:, 0:128], in_=qsr_h[:, :]), 16)
                sems.inc("ld", sync.dma_start(out=qsc[:, :], in_=qsc_h[:, :]), 16)
                sems.inc("ld", sync.dma_start(out=srow[:, 140:141], in_=hflag_h[:, :]), 16)

            @block.gpsimd
            def _(g):
                if xsrc_h is None:
                    sems.inc("ldx", g.dma_start(
                        out=xb[:, :],
                        in_=xf_h.ap().rearrange("(c p) n -> p c n", p=128)), 16)
                else:
                    x0v = xb[:, 0:1]
                    sems.inc("ldx", g.dma_start(
                        out=bass.AP(x0v.tensor, x0v.offset,
                                    [list(x0v.ap[0]), [CS, 8], [1, 257]]),
                        in_=xh_h.ap().rearrange("(c p) n -> p c n", p=128)
                        [:, 0:8, :]), 16)
                    sems.inc("ldx", g.dma_start(
                        out=bass.AP(x0v.tensor, x0v.offset + 257,
                                    [list(x0v.ap[0]), [CS, 8], [1, ROWS]]),
                        in_=xsrc_h.ap().rearrange("(c p) n -> p c n", p=128)), 16)
                sems.inc("ldw", g.dma_start(out=tri[:, :], in_=tri_h[:, :]), 16)
                sems.inc("ldw", g.dma_start(out=hpo[:, :], in_=hpo_h[:, :]), 16)
                sems.inc("ldw", g.dma_start(out=ind[:, :], in_=ind_h[:, :]), 16)
                sems.inc("ldw", g.dma_start(out=indt[:, :], in_=indt_h[:, :]), 16)
                sems.inc("ldw", g.dma_start(
                    out=wsb[:, :], in_=wq_h.ap().rearrange("(kc p) m -> p kc m", p=128)), 16)

            @block.vector
            def _(vector):
                vector.memset(onesr[:, :], 1.0)
                vector.memset(onesrb[:, :], 1.0)
                vector.memset(ones_col[:, :], 1.0)
                vector.memset(colbuf[:, 7:8], 1e-24)
                sems.chain(vector, "vch", vector.memset(vx[:, :], 0.0))
                v0 = vx[:, 0:1]
                for rb in range(NKV):
                    vector.memset(bass.AP(v0.tensor, v0.offset + rb * 1040 + 64,
                                          [list(v0.ap[0]), [65, 16], [1, 1]]), 1.0)

        with nc.Block("a_hbc") as block:
            @block.tensor
            def _(tensor):
                tensor.wait_ge(sems.h["ld"], sems.n["ld"])
                tensor.matmul(PC[:, 0:1], onesr[:, :], srow[:, 140:141],
                              start=True, stop=True)

        with nc.Block("a_hbc2") as block:
            @block.scalar
            def _(scalar):
                scalar.copy(colbuf[:, 2:3], PC[:, 0:1])

        with nc.Block("a_hz") as block:
            @block.vector
            def _(vector):
                vector.wait_ge(sems.h["ldx"], sems.n["ldx"])
                x0v = xb[:, 0:1]
                hv = bass.AP(x0v.tensor, x0v.offset, [list(x0v.ap[0]), [CS, 8], [1, 257]])
                vector.tensor_scalar(hv, hv, colbuf[:, 2:3], None, ALU.mult, ALU.bypass)

        def emit_proj_B(tag, out_tile, nrows, offfn):
            rts = chunked(nrows, 512)
            jobs = [(mc, r0, rl) for mc in range(8) for (r0, rl) in rts]
            mm_base = sems.n["mm"]
            ev_base = sems.n["ev"]
            with nc.Block(tag) as block:
                @block.tensor
                def _(tensor):
                    tensor.wait_ge(sems.h["ldw"], sems.n["ldw"])
                    tensor.wait_ge(sems.h["ldx"], sems.n["ldx"])
                    for i, (mc, r0, rl) in enumerate(jobs):
                        half = (i % 2) * 512
                        if i >= 2:
                            tensor.wait_ge(sems.h["ev"], ev_base + i - 1)
                        last = None
                        for kc in range(8):
                            last = tensor.matmul(
                                PA[:, half:half + rl],
                                wsb[:, (kc * 8 + mc) * 128:(kc * 8 + mc) * 128 + 128],
                                xb[:, offfn(kc, r0):offfn(kc, r0) + rl],
                                start=(kc == 0), stop=(kc == 7))
                        sems.inc("mm", last)

                @block.scalar
                def _(scalar):
                    for i, (mc, r0, rl) in enumerate(jobs):
                        half = (i % 2) * 512
                        scalar.wait_ge(sems.h["mm"], mm_base + i + 1)
                        sems.inc("ev", scalar.copy(
                            out_tile[:, mc * nrows + r0: mc * nrows + r0 + rl],
                            PA[:, half:half + rl]))

        def emit_proj_A(tag, nrows, offfn):
            rbs = nrows // 128
            jobs = [(rb, n0) for rb in range(rbs) for n0 in (0, 512)]
            mm_base = sems.n["mm"]
            ev_base = sems.n["ev"]
            with nc.Block(tag) as block:
                @block.tensor
                def _(tensor):
                    tensor.wait_ge(sems.h["ldw"], sems.n["ldw"])
                    for i, (rb, n0) in enumerate(jobs):
                        half = (i % 2) * 512
                        if i >= 2:
                            tensor.wait_ge(sems.h["ev"], ev_base + i - 1)
                        last = None
                        for kc in range(8):
                            last = tensor.matmul(
                                PA[:, half:half + 512],
                                xb[:, offfn(kc, rb * 128):offfn(kc, rb * 128) + 128],
                                wsb[:, kc * 1024 + n0: kc * 1024 + n0 + 512],
                                start=(kc == 0), stop=(kc == 7))
                        sems.inc("mm", last)

                @block.scalar
                def _(scalar):
                    for i, (rb, n0) in enumerate(jobs):
                        half = (i % 2) * 512
                        scalar.wait_ge(sems.h["mm"], mm_base + i + 1)
                        v0 = vx[:, 0:1]
                        dst = bass.AP(v0.tensor, v0.offset + rb * 1040 + (n0 // 64) * 65,
                                      [list(v0.ap[0]), [65, 8], [1, 64]])
                        sems.inc("ev", scalar.copy(dst, PA[:, half:half + 512]))

        emit_proj_B("a_projq", qtil, ROWS, _qoff)
        with nc.Block("a_ldk") as block:
            @block.gpsimd
            def _(g):
                sems.inc("ldw", g.dma_start(
                    out=wsb[:, :], in_=wk_h.ap().rearrange("(kc p) m -> p kc m", p=128)), 16)
        emit_proj_B("a_projk", ktil, 2304, _kvoff)
        with nc.Block("a_ldv") as block:
            @block.gpsimd
            def _(g):
                sems.inc("ldw", g.dma_start(
                    out=wsb[:, :], in_=wv_h.ap().rearrange("(kc p) m -> p kc m", p=128)), 16)
        emit_proj_A("a_projv", 2304, _kvoff)
        with nc.Block("a_ldo") as block:
            @block.gpsimd
            def _(g):
                sems.inc("ldw", g.dma_start(
                    out=wsb[:, :], in_=wo_h.ap().rearrange("(kc p) m -> p kc m", p=128)), 16)

        def emit_l2norm(tag, tile, nrows, sc_x1, sc_x2):
            for ri, (r0, rl) in enumerate(chunked(nrows, 512)):
                with nc.Block(tag + "_n%d" % ri) as block:
                    vch_base = sems.n["vch"]
                    mm_base = sems.n["mm"]
                    sch_base = sems.n["sch"]
                    ev_base = sems.n["ev"]

                    @block.vector
                    def _(vector):
                        vector.wait_ge(sems.h["ld"], sems.n["ld"])
                        for c in range(8):
                            if c:
                                vector.wait_ge(sems.h["mm"], mm_base + c)
                            sems.inc("vch", vector.tensor_tensor(
                                sqs[:, 0:rl], tile[:, c * nrows + r0:c * nrows + r0 + rl],
                                tile[:, c * nrows + r0:c * nrows + r0 + rl], ALU.mult))
                        vector.wait_ge(sems.h["sch"], sch_base + 3)
                        sems.inc("vch", vector.tensor_tensor(
                            rstlo[:, 0:rl], rstd[:, 0:rl], rsthi[:, 0:rl], ALU.subtract))
                        for c in range(8):
                            sc = sc_x2 if c >= 4 else sc_x1
                            half = (c % 2) * 512
                            vector.wait_ge(sems.h["mm"], mm_base + 8 + c + 1)
                            sems.chain(vector, "vch", vector.tensor_scalar(
                                bsc[:, half:half + rl], PA[:, half:half + rl], sc,
                                None, ALU.mult, ALU.bypass))
                            sems.inc("ev", vector.tensor_tensor(
                                tile[:, c * nrows + r0:c * nrows + r0 + rl],
                                tile[:, c * nrows + r0:c * nrows + r0 + rl],
                                bsc[:, half:half + rl], ALU.mult))

                    @block.tensor
                    def _(tensor):
                        for c in range(8):
                            tensor.wait_ge(sems.h["vch"], vch_base + c + 1)
                            sems.inc("mm", tensor.matmul(
                                PB[0:16, 0:rl], ind[:, (c % 4) * 16:(c % 4) * 16 + 16],
                                sqs[:, 0:rl], start=(c == 0), stop=(c == 7)))
                        for c in range(8):
                            tensor.wait_ge(sems.h["vch"], vch_base + 9)
                            half = (c % 2) * 512
                            if c >= 2:
                                tensor.wait_ge(sems.h["ev"], ev_base + c - 1)
                            tensor.matmul(
                                PA[:, half:half + rl],
                                indt[:, (c % 4) * 128:(c % 4) * 128 + 128],
                                rsthi[:, 0:rl], start=True, stop=False)
                            sems.inc("mm", tensor.matmul(
                                PA[:, half:half + rl],
                                indt[:, (c % 4) * 128:(c % 4) * 128 + 128],
                                rstlo[:, 0:rl], start=False, stop=True))

                    @block.scalar
                    def _(scalar):
                        scalar.wait_ge(sems.h["mm"], mm_base + 8)
                        sems.chain(scalar, "sch", scalar.activation(
                            rstd[:, 0:rl], PB[0:16, 0:rl], AF.Ln,
                            bias=colbuf[0:16, 7:8]))
                        sems.chain(scalar, "sch", scalar.activation(
                            rstd[:, 0:rl], rstd[:, 0:rl], AF.Exp, scale=-0.5))
                        sems.inc("sch", scalar.copy(rsthi[:, 0:rl], rstd[:, 0:rl]))

        emit_l2norm("a_nq", qtil, ROWS, qsc[:, 0:1], qsc[:, 1:2])
        emit_l2norm("a_nk", ktil, 2304, qsc[:, 2:3], qsc[:, 3:4])

        with nc.Block("a_m") as block:
            @block.vector
            def _(vector):
                vector.wait_ge(sems.h["ld"], sems.n["ld"])
                sems.chain(vector, "vch", vector.reduce_max(
                    srow[:, 128:129], srow[:, 0:64], axis=mybir.AxisListType.X,
                    apply_absolute_value=True))
                sems.chain(vector, "vch", vector.reduce_max(
                    srow[:, 129:130], srow[:, 64:128], axis=mybir.AxisListType.X,
                    apply_absolute_value=True))
                sems.chain(vector, "vch", vector.tensor_tensor(
                    srow[:, 130:131], srow[:, 128:129], srow[:, 129:130], ALU.mult))
                sems.chain(vector, "vch", vector.tensor_scalar(
                    srow[:, 130:131], srow[:, 130:131], -8.0, None, ALU.mult, ALU.bypass))

        with nc.Block("a_mbc") as block:
            @block.tensor
            def _(tensor):
                sems.inc("mm", tensor.matmul(PB[:, 0:1], onesr[:, :],
                                             srow[:, 130:131], start=True, stop=True))

        with nc.Block("a_mbc2") as block:
            @block.scalar
            def _(scalar):
                scalar.copy(colbuf[:, 0:1], PB[:, 0:1])

        rope_jobs = []
        for c in range(4):
            for (r0, rl) in chunked(ROWS, 512):
                rope_jobs.append((qtil, ROWS, rcq_h, rsq_h, c, r0, rl))
            for (r0, rl) in chunked(2304, 512):
                rope_jobs.append((ktil, 2304, rck_h, rsk_h, c, r0, rl))
        RPC = len(rope_jobs) // 4
        ldr_base = sems.n["ldr"]
        ldr2_base = sems.n["ldr2"]
        rch_base = sems.n["rch"]

        def rope_stream(sync, j0, j1):
            for i in range(j0, j1):
                tile, nrows, ch, sh, c, r0, rl = rope_jobs[i]
                half = (i % 2) * 512
                sm = "ldr" if i % 2 == 0 else "ldr2"
                if i >= 2:
                    sync.wait_ge(sems.h["rch"], rch_base + 6 * (i - 1))
                sems.inc(sm, sync.dma_start(
                    out=rta[:, half:half + rl],
                    in_=ch[c * 128:(c + 1) * 128, r0:r0 + rl]), 16)
                sems.inc(sm, sync.dma_start(
                    out=rtb[:, half:half + rl],
                    in_=sh[c * 128:(c + 1) * 128, r0:r0 + rl]), 16)

        def rope_exec(vector, j0, j1):
            for i in range(j0, j1):
                tile, nrows, ch, sh, c, r0, rl = rope_jobs[i]
                half = (i % 2) * 512
                sm = "ldr" if i % 2 == 0 else "ldr2"
                sb = ldr_base if i % 2 == 0 else ldr2_base
                vector.wait_ge(sems.h[sm], sb + 32 * (i // 2 + 1))
                x1 = tile[:, c * nrows + r0: c * nrows + r0 + rl]
                x2 = tile[:, (4 + c) * nrows + r0: (4 + c) * nrows + r0 + rl]
                co = rta[:, half:half + rl]
                si = rtb[:, half:half + rl]
                t1 = bsc[:, 0:rl]
                t2 = bsc[:, 512:512 + rl]
                sems.chain(vector, "rch", vector.tensor_tensor(t1, x1, si, ALU.mult))
                sems.chain(vector, "rch", vector.tensor_tensor(t2, x2, si, ALU.mult))
                sems.chain(vector, "rch", vector.tensor_tensor(x1, x1, co, ALU.mult))
                sems.chain(vector, "rch", vector.tensor_tensor(x2, x2, co, ALU.mult))
                sems.chain(vector, "rch", vector.tensor_tensor(x1, x1, t2, ALU.subtract))
                sems.chain(vector, "rch", vector.tensor_tensor(x2, x2, t1, ALU.add))

        with nc.Block("a_rp0") as block:
            @block.sync
            def _(sync):
                rope_stream(sync, 0, RPC)

            @block.vector
            def _(vector):
                rope_exec(vector, 0, RPC)

        SP = [PA[:, 0:512], PA[:, 512:1024], PB[:, 0:512], PB[:, 512:1024]]
        for c in range(4):
            mm_base = sems.n["mm"]
            ex_base = sems.n["ex"]
            pv_base = sems.n["pv"]
            ev_base = sems.n["ev"]
            with nc.Block("a_att%d" % c) as block:
                if c < 3:
                    @block.sync
                    def _(sync):
                        rope_stream(sync, (c + 1) * RPC, (c + 2) * RPC)

                @block.tensor
                def _(tensor):
                    tensor.wait_ge(sems.h["rch"], rch_base + 6 * (c + 1) * RPC)
                    for ib in range(NIB + 1):
                        if ib > 0:
                            tensor.wait_ge(sems.h["ex"], ex_base + ib)
                        if ib < NIB:
                            last = None
                            poisons = [0, 1] if ib == 0 else ([0] if ib == 1 else [])
                            for hh in range(4):
                                b32 = 32 * hh
                                tp = (b32, 0)
                                for sub in range(3):
                                    bl = ib + sub
                                    tensor.matmul(
                                        SP[hh][:, sub * 128:sub * 128 + 128],
                                        ktil[b32:b32 + 32, c * 2304 + bl * 128: c * 2304 + bl * 128 + 128],
                                        qtil[b32:b32 + 32, c * ROWS + ib * 128: c * ROWS + ib * 128 + 128],
                                        start=(sub == 0), stop=False, tile_position=tp)
                                    tensor.matmul(
                                        SP[hh][:, sub * 128:sub * 128 + 128],
                                        ktil[b32:b32 + 32, (4 + c) * 2304 + bl * 128: (4 + c) * 2304 + bl * 128 + 128],
                                        qtil[b32:b32 + 32, (4 + c) * ROWS + ib * 128: (4 + c) * ROWS + ib * 128 + 128],
                                        start=False, stop=False, tile_position=tp)
                                tensor.matmul(
                                    SP[hh][:, 0:128], tri[:, 128:256], tri[:, 0:128],
                                    start=False, stop=False)
                                for sub in poisons:
                                    tensor.matmul(
                                        SP[hh][:, sub * 128:sub * 128 + 128],
                                        hpo[:, :], onesrb[:, :],
                                        start=False, stop=False)
                                last = tensor.matmul(
                                    SP[hh][:, 256:384], tri[:, 256:384], tri[:, 0:128],
                                    start=False, stop=True)
                            sems.inc("mm", last)
                        if ib > 0:
                            jb = ib - 1
                            if jb >= 2:
                                tensor.wait_ge(sems.h["ev"], ev_base + jb - 1)
                            half = (jb % 2) * 512
                            last = None
                            for hh in range(4):
                                for sub in range(3):
                                    last = tensor.matmul(
                                        PC[:, half + hh * 128: half + hh * 128 + 65],
                                        ptil[:, hh * 384 + sub * 128: hh * 384 + sub * 128 + 128],
                                        vx[:, (jb + sub) * 1040 + (4 * c + hh) * 65:
                                           (jb + sub) * 1040 + (4 * c + hh) * 65 + 65],
                                        start=(sub == 0), stop=(sub == 2))
                            sems.inc("pv", last)

                @block.scalar
                def _(scalar):
                    for ib in range(NIB):
                        scalar.wait_ge(sems.h["mm"], mm_base + ib + 1)
                        if ib > 0:
                            scalar.wait_ge(sems.h["pv"], pv_base + ib)
                        last = None
                        for hh in range(4):
                            last = scalar.activation(
                                ptil[:, hh * 384: hh * 384 + 384], SP[hh][:, 0:384],
                                AF.Exp, bias=colbuf[:, 0:1])
                        sems.inc("ex", last)

                @block.vector
                def _(vector):
                    if c < 3:
                        rope_exec(vector, (c + 1) * RPC, (c + 2) * RPC)
                    for jb in range(NIB):
                        vector.wait_ge(sems.h["pv"], pv_base + jb + 1)
                        half = (jb % 2) * 512
                        for hh in range(4):
                            sems.chain(vector, "vch", vector.reciprocal(
                                rdc[:, hh:hh + 1],
                                PC[:, half + hh * 128 + 64: half + hh * 128 + 65]))
                            sems.chain(vector, "vch", vector.tensor_scalar(
                                attn[:, jb * 1024 + (4 * c + hh) * 64:
                                     jb * 1024 + (4 * c + hh) * 64 + 64],
                                PC[:, half + hh * 128: half + hh * 128 + 64],
                                rdc[:, hh:hh + 1], None, ALU.mult, ALU.bypass))
                        sems.inc("ev", vector.engine_nop())

        with nc.Block("a_tr") as block:
            mm_base = sems.n["mm"]
            ev_base = sems.n["ev"]
            jobs = [(rb, cc) for rb in range(NIB) for cc in range(8)]

            @block.tensor
            def _(tensor):
                for i, (rb, cc) in enumerate(jobs):
                    pt = PT1 if i % 2 == 0 else PT2
                    if i >= 2:
                        tensor.wait_ge(sems.h["ev"], ev_base + i - 1)
                    sems.inc("mm", tensor.transpose(
                        pt[:, 0:128],
                        attn[:, rb * 1024 + cc * 128: rb * 1024 + cc * 128 + 128],
                        tri[:, 0:128]))

            @block.scalar
            def _(scalar):
                for i, (rb, cc) in enumerate(jobs):
                    pt = PT1 if i % 2 == 0 else PT2
                    scalar.wait_ge(sems.h["mm"], mm_base + i + 1)
                    sems.inc("ev", scalar.copy(
                        attnT[:, cc * ROWS + rb * 128: cc * ROWS + rb * 128 + 128],
                        pt[:, 0:128]))

        with nc.Block("a_wo") as block:
            mm_base = sems.n["mm"]
            ev_base = sems.n["ev"]
            lw0 = {"lwa": sems.n["lwa"], "lwb": sems.n["lwb"]}
            st0 = {"st": sems.n["st"], "st2": sems.n["st2"]}
            sx0 = sems.n["sx"]
            jobs = [(mc, r0) for mc in range(8) for r0 in range(0, ROWS, 512)]

            @block.sync
            def _(sync):
                for i, (mc, r0) in enumerate(jobs):
                    half = (i % 2) * 512
                    sm = "lwa" if i % 2 == 0 else "lwb"
                    if i >= 2:
                        sync.wait_ge(sems.h["ev"], ev_base + i - 1)
                    src = (xf_h[mc * 128:(mc + 1) * 128, 257 + r0: 257 + r0 + 512]
                           if xsrc_h is None else
                           xsrc_h[mc * 128:(mc + 1) * 128, r0:r0 + 512])
                    sems.inc(sm, sync.dma_start(
                        out=xff[:, half:half + 512], in_=src), 16)

            @block.tensor
            def _(tensor):
                tensor.wait_ge(sems.h["ldw"], sems.n["ldw"])
                for i, (mc, r0) in enumerate(jobs):
                    half = (i % 2) * 512
                    if i >= 2:
                        tensor.wait_ge(sems.h["ev"], ev_base + i - 1)
                    last = None
                    for kc in range(8):
                        last = tensor.matmul(
                            PA[:, half:half + 512],
                            wsb[:, (kc * 8 + mc) * 128:(kc * 8 + mc) * 128 + 128],
                            attnT[:, kc * ROWS + r0: kc * ROWS + r0 + 512],
                            start=(kc == 0), stop=(kc == 7))
                    sems.inc("mm", last)

            @block.vector
            def _(vector):
                for i, (mc, r0) in enumerate(jobs):
                    half = (i % 2) * 512
                    sm = "lwa" if i % 2 == 0 else "lwb"
                    vector.wait_ge(sems.h["mm"], mm_base + i + 1)
                    vector.wait_ge(sems.h[sm], lw0[sm] + 16 * (i // 2 + 1))
                    if i >= 2:
                        sm2 = "st" if i % 2 == 0 else "st2"
                        vector.wait_ge(sems.h[sm2], st0[sm2] + 16 * ((i - 2) // 2 + 1))
                        if (i - 2) % 4 == 3:
                            vector.wait_ge(sems.h["sx"], sx0 + 16 * ((i - 1) // 4))
                    sems.inc("ev", vector.scalar_tensor_tensor(
                        outb[:, half:half + 512], PA[:, half:half + 512], 1.0,
                        xff[:, half:half + 512], ALU.mult, ALU.add))

            @block.scalar
            def _(scalar):
                for i, (mc, r0) in enumerate(jobs):
                    half = (i % 2) * 512
                    sm2 = "st" if i % 2 == 0 else "st2"
                    scalar.wait_ge(sems.h["ev"], ev_base + i + 1)
                    sems.inc(sm2, scalar.dma_start(
                        out=xa_h[mc * 128:(mc + 1) * 128, r0:r0 + 512],
                        in_=outb[:, half:half + 512]), 16)
                    if r0 == 1536:
                        sems.inc("sx", scalar.dma_start(
                            out=xl_h[mc * 128:(mc + 1) * 128, :],
                            in_=outb[:, half + 511:half + 512]), 16)
                scalar.wait_ge(sems.h["st"], sems.n["st"])
                scalar.wait_ge(sems.h["st2"], sems.n["st2"])
                scalar.wait_ge(sems.h["sx"], sems.n["sx"])



def build_ffn0_attn():
    nc = bass.Bass("TRN2", target_bir_lowering=False)
    H = dram_ffn0_attn(nc)
    with contextlib.ExitStack() as c0:
        sems = Sems(nc, c0, ["ld", "ldw", "ldr", "ldr2", "ldx", "lwa", "lwb", "ev", "st", "st2", "vch", "sch", "mm", "ex", "pv", "sx", "rch"])
        emit_ffn0_attn(nc, sems, H)
    return nc


CS2 = 1 + ROWS
NH1 = 22


def _foff(c, r):
    return c * CS2 + (1 if c < 4 else 0) + r


def dram_ffn0_ffn(nc, sfx="", ext=True):
    k = lambda io: ({"kind": "External" + io} if ext else {})
    return dict(
        xa=nc.dram_tensor("xa" + sfx, [1024, CS2], F32, **k("Input")),
        w1a=nc.dram_tensor("w1a" + sfx, [1024, 2816], F32, **k("Input")),
        w1g=nc.dram_tensor("w1g" + sfx, [1024, 2816], F32, **k("Input")),
        w2=nc.dram_tensor("w2" + sfx, [2816, 1024], F32, **k("Input")),
        lng=nc.dram_tensor("lng" + sfx, [128, 8], F32, **k("Input")),
        lnb=nc.dram_tensor("lnb" + sfx, [128, 8], F32, **k("Input")),
        b1a=nc.dram_tensor("b1a" + sfx, [128, 22], F32, **k("Input")),
        b1g=nc.dram_tensor("b1g" + sfx, [128, 22], F32, **k("Input")),
        b2=nc.dram_tensor("b2" + sfx, [128, 8], F32, **k("Input")),
        xo=nc.dram_tensor("xo" + sfx, [1024, ROWS], F32, **k("Output")),
        hflag=nc.dram_tensor("hflag" + sfx, [1, 1], F32, **k("Input")),
        xt=nc.dram_tensor("xt" + sfx, [1024, 257], F32, **k("Output")),
    )


def emit_ffn0_ffn(nc, sems, H):
    xa_h = H["xa"]
    xt_h = H["xt"]
    xh_h = H.get("xh")
    w1a_h = H["w1a"]
    w1g_h = H["w1g"]
    w2_h = H["w2"]
    lng_h = H["lng"]
    lnb_h = H["lnb"]
    b1a_h = H["b1a"]
    b1g_h = H["b1g"]
    b2_h = H["b2"]
    xo_h = H["xo"]
    hflag_h = H["hflag"]

    n = ROWS
    with contextlib.ExitStack() as ctx:
        e = ctx.enter_context
        xs = e(nc.sbuf_tensor("xs_s", [128, 8 * CS2], BF16))
        xnb = e(nc.sbuf_tensor("xnb_s", [128, 8 * n], BF16))
        wsb = e(nc.sbuf_tensor("wsb_s", [128, NH1 * 8 * 128], BF16))
        hb = e(nc.sbuf_tensor("hb_s", [128, NH1 * 1024], BF16))
        gb = e(nc.sbuf_tensor("gb_s", [128, 1024], BF16))
        sq = e(nc.sbuf_tensor("sq_s", [128, 1024], BF16))
        stat = e(nc.sbuf_tensor("stat_s", [1, 2 * n], F32))
        fsc = e(nc.sbuf_tensor("fsc_s", [128, 512], F32))
        eps_col = e(nc.sbuf_tensor("eps_s", [1, 1], F32))
        ones_colb = e(nc.sbuf_tensor("onescb_s", [128, 1], BF16))
        Ab = e(nc.sbuf_tensor("Ab_s", [128, n], F32))
        Bb = e(nc.sbuf_tensor("Bb_s", [128, n], F32))
        cols = e(nc.sbuf_tensor("cols_s", [128, 80], F32))
        xff = e(nc.sbuf_tensor("xff_s", [128, 1024], F32))
        outb = e(nc.sbuf_tensor("outb_s", [128, 1024], F32))
        ones_col = e(nc.sbuf_tensor("ones_col_s", [128, 1], F32))
        onesr = e(nc.sbuf_tensor("onesr_s", [1, 128], F32))
        PS = e(nc.psum_tensor("PS", [1, n], F32))
        PB2 = e(nc.psum_tensor("PB2", [128, 1024], F32))
        PA = e(nc.psum_tensor("PA", [128, 1024], F32))
        LNG = cols[:, 0:8]
        LNB = cols[:, 8:16]
        B1A = cols[:, 16:38]
        B1G = cols[:, 38:60]

        ld_a0B = [None]
        ld_a0A_l = [None]
        ldg_c = [None] * 8
        ldg_h = [None]
        with nc.Block("f_ld") as block:
            @block.sync
            def _(sync):
                sems.inc("ld", sync.dma_start(out=cols[:, 0:8], in_=lng_h[:, :]), 16)
                sems.inc("ld", sync.dma_start(out=cols[:, 8:16], in_=lnb_h[:, :]), 16)
                sems.inc("ld", sync.dma_start(out=cols[:, 16:38], in_=b1a_h[:, :]), 16)
                sems.inc("ld", sync.dma_start(out=cols[:, 38:60], in_=b1g_h[:, :]), 16)
                sems.inc("ld", sync.dma_start(out=cols[0:1, 70:71], in_=hflag_h[:, :]), 16)

            @block.gpsimd
            def _(g):
                if xh_h is None:
                    for c in range(8):
                        sm = "lc%d" % c
                        sems.inc(sm, g.dma_start(
                            out=xs[:, c * CS2:(c + 1) * CS2],
                            in_=xa_h[c * 128:(c + 1) * 128, :]), 16)
                        ldg_c[c] = (sems.h[sm], sems.n[sm])
                    ldg_h[0] = sems.n["ldg"]
                else:
                    x0v = xs[:, 0:1]
                    sems.inc("ldg", g.dma_start(
                        out=bass.AP(x0v.tensor, x0v.offset,
                                    [list(x0v.ap[0]), [CS2, 8], [1, 1]]),
                        in_=xh_h.ap().rearrange("(c p) n -> p c n", p=128)
                        [:, 0:8, :], allow_slow_non_contiguous=True), 16)
                    ldg_h[0] = sems.n["ldg"]
                    for c in range(8):
                        sm = "lc%d" % c
                        sems.inc(sm, g.dma_start(
                            out=xs[:, c * CS2 + 1:(c + 1) * CS2],
                            in_=xa_h[c * 128:(c + 1) * 128, :]), 16)
                        ldg_c[c] = (sems.h[sm], sems.n[sm])
                w1av0 = w1a_h.ap().rearrange("(kc p) m -> p kc m", p=128)
                sems.inc("ldw", g.dma_start(
                    out=wsb[:, 0:11 * 8 * 128], in_=w1av0[:, :, 0:11 * 128]), 16)
                ld_a0A = sems.n["ldw"]
                sems.inc("ldw", g.dma_start(
                    out=wsb[:, 11 * 8 * 128:22 * 8 * 128],
                    in_=w1av0[:, :, 11 * 128:22 * 128]), 16)
                ld_a0B[0] = sems.n["ldw"]
                ld_a0A_l[0] = ld_a0A

            @block.vector
            def _(vector):
                vector.memset(ones_col[:, :], 1.0)
                vector.memset(ones_colb[:, :], 1.0)
                vector.memset(onesr[:, :], 1.0)
                vector.memset(eps_col[:, :], 1e-5)

        with nc.Block("f_ld2") as block:
            @block.sync
            def _(sync):
                sems.inc("ld", sync.dma_start(out=cols[:, 60:68], in_=b2_h[:, :]), 16)

        with nc.Block("f_hbc") as block:
            @block.tensor
            def _(tensor):
                tensor.wait_ge(sems.h["ld"], sems.n["ld"])
                tensor.matmul(PB2[:, 0:1], onesr[:, :], cols[0:1, 70:71],
                              start=True, stop=True)

        with nc.Block("f_hbc2") as block:
            @block.scalar
            def _(scalar):
                scalar.copy(cols[:, 71:72], PB2[:, 0:1])

        with nc.Block("f_hz") as block:
            @block.vector
            def _(vector):
                if xh_h is not None:
                    vector.wait_ge(sems.h["ldg"], ldg_h[0])
                else:
                    vector.wait_ge(*ldg_c[7])
                x0v = xs[:, 0:1]
                hv = bass.AP(x0v.tensor, x0v.offset, [list(x0v.ap[0]), [CS2, 8], [1, 1]])
                vector.tensor_scalar(hv, hv, cols[:, 71:72], None, ALU.mult, ALU.bypass)

        with nc.Block("f_wait") as block:
            @block.tensor
            def _(tensor):
                tensor.wait_ge(sems.h["ld"], sems.n["ld"])
        emit_ln(nc, sems, xs, n, sq, stat, ones_colb, onesr, PS, PB2, Ab, Bb,
                "f_ln", x_off=_foff0, chunk_stride=None, eps_col=eps_col,
                chunk_ready=ldg_c)

        HJ = 11
        HW = HJ * 8 * 128
        R1 = wsb[:, 0:HW]
        R2 = wsb[:, HW:2 * HW]
        R3 = xs[:, 0:HW]
        w1av = w1a_h.ap().rearrange("(kc p) m -> p kc m", p=128)
        w1gv = w1g_h.ap().rearrange("(kc p) m -> p kc m", p=128)
        w2v = w2_h.ap().rearrange("(kc p) m -> p kc m", p=128)
        w1aA, w1aB = w1av[:, :, 0:HJ * 128], w1av[:, :, HJ * 128:NH1 * 128]
        w1gA, w1gB = w1gv[:, :, 0:HJ * 128], w1gv[:, :, HJ * 128:NH1 * 128]
        w2A, w2B = w2v[:, 0:HJ, :], w2v[:, HJ:NH1, :]

        def wload(g, region, src):
            sems.inc("ldw", g.dma_start(out=region, in_=src), 16)
            return sems.n["ldw"]

        regs = {"a": (R1, R2), "g": (R3, R1), "w": (R2, R3)}
        lc = {("a", 0): [ld_a0A_l[0], ld_a0B[0]]}

        app_cnt = {}

        def emit_app(vector, tiles):
            vector.wait_ge(sems.h["ld"], sems.n["ld"])
            for t0 in tiles:
                for c in range(8):
                    sems.chain(vector, "vch", vector.tensor_tensor(
                        fsc[:, 0:512], xs[:, _foff(c, t0):_foff(c, t0) + 512],
                        Ab[:, t0:t0 + 512], ALU.mult))
                    sems.chain(vector, "vch", vector.tensor_tensor(
                        fsc[:, 0:512], fsc[:, 0:512], Bb[:, t0:t0 + 512],
                        ALU.add))
                    sems.chain(vector, "vch", vector.tensor_scalar(
                        xnb[:, c * n + t0:c * n + t0 + 512], fsc[:, 0:512],
                        cols[:, c:c + 1], cols[:, 8 + c:9 + c],
                        ALU.mult, ALU.add))

        def emit_w1_pass(rg, kind, prefetch, app_tiles=None):
            R0 = rg * 1024
            regA, regB = regs[kind]
            jobs = ([(mc, n0) for mc in range(HJ) for n0 in (0, 512)] +
                    [(mc, n0) for mc in range(HJ, NH1) for n0 in (0, 512)])
            mm_base = sems.n["mm"]
            ex_base = sems.n["ex"]
            ev_base = sems.n["ev"]
            cA, cB = lc[(kind, rg)]
            out_counts = []
            vch0 = sems.n["vch"]
            app_end = None
            if app_tiles:
                for i, t0 in enumerate(app_tiles):
                    app_cnt[t0 // 512] = vch0 + 24 * (i + 1)
                app_end = vch0 + 24 * len(app_tiles)
            with nc.Block("f_%s%d" % (kind, rg)) as block:
                @block.gpsimd
                def _(g):
                    for region, src, mode in prefetch:
                        if mode == "app":
                            g.wait_ge(sems.h["vch"], app_end)
                        elif mode is True:
                            g.wait_ge(sems.h["mm"], mm_base + 2 * HJ)
                        out_counts.append(wload(g, region, src))

                @block.tensor
                def _(tensor):
                    if cA is not None:
                        tensor.wait_ge(sems.h["ldw"], cA)
                    cBr = (out_counts[cB[1]] if isinstance(cB, tuple)
                           else cB)
                    gate = "ex" if kind == "g" else "ev"
                    gbase = ex_base if kind == "g" else ev_base
                    for i, (mc, n0) in enumerate(jobs):
                        ti = (R0 + n0) // 512
                        if kind == "a" and ti in app_cnt:
                            tensor.wait_ge(sems.h["vch"], app_cnt[ti])
                        if i == 2 * HJ and cBr is not None:
                            tensor.wait_ge(sems.h["ldw"], cBr)
                        half = (i % 2) * 512
                        if i >= 2:
                            tensor.wait_ge(sems.h[gate], gbase + i - 1)
                        reg = regA if i < 2 * HJ else regB
                        mcl = mc if mc < HJ else mc - HJ
                        last = None
                        for kc in range(8):
                            last = tensor.matmul(
                                PA[:, half:half + 512],
                                reg[:, (kc * HJ + mcl) * 128:(kc * HJ + mcl) * 128 + 128],
                                xnb[:, kc * n + R0 + n0: kc * n + R0 + n0 + 512],
                                start=(kc == 0), stop=(kc == 7))
                        sems.inc("mm", last)

                if kind == "a":
                    @block.vector
                    def _(vector):
                        if app_tiles:
                            emit_app(vector, app_tiles)
                        for i, (mc, n0) in enumerate(jobs):
                            half = (i % 2) * 512
                            vector.wait_ge(sems.h["mm"], mm_base + i + 1)
                            sems.inc("ev", vector.tensor_scalar(
                                hb[:, mc * 1024 + n0: mc * 1024 + n0 + 512],
                                PA[:, half:half + 512],
                                cols[:, 16 + mc:17 + mc], None, ALU.add, ALU.bypass))
                else:
                    @block.scalar
                    def _(scalar):
                        for i, (mc, n0) in enumerate(jobs):
                            half = (i % 2) * 512
                            scalar.wait_ge(sems.h["mm"], mm_base + i + 1)
                            if i >= 2:
                                scalar.wait_ge(sems.h["ev"], ev_base + i - 1)
                            sems.inc("ex", scalar.activation(
                                gb[:, half:half + 512], PA[:, half:half + 512],
                                AF.Gelu, bias=cols[:, 38 + mc:39 + mc]))

                    @block.vector
                    def _(vector):
                        if app_tiles:
                            emit_app(vector, app_tiles)
                        for i, (mc, n0) in enumerate(jobs):
                            half = (i % 2) * 512
                            vector.wait_ge(sems.h["ex"], ex_base + i + 1)
                            sems.inc("ev", vector.tensor_tensor(
                                hb[:, mc * 1024 + n0: mc * 1024 + n0 + 512],
                                hb[:, mc * 1024 + n0: mc * 1024 + n0 + 512],
                                gb[:, half:half + 512], ALU.mult))
            return out_counts

        def emit_w2_pass(rg, prefetch):
            R0 = rg * 1024
            regA, regB = regs["w"]
            jobs2 = [(mc, n0) for mc in range(8) for n0 in (0, 512)]
            mm_base = sems.n["mm"]
            ev_base = sems.n["ev"]
            lwa_base = sems.n["lwa"]
            lwb_base = sems.n["lwb"]
            st_base = sems.n["st"]
            st2_base = sems.n["st2"]
            sx_base = sems.n["sx"]
            cA, cB = lc[("w", rg)]
            out_counts = []
            with nc.Block("f_w2%d" % rg) as block:
                @block.gpsimd
                def _(g):
                    for region, src, after_A in prefetch:
                        out_counts.append(wload(g, region, src))

                @block.sync
                def _(sync):
                    for i, (mc, n0) in enumerate(jobs2):
                        half = (i % 2) * 512
                        sm = "lwa" if i % 2 == 0 else "lwb"
                        if i >= 2:
                            sync.wait_ge(sems.h["ev"], ev_base + i - 1)
                        a0_ = 0 if xh_h is not None else 1
                        sems.inc(sm, sync.dma_start(
                            out=xff[:, half:half + 512],
                            in_=xa_h[mc * 128:(mc + 1) * 128,
                                     a0_ + R0 + n0: a0_ + R0 + n0 + 512]), 16)

                @block.tensor
                def _(tensor):
                    tensor.wait_ge(sems.h["ldw"], cB)
                    for i, (mc, n0) in enumerate(jobs2):
                        half = (i % 2) * 512
                        if i >= 2:
                            tensor.wait_ge(sems.h["ev"], ev_base + i - 1)
                        last = None
                        for kc in range(NH1):
                            reg = regA if kc < HJ else regB
                            kcl = kc if kc < HJ else kc - HJ
                            last = tensor.matmul(
                                PA[:, half:half + 512],
                                reg[:, (kcl * 8 + mc) * 128:(kcl * 8 + mc) * 128 + 128],
                                hb[:, kc * 1024 + n0: kc * 1024 + n0 + 512],
                                start=(kc == 0), stop=(kc == NH1 - 1))
                        sems.inc("mm", last)

                @block.vector
                def _(vector):
                    for i, (mc, n0) in enumerate(jobs2):
                        half = (i % 2) * 512
                        sm = "lwa" if i % 2 == 0 else "lwb"
                        vector.wait_ge(sems.h["mm"], mm_base + i + 1)
                        lbase = lwa_base if i % 2 == 0 else lwb_base
                        vector.wait_ge(sems.h[sm], lbase + 16 * (i // 2 + 1))
                        if i >= 2:
                            sm2 = "st" if i % 2 == 0 else "st2"
                            sbase = st_base if i % 2 == 0 else st2_base
                            vector.wait_ge(sems.h[sm2], sbase + 16 * ((i - 2) // 2 + 1))
                            if rg == 1 and i % 2 == 1:
                                vector.wait_ge(sems.h["sx"],
                                               sx_base + 16 * ((i - 1) // 2))
                        sems.chain(vector, "vch", vector.scalar_tensor_tensor(
                            outb[:, half:half + 512], PA[:, half:half + 512],
                            cols[:, 60 + mc:61 + mc], xff[:, half:half + 512],
                            ALU.add, ALU.add))
                        sems.inc("ev", vector.engine_nop())

                @block.scalar
                def _(scalar):
                    for i, (mc, n0) in enumerate(jobs2):
                        half = (i % 2) * 512
                        sm2 = "st" if i % 2 == 0 else "st2"
                        scalar.wait_ge(sems.h["ev"], ev_base + i + 1)
                        sems.inc(sm2, scalar.dma_start(
                            out=xo_h[mc * 128:(mc + 1) * 128, R0 + n0: R0 + n0 + 512],
                            in_=outb[:, half:half + 512]), 16)
                        if rg == 1 and n0 == 512:
                            sems.inc("sx", scalar.dma_start(
                                out=xt_h[mc * 128:(mc + 1) * 128, :],
                                in_=outb[:, half + 255:half + 512]), 16)
                    scalar.wait_ge(sems.h["st"], sems.n["st"])
                    scalar.wait_ge(sems.h["st2"], sems.n["st2"])
                    scalar.wait_ge(sems.h["sx"], sems.n["sx"])
            return out_counts

        c = emit_w1_pass(0, "a", [(R3, w1gA, "app"), (R1, w1gB, True)],
                         app_tiles=[0, 512, 1024, 1536])
        lc[("g", 0)] = c
        c = emit_w1_pass(0, "g", [(R2, w2A, False), (R3, w2B, True)])
        lc[("w", 0)] = c
        c = emit_w2_pass(0, [(R1, w1aA, False)])
        lc[("a", 1)] = [c[0], ("self", 0)]
        c = emit_w1_pass(1, "a", [(R2, w1aB, False), (R3, w1gA, False),
                                  (R1, w1gB, True)])
        lc[("g", 1)] = [c[1], c[2]]
        c = emit_w1_pass(1, "g", [(R2, w2A, False), (R3, w2B, True)])
        lc[("w", 1)] = c
        emit_w2_pass(1, [])


def _foff0(c):
    return _foff(c, 0)



def build_ffn0_ffn():
    nc = bass.Bass("TRN2", target_bir_lowering=False)
    H = dram_ffn0_ffn(nc)
    with contextlib.ExitStack() as c0:
        sems = Sems(nc, c0, ["ld", "ldg", "ldw", "lwa", "lwb", "ev", "st", "st2", "vch", "sch", "mm", "ex", "bc_mm", "bc_ev", "sx"] + ["lc%d" % i for i in range(8)])
        emit_ffn0_ffn(nc, sems, H)
    return nc


def dram_attn0(nc, sfx="", ext=True):
    k = lambda io: ({"kind": "External" + io} if ext else {})
    return dict(
        xnk=nc.dram_tensor("xnk" + sfx, [1024, 2 * ROWS], F32, **k("Input")),
        wqkv=nc.dram_tensor("wqkv" + sfx, [128, 3 * 1024], F32, **k("Input")),
        rc=nc.dram_tensor("rc" + sfx, [512, 2 * ROWS], F32, **k("Input")),
        rs=nc.dram_tensor("rs" + sfx, [512, 2 * ROWS], F32, **k("Input")),
        tri=nc.dram_tensor("tri" + sfx, [128, 384], F32, **k("Input")),
        vflag=nc.dram_tensor("vflag" + sfx, [1, 1], F32, **k("Input")),
        a0=nc.dram_tensor("a0" + sfx, [ROWS, 1024], BF16, **k("Output")),
    )


def emit_attn0(nc, sems, H):
    xnk_h = H["xnk"]
    wqkv_h = H["wqkv"]
    rc_h = H["rc"]
    rs_h = H["rs"]
    tri_h = H["tri"]
    vflag_h = H["vflag"]
    a0_h = H["a0"]

    NK = 2 * ROWS
    with contextlib.ExitStack() as ctx:
        e = ctx.enter_context
        xtl = e(nc.sbuf_tensor("xtl_s", [128, 8 * 512], BF16))
        wsb = e(nc.sbuf_tensor("wsb_a", [128, 3 * 1024], BF16))
        qhat = e(nc.sbuf_tensor("qhat_a", [128, 8 * ROWS], BF16))
        khat = e(nc.sbuf_tensor("khat_a", [128, 8 * NK], BF16))
        vx = e(nc.sbuf_tensor("vx_a", [128, 32 * 1025], BF16))
        vtmp = e(nc.sbuf_tensor("vtmp_a", [128, 2 * 8 * 512], BF16))
        rta = e(nc.sbuf_tensor("rta_a", [128, 2 * 512], F32))
        rtb = e(nc.sbuf_tensor("rtb_a", [128, 2 * 512], F32))
        bsc = e(nc.sbuf_tensor("bsc_a", [128, 1024], F32))
        sqs = e(nc.sbuf_tensor("sqs_a", [128, 512], BF16))
        nrm = e(nc.sbuf_tensor("nrm_a", [1, 512], F32))
        srow = e(nc.sbuf_tensor("srow_a", [1, 16], F32))
        colbuf = e(nc.sbuf_tensor("colbuf_a", [128, 8], F32))
        tri = e(nc.sbuf_tensor("tri_a", [128, 384], BF16))
        onesr = e(nc.sbuf_tensor("onesr_a", [1, 128], F32))
        ones_colb = e(nc.sbuf_tensor("onescb_a", [128, 1], BF16))
        ptil = e(nc.sbuf_tensor("ptil_a", [128, 512], BF16))
        SA = e(nc.psum_tensor("SA", [128, 512], F32))
        SB = e(nc.psum_tensor("SB", [128, 512], F32))
        PT = e(nc.psum_tensor("PT", [128, 1024], BF16))
        PT2 = e(nc.psum_tensor("PT2", [128, 1024], BF16))
        PV1 = e(nc.psum_tensor("PV1", [128, 1536], F32))
        PM = e(nc.psum_tensor("PM", [128, 512], F32))

        with nc.Block("b_ld") as block:
            @block.sync
            def _(sync):
                sems.inc("ld", sync.dma_start(out=srow[:, 0:1], in_=vflag_h[:, :]), 16)

            @block.gpsimd
            def _(g):
                sems.inc("ldw", g.dma_start(out=tri[:, :], in_=tri_h[:, :]), 16)
                sems.inc("ldw", g.dma_start(out=wsb[:, :], in_=wqkv_h[:, :]), 16)

            @block.vector
            def _(vector):
                vector.memset(onesr[:, :], 1.0)
                vector.memset(ones_colb[:, :], 1.0)
                v0 = vx[:, 0:1]
                sems.chain(vector, "vch", vector.memset(vx[:, :], 0.0))
                vector.memset(bass.AP(v0.tensor, v0.offset + 1024,
                                      [list(v0.ap[0]), [1025, 32], [1, 1]]), 1.0)

        for rt in range(9):
            with nc.Block("b_pj%d" % rt) as block:
                half = 0
                pvh = ((rt - 1) % 2) * 8 * 512
                mm_base = sems.n["mm"]
                ev_base = sems.n["ev"]
                tp_base = sems.n["tp"]
                ev2_base = sems.n["sch"]
                lxa_base = sems.n["lxa"]
                projs = ([0, 1, 2] if rt < 4 else [1, 2]) if rt < 8 else []

                @block.gpsimd
                def _(g):
                    if rt < 8:
                        sems.inc("lxa", g.dma_start(
                            out=xtl[:, 0:8 * 512],
                            in_=xnk_h.ap().rearrange("(c p) n -> p c n", p=128)
                            [:, :, rt * 512:(rt + 1) * 512]), 16)

                @block.tensor
                def _(tensor):
                    if projs:
                        tensor.wait_ge(sems.h["ldw"], sems.n["ldw"])
                        tensor.wait_ge(sems.h["lxa"], lxa_base + 16)
                    nmm = 0
                    for pi, pj in enumerate(projs):
                        for c in range(8):
                            ps = SA if nmm % 2 == 0 else SB
                            nmm += 1
                            if nmm > 2:
                                tensor.wait_ge(sems.h["ev"], ev_base + nmm - 2)
                            last = None
                            for sub in range(2):
                                base = sub * 64
                                last = tensor.matmul(
                                    ps[base:base + 64, 0:512],
                                    wsb[base:base + 64,
                                        pj * 1024 + (c * 2 + sub) * 64:
                                        pj * 1024 + (c * 2 + sub) * 64 + 64],
                                    xtl[base:base + 64, half + c * 512: half + c * 512 + 512],
                                    start=True, stop=True, tile_position=(base, base))
                            sems.inc("mm", last)
                    if rt >= 1:
                        ntp = 0
                        for jb4 in range(4):
                            for cc in range(8):
                                pt = PT if ntp % 2 == 0 else PT2
                                ntp += 1
                                if ntp > 2:
                                    tensor.wait_ge(sems.h["sch"], ev2_base + ntp - 2)
                                sems.inc("tp", tensor.transpose(
                                    pt[:, 0:128],
                                    vtmp[:, pvh + cc * 512 + jb4 * 128:
                                         pvh + cc * 512 + jb4 * 128 + 128],
                                    tri[:, 0:128]))

                @block.scalar
                def _(scalar):
                    nmm = 0
                    for pi, pj in enumerate(projs):
                        for c in range(8):
                            ps = SA if nmm % 2 == 0 else SB
                            nmm += 1
                            scalar.wait_ge(sems.h["mm"], mm_base + nmm)
                            if pj == 0:
                                dst = qhat[:, c * ROWS + rt * 512: c * ROWS + rt * 512 + 512]
                                sc = 1.0 / 32.0
                            elif pj == 1:
                                dst = khat[:, c * NK + rt * 512: c * NK + rt * 512 + 512]
                                sc = 1.0
                            else:
                                dst = vtmp[:, (rt % 2) * 8 * 512 + c * 512:
                                           (rt % 2) * 8 * 512 + c * 512 + 512]
                                sc = 1.0
                            sems.inc("ev", scalar.activation(dst, ps[:, 0:512],
                                                             AF.Copy, scale=sc))
                    if rt >= 1:
                        ntp = 0
                        for jb4 in range(4):
                            for cc in range(8):
                                pt = PT if ntp % 2 == 0 else PT2
                                ntp += 1
                                scalar.wait_ge(sems.h["tp"], tp_base + ntp)
                                jb = (rt - 1) * 4 + jb4
                                sems.inc("sch", scalar.copy(
                                    vx[:, jb * 1025 + cc * 128: jb * 1025 + cc * 128 + 128],
                                    pt[:, 0:128]))

        def emit_norm2(tag, tile, nrows, cell):
            rts = chunked(nrows, 512)
            with nc.Block(tag) as block:
                vch_base = sems.n["vch"]
                mm_base2 = sems.n["mm"]
                sch_base = sems.n["sch"]

                @block.vector
                def _(vector):
                    for ri, (t0, tl) in enumerate(rts):
                        for c in range(8):
                            j = ri * 8 + c
                            if j:
                                vector.wait_ge(sems.h["mm"], mm_base2 + j)
                            sems.inc("vch", vector.tensor_tensor(
                                sqs[:, 0:tl],
                                tile[:, c * nrows + t0: c * nrows + t0 + tl],
                                tile[:, c * nrows + t0: c * nrows + t0 + tl], ALU.mult))
                        vector.wait_ge(sems.h["sch"], sch_base + ri + 1)
                        sems.inc("vch", vector.reduce_max(
                            srow[:, 8:9], nrm[:, 0:tl], axis=mybir.AxisListType.X))
                        vector.wait_ge(sems.h["vch"], sems.n["vch"])
                        sems.inc("vch", vector.tensor_tensor(
                            srow[:, cell:cell + 1], srow[:, cell:cell + 1],
                            srow[:, 8:9], ALU.max))

                @block.tensor
                def _(tensor):
                    for ri, (t0, tl) in enumerate(rts):
                        for c in range(8):
                            tensor.wait_ge(sems.h["vch"], vch_base + 10 * ri + c + 1)
                            if c == 0 and ri > 0:
                                tensor.wait_ge(sems.h["sch"], sch_base + ri)
                            sems.inc("mm", tensor.matmul(
                                PM[0:1, 0:tl], ones_colb[:, :], sqs[:, 0:tl],
                                start=(c == 0), stop=(c == 7)))

                @block.scalar
                def _(scalar):
                    for ri, (t0, tl) in enumerate(rts):
                        scalar.wait_ge(sems.h["mm"], mm_base2 + 8 * (ri + 1))
                        if ri >= 1:
                            scalar.wait_ge(sems.h["vch"], vch_base + 10 * ri)
                        sems.inc("sch", scalar.copy(nrm[:, 0:tl], PM[0:1, 0:tl]))

        with nc.Block("b_m0") as block:
            @block.vector
            def _(vector):
                vector.memset(srow[:, 1:3], 0.0)

        emit_norm2("b_nq", qhat, ROWS, 1)
        emit_norm2("b_nk", khat, NK, 2)

        with nc.Block("b_m") as block:
            @block.vector
            def _(vector):
                sems.chain(vector, "vch", vector.tensor_tensor(
                    srow[:, 3:4], srow[:, 1:2], srow[:, 2:3], ALU.mult))

            @block.scalar
            def _(scalar):
                scalar.wait_ge(sems.h["vch"], sems.n["vch"])
                sems.inc("sch", scalar.activation(srow[:, 4:5], srow[:, 3:4], AF.Sqrt))

            @block.vector
            def _(vector):
                vector.wait_ge(sems.h["sch"], sems.n["sch"])
                sems.chain(vector, "vch", vector.tensor_scalar(
                    srow[:, 5:6], srow[:, 4:5], -1.0, None, ALU.mult, ALU.bypass))

        with nc.Block("b_mbc") as block:
            @block.tensor
            def _(tensor):
                tensor.matmul(PM[:, 0:1], onesr[:, :], srow[:, 5:6], start=True, stop=True)

        with nc.Block("b_mbc2") as block:
            @block.scalar
            def _(scalar):
                scalar.copy(colbuf[:, 0:1], PM[:, 0:1])

        def emit_rope_a(tag, tile, nrows):
            jobs = [(c, t0, tl) for c in range(4) for (t0, tl) in chunked(nrows, 512)]
            with nc.Block(tag) as block:
                ldr_base = sems.n["ldr"]
                ldr2_base = sems.n["ldr2"]
                vch_base0 = sems.n["vch"]

                @block.sync
                def _(sync):
                    for i, (c, t0, tl) in enumerate(jobs):
                        half = (i % 2) * 512
                        sm = "ldr" if i % 2 == 0 else "ldr2"
                        if i >= 2:
                            sync.wait_ge(sems.h["vch"], vch_base0 + 6 * (i - 1))
                        sems.inc(sm, sync.dma_start(
                            out=rta[:, half:half + tl],
                            in_=rc_h[c * 128:(c + 1) * 128, t0:t0 + tl]), 16)
                        sems.inc(sm, sync.dma_start(
                            out=rtb[:, half:half + tl],
                            in_=rs_h[c * 128:(c + 1) * 128, t0:t0 + tl]), 16)

                @block.vector
                def _(vector):
                    for i, (c, t0, tl) in enumerate(jobs):
                        half = (i % 2) * 512
                        sm = "ldr" if i % 2 == 0 else "ldr2"
                        sb = ldr_base if i % 2 == 0 else ldr2_base
                        vector.wait_ge(sems.h[sm], sb + 32 * (i // 2 + 1))
                        x1 = tile[:, c * nrows + t0: c * nrows + t0 + tl]
                        x2 = tile[:, (4 + c) * nrows + t0: (4 + c) * nrows + t0 + tl]
                        co = rta[:, half:half + tl]
                        si = rtb[:, half:half + tl]
                        t1 = bsc[:, 0:tl]
                        t2 = bsc[:, 512:512 + tl]
                        sems.chain(vector, "vch", vector.tensor_tensor(t1, x1, si, ALU.mult))
                        sems.chain(vector, "vch", vector.tensor_tensor(t2, x2, si, ALU.mult))
                        sems.chain(vector, "vch", vector.tensor_tensor(x1, x1, co, ALU.mult))
                        sems.chain(vector, "vch", vector.tensor_tensor(x2, x2, co, ALU.mult))
                        sems.chain(vector, "vch", vector.tensor_tensor(x1, x1, t2, ALU.subtract))
                        sems.chain(vector, "vch", vector.tensor_tensor(x2, x2, t1, ALU.add))

        emit_rope_a("b_ropeq", qhat, ROWS)
        emit_rope_a("b_ropek", khat, NK)

        with nc.Block("b_vf") as block:
            @block.tensor
            def _(tensor):
                tensor.wait_ge(sems.h["ld"], sems.n["ld"])
                tensor.matmul(PM[:, 1:2], onesr[:, :], srow[:, 0:1], start=True, stop=True)

        with nc.Block("b_vf2") as block:
            @block.scalar
            def _(scalar):
                scalar.copy(colbuf[:, 1:2], PM[:, 1:2])

        with nc.Block("b_vf3") as block:
            @block.vector
            def _(vector):
                vector.tensor_scalar(vx[:, 16 * 1025:32 * 1025],
                                     vx[:, 16 * 1025:32 * 1025],
                                     colbuf[:, 1:2], None, ALU.mult, ALU.bypass)

        with nc.Block("b_att") as block:
            mm_base3 = sems.n["mm"]
            ex_base = sems.n["ex"]
            pv_base = sems.n["pv"]
            ev_base3 = sems.n["ev"]
            st_base = sems.n["sg"]
            st2_base = sems.n["sg2"]
            all_batches = []
            for ib in range(NIB):
                chunksl = list(range(0, ib + 1)) + list(range(16, 32))
                all_batches.append([chunksl[i:i + 4] for i in range(0, len(chunksl), 4)])

            flat = []
            for ib in range(NIB):
                for bi, batch in enumerate(all_batches[ib]):
                    flat.append((ib, batch, bi == len(all_batches[ib]) - 1,
                                 bi == 0))

            @block.tensor
            def _(tensor):
                def emit_pv(j):
                    ibj, batchj, lastbj, firstbj = flat[j]
                    tensor.wait_ge(sems.h["ex"], ex_base + j + 1)
                    if firstbj and ibj >= 1:
                        tensor.wait_ge(sems.h["ev"], ev_base3 + ibj)
                    pslot = ptil if j % 2 == 0 else sqs
                    last = None
                    for k, c in enumerate(batchj):
                        lastc = lastbj and (k == len(batchj) - 1)
                        for n0, nl in ((0, 512), (512, 512), (1024, 1)):
                            last = tensor.matmul(
                                PV1[:, n0:n0 + nl],
                                pslot[:, k * 128:k * 128 + 128],
                                vx[:, c * 1025 + n0: c * 1025 + n0 + nl],
                                start=(firstbj and k == 0), stop=lastc)
                    sems.inc("pv", last)

                for nx, (ib, batch, lastb, firstb) in enumerate(flat):
                    ps = SA if (nx % 2 == 0) else SB
                    if nx >= 2:
                        tensor.wait_ge(sems.h["ex"], ex_base + nx - 1)
                    last = None
                    for k, c in enumerate(batch):
                        for cc in range(8):
                            last = tensor.matmul(
                                ps[:, k * 128:k * 128 + 128],
                                khat[:, cc * NK + c * 128: cc * NK + c * 128 + 128],
                                qhat[:, cc * ROWS + ib * 128: cc * ROWS + ib * 128 + 128],
                                start=(cc == 0), stop=(cc == 7 and c != ib))
                        if c == ib:
                            last = tensor.matmul(
                                ps[:, k * 128:k * 128 + 128],
                                tri[:, 256:384], tri[:, 0:128],
                                start=False, stop=True)
                    sems.inc("mm", last)
                    if nx >= 1:
                        emit_pv(nx - 1)
                emit_pv(len(flat) - 1)

            @block.scalar
            def _(scalar):
                for nx, (ib, batch, lastb, firstb) in enumerate(flat):
                    ps = SA if (nx % 2 == 0) else SB
                    pslot = ptil if nx % 2 == 0 else sqs
                    scalar.wait_ge(sems.h["mm"], mm_base3 + nx + 1)
                    if nx >= 2:
                        scalar.wait_ge(sems.h["pv"], pv_base + nx - 1)
                    sems.inc("ex", scalar.activation(
                        pslot[:, 0:128 * len(batch)],
                        ps[:, 0:128 * len(batch)],
                        AF.Exp, bias=colbuf[:, 0:1]))

            @block.vector
            def _(vector):
                npvtot = 0
                for ib in range(NIB):
                    npvtot += len(all_batches[ib])
                    ob = rta if ib % 2 == 0 else rtb
                    vector.wait_ge(sems.h["pv"], pv_base + npvtot)
                    sems.chain(vector, "vch", vector.reciprocal(
                        colbuf[:, 2:3], PV1[:, 1024:1025]))
                    if ib >= 2:
                        sm2 = "sg" if ib % 2 == 0 else "sg2"
                        sbb = st_base if ib % 2 == 0 else st2_base
                        vector.wait_ge(sems.h[sm2], sbb + 16 * ((ib - 2) // 2 + 1))
                    sems.chain(vector, "vch", vector.tensor_scalar(
                        ob[:, 0:1024], PV1[:, 0:1024], colbuf[:, 2:3],
                        None, ALU.mult, ALU.bypass))
                    sems.inc("ev", vector.engine_nop())

            @block.gpsimd
            def _(g):
                for ib in range(NIB):
                    ob = rta if ib % 2 == 0 else rtb
                    sm2 = "sg" if ib % 2 == 0 else "sg2"
                    g.wait_ge(sems.h["ev"], ev_base3 + ib + 1)
                    sems.inc(sm2, g.dma_start(
                        out=a0_h[ib * 128:(ib + 1) * 128, :],
                        in_=ob[:, 0:1024]), 16)
                g.wait_ge(sems.h["sg"], sems.n["sg"])
                g.wait_ge(sems.h["sg2"], sems.n["sg2"])



def build_attn0():
    nc = bass.Bass("TRN2", target_bir_lowering=False)
    H = dram_attn0(nc)
    with contextlib.ExitStack() as c0:
        sems = Sems(nc, c0, ["ld", "ldg", "ldw", "ldr", "ldr2", "lxa", "lxb", "ev", "st", "st2", "vch", "sch", "mm", "ex", "pv", "tp", "sg", "sg2"])
        emit_attn0(nc, sems, H)
    return nc


def dram_combine(nc, sfx="", ext=True):
    k = lambda io: ({"kind": "External" + io} if ext else {})
    return dict(
        x0=nc.dram_tensor("x0" + sfx, [1024, ROWS], F32, **k("Input")),
        a0=nc.dram_tensor("a0" + sfx, [ROWS, 1024], BF16, **k("Input")),
        f0=nc.dram_tensor("f0" + sfx, [1024, ROWS], F32, **k("Input")),
        x1=nc.dram_tensor("x1" + sfx, [1024, ROWS], F32, **k("Output")),
        xn=nc.dram_tensor("xn" + sfx, [1024, ROWS], F32, **k("Output")),
    )


def emit_combine(nc, sems, H):
    x0_h = H["x0"]
    a0_h = H["a0"]
    f0_h = H["f0"]
    x1_h = H["x1"]
    xn_h = H["xn"]
    n = ROWS
    with contextlib.ExitStack() as ctx:
        e = ctx.enter_context
        xs = e(nc.sbuf_tensor("xs", [128, 8 * n], F32))
        ya = e(nc.sbuf_tensor("ya", [128, 8 * n], F32))
        sq = e(nc.sbuf_tensor("sq", [128, 1024], F32))
        stat = e(nc.sbuf_tensor("stat", [1, 2 * n], F32))
        ones_col = e(nc.sbuf_tensor("ones_col", [128, 1], F32))
        onesr = e(nc.sbuf_tensor("onesr", [1, 128], F32))
        eps_col = e(nc.sbuf_tensor("eps_col", [1, 1], F32))
        Ab = e(nc.sbuf_tensor("Ab", [128, n], F32))
        Bb = e(nc.sbuf_tensor("Bb", [128, n], F32))
        ps = e(nc.psum_tensor("ps", [1, n], F32))
        pb2 = e(nc.psum_tensor("pb2", [128, 1024], F32))

        yab = e(nc.sbuf_tensor("yab", [128, 8 * n], BF16))
        with nc.Block("c_ld") as block:
            @block.sync
            def _(sync):
                sems.inc("ld", sync.dma_start(
                    out=xs[:, :], in_=x0_h.ap().rearrange("(c p) n -> p c n", p=128)
                    [:, :, 0:n]), 16)
                for c in range(8):
                    sems.inc("ld", sync.dma_start(
                        out=yab[:, c * n:(c + 1) * n],
                        in_=a0_h[:, c * 128:(c + 1) * 128], transpose=True), 16)

            @block.vector
            def _(vector):
                vector.memset(ones_col[:, :], 1.0)
                vector.memset(onesr[:, :], 1.0)
                vector.memset(eps_col[:, :], 1e-5)

        with nc.Block("c_add1") as block:
            @block.vector
            def _(vector):
                vector.wait_ge(sems.h["ld"], sems.n["ld"])
                for c in range(8):
                    sems.inc("vch", vector.tensor_tensor(
                        xs[:, c * n:(c + 1) * n], xs[:, c * n:(c + 1) * n],
                        yab[:, c * n:(c + 1) * n], ALU.add))

        with nc.Block("c_ld2") as block:
            @block.sync
            def _(sync):
                sems.inc("ld", sync.dma_start(
                    out=ya[:, :], in_=f0_h.ap().rearrange("(c p) n -> p c n", p=128)), 16)

        with nc.Block("c_add2") as block:
            @block.vector
            def _(vector):
                vector.wait_ge(sems.h["ld"], sems.n["ld"])
                for c in range(8):
                    sems.inc("vch", vector.tensor_tensor(
                        xs[:, c * n:(c + 1) * n], xs[:, c * n:(c + 1) * n],
                        ya[:, c * n:(c + 1) * n], ALU.add))

        with nc.Block("c_st1") as block:
            @block.sync
            def _(sync):
                sems.inc("st", sync.dma_start(
                    out=x1_h.ap().rearrange("(c p) n -> p c n", p=128), in_=xs[:, :]), 16)

        emit_ln(nc, sems, xs, n, sq, stat, ones_col, onesr, ps, pb2, Ab, Bb,
                "c_ln", eps_col=eps_col)

        with nc.Block("c_app") as block:
            @block.vector
            def _(vector):
                for c in range(8):
                    sems.chain(vector, "vch", vector.tensor_tensor(
                        ya[:, c * n:(c + 1) * n], xs[:, c * n:(c + 1) * n],
                        Ab[:, :], ALU.mult))
                    vector.tensor_tensor(ya[:, c * n:(c + 1) * n],
                                         ya[:, c * n:(c + 1) * n], Bb[:, :], ALU.add)

        with nc.Block("c_st2") as block:
            @block.sync
            def _(sync):
                sems.inc("st", sync.dma_start(
                    out=xn_h.ap().rearrange("(c p) n -> p c n", p=128), in_=ya[:, :]), 16)
                sync.wait_ge(sems.h["st"], sems.n["st"])





def build_combine():
    nc = bass.Bass("TRN2", target_bir_lowering=False)
    H = dram_combine(nc)
    with contextlib.ExitStack() as c0:
        sems = Sems(nc, c0, ["ld", "ev", "st", "bc_mm", "bc_ev", "vch", "sch", "mm"])
        emit_combine(nc, sems, H)
    return nc


def dram_attn1a(nc, sfx="", ext=True):
    k = lambda io: ({"kind": "External" + io} if ext else {})
    return dict(
        xnk=nc.dram_tensor("xnk" + sfx, [1024, 2 * ROWS], F32, **k("Input")),
        wqkv=nc.dram_tensor("wqkv" + sfx, [1024, 3 * 1024], F32, **k("Input")),
        rtab=nc.dram_tensor("rtab" + sfx, [2 * ROWS, 512], F32, **k("Input")),
        tri=nc.dram_tensor("tri" + sfx, [128, 384], F32, **k("Input")),
        vflag=nc.dram_tensor("vflag" + sfx, [1, 1], F32, **k("Input")),
        at=nc.dram_tensor("at" + sfx, [ROWS, 1024], BF16, **k("Output")),
    )


def emit_attn1a(nc, sems, H):
    xnk_h = H["xnk"]
    wqkv_h = H["wqkv"]
    rtab_h = H["rtab"]
    tri_h = H["tri"]
    vflag_h = H["vflag"]
    at_h = H["at"]

    NK = 2 * ROWS
    NRB = 32
    with contextlib.ExitStack() as ctx:
        e = ctx.enter_context
        xtl = e(nc.sbuf_tensor("xtl_1", [128, 8 * 128], BF16))
        wsb = e(nc.sbuf_tensor("wsb_1", [128, 8 * 1024], BF16))
        qT = e(nc.sbuf_tensor("qT_1", [128, 8 * ROWS], BF16))
        kT = e(nc.sbuf_tensor("kT_1", [128, 8 * NK], BF16))
        vx = e(nc.sbuf_tensor("vx_1", [128, NRB * 1040], BF16))
        rowsc = e(nc.sbuf_tensor("rowsc_1", [128, 1024], BF16))
        rtab = e(nc.sbuf_tensor("rtab_1", [128, 512], F32))
        bsc = e(nc.sbuf_tensor("bsc_1", [128, 512], F32))
        nrm = e(nc.sbuf_tensor("nrm_1", [1, 512], F32))
        srow = e(nc.sbuf_tensor("srow_1", [1, 16], F32))
        colbuf = e(nc.sbuf_tensor("colbuf_1", [128, 8], F32))
        tri = e(nc.sbuf_tensor("tri_1", [128, 384], BF16))
        onesr = e(nc.sbuf_tensor("onesr_1", [1, 128], F32))
        ones_colb = e(nc.sbuf_tensor("onescb_1", [128, 1], BF16))
        ptil = e(nc.sbuf_tensor("ptil_1", [128, 2 * 1024], BF16))
        attnb = e(nc.sbuf_tensor("attnb_1", [128, 4 * 128], BF16))
        sqs = ptil
        SA = e(nc.psum_tensor("SA1", [128, 512], F32))
        SB = e(nc.psum_tensor("SB1", [128, 512], F32))
        SC = e(nc.psum_tensor("SC1", [128, 512], F32))
        SD = e(nc.psum_tensor("SD1", [128, 512], F32))
        PM = e(nc.psum_tensor("PM1", [128, 512], F32))
        PT = e(nc.psum_tensor("PTa", [128, 1024], BF16))
        PT2 = e(nc.psum_tensor("PTb", [128, 1024], BF16))
        PV1 = e(nc.psum_tensor("PVa", [128, 512], F32))

        with nc.Block("d_ld") as block:
            @block.sync
            def _(sync):
                sems.inc("ld", sync.dma_start(out=srow[:, 0:1], in_=vflag_h[:, :]), 16)

            @block.gpsimd
            def _(g):
                sems.inc("ldw", g.dma_start(out=tri[:, :], in_=tri_h[:, :]), 16)

            @block.vector
            def _(vector):
                vector.memset(onesr[:, :], 1.0)
                vector.memset(ones_colb[:, :], 1.0)
                v0 = vx[:, 0:1]
                sems.chain(vector, "vch", vector.memset(vx[:, :], 0.0))
                vector.memset(bass.AP(v0.tensor, v0.offset + 64,
                                      [list(v0.ap[0]), [1040, NRB], [65, 16], [1, 1]]),
                              1.0)

        for pj in range(3):
            with nc.Block("d_w%d" % pj) as block:
                @block.gpsimd
                def _(g):
                    sems.inc("ldw", g.dma_start(
                        out=wsb[:, :], in_=wqkv_h.ap()
                        .rearrange("(kc p) m -> p kc m", p=128)
                        [:, :, pj * 1024:(pj + 1) * 1024]), 16)

            nrb = 16 if pj == 0 else NRB
            for rb in range(nrb):
                with nc.Block("d_p%d_%d" % (pj, rb)) as block:
                    mm_base = sems.n["mm"]
                    ev_base = sems.n["ev"]
                    tp_base = sems.n["tp"]
                    sch_base = sems.n["sch"]
                    vch_base = sems.n["vch"]
                    ldr_base = sems.n["ldr"]
                    lxa_base = sems.n["lxa"]

                    @block.gpsimd
                    def _(g):
                        sems.inc("lxa", g.dma_start(
                            out=xtl[:, :].rearrange("p (c m) -> p c m", c=8),
                            in_=xnk_h.ap().rearrange("(c p) n -> p c n", p=128)
                            [:, :, rb * 128:(rb + 1) * 128]), 16)

                    @block.sync
                    def _(sync):
                        if pj < 2:
                            sems.inc("ldr", sync.dma_start(
                                out=rtab[:, :],
                                in_=rtab_h[rb * 128:(rb + 1) * 128, :]), 16)

                    @block.tensor
                    def _(tensor):
                        tensor.wait_ge(sems.h["ldw"], sems.n["ldw"])
                        tensor.wait_ge(sems.h["lxa"], lxa_base + 16)
                        for i, n0 in enumerate((0, 512)):
                            ps = SA if i == 0 else SB
                            last = None
                            for kc in range(8):
                                last = tensor.matmul(
                                    ps[:, 0:512],
                                    xtl[:, kc * 128:(kc + 1) * 128],
                                    wsb[:, kc * 1024 + n0: kc * 1024 + n0 + 512],
                                    start=(kc == 0), stop=(kc == 7))
                            sems.inc("mm", last)
                        if pj == 2:
                            return
                        tensor.wait_ge(sems.h["vch"], vch_base + 6)
                        for cc in range(8):
                            pt = PT if cc % 2 == 0 else PT2
                            if cc >= 2:
                                tensor.wait_ge(sems.h["sch"], sch_base + cc - 1)
                            sems.inc("tp", tensor.transpose(
                                pt[:, 0:128],
                                rowsc[:, cc * 128:(cc + 1) * 128],
                                tri[:, 0:128]))

                    @block.scalar
                    def _(scalar):
                        for i, n0 in enumerate((0, 512)):
                            ps = SA if i == 0 else SB
                            scalar.wait_ge(sems.h["mm"], mm_base + i + 1)
                            if pj == 2:
                                v0 = vx[:, 0:1]
                                dst = bass.AP(v0.tensor,
                                              v0.offset + rb * 1040 + (n0 // 64) * 65,
                                              [list(v0.ap[0]), [65, 8], [1, 64]])
                                sems.inc("ev", scalar.copy(dst, ps[:, 0:512]))
                            else:
                                sems.inc("ev", scalar.activation(
                                    rowsc[:, n0:n0 + 512], ps[:, 0:512], AF.Copy,
                                    scale=(0.125 if pj == 0 else 1.0)))
                        if pj == 2:
                            return
                        dstT = qT if pj == 0 else kT
                        nlen = ROWS if pj == 0 else NK
                        for cc in range(8):
                            pt = PT if cc % 2 == 0 else PT2
                            scalar.wait_ge(sems.h["tp"], tp_base + cc + 1)
                            sems.inc("sch", scalar.copy(
                                dstT[:, cc * nlen + rb * 128: cc * nlen + rb * 128 + 128],
                                pt[:, 0:128]))

                    @block.vector
                    def _(vector):
                        if pj == 2:
                            return
                        vector.wait_ge(sems.h["ev"], ev_base + 2)
                        vector.wait_ge(sems.h["ldr"], ldr_base + 16)
                        r0 = rowsc[:, 0:1]
                        p0 = list(r0.ap[0])
                        ev_ = bass.AP(r0.tensor, r0.offset, [p0, [64, 16], [2, 16]])
                        od = bass.AP(r0.tensor, r0.offset + 1, [p0, [64, 16], [2, 16]])
                        co = rtab[:, 0:256].rearrange("p (h i) -> p h i", h=16)
                        si = rtab[:, 256:512].rearrange("p (h i) -> p h i", h=16)
                        t1 = bsc[:, 0:256].rearrange("p (h i) -> p h i", h=16)
                        t2 = bsc[:, 256:512].rearrange("p (h i) -> p h i", h=16)
                        sems.chain(vector, "vch", vector.tensor_tensor(t1, ev_, si, ALU.mult))
                        sems.chain(vector, "vch", vector.tensor_tensor(t2, od, si, ALU.mult))
                        sems.chain(vector, "vch", vector.tensor_tensor(ev_, ev_, co, ALU.mult))
                        sems.chain(vector, "vch", vector.tensor_tensor(od, od, co, ALU.mult))
                        sems.chain(vector, "vch", vector.tensor_tensor(ev_, ev_, t2, ALU.subtract))
                        sems.chain(vector, "vch", vector.tensor_tensor(od, od, t1, ALU.add))

        def emit_norm1(tag, tile, nrows, cell):
            rts = chunked(nrows, 512)
            with nc.Block(tag) as block:
                vch_base = sems.n["vch"]
                mm_base2 = sems.n["mm"]
                sch_base = sems.n["sch"]

                @block.vector
                def _(vector):
                    for ri, (t0, tl) in enumerate(rts):
                        for c in range(8):
                            j = ri * 8 + c
                            if j:
                                vector.wait_ge(sems.h["mm"], mm_base2 + j)
                            sems.inc("vch", vector.tensor_tensor(
                                sqs[:, 0:tl],
                                tile[:, c * nrows + t0: c * nrows + t0 + tl],
                                tile[:, c * nrows + t0: c * nrows + t0 + tl], ALU.mult))
                        vector.wait_ge(sems.h["sch"], sch_base + ri + 1)
                        sems.inc("vch", vector.reduce_max(
                            srow[:, 8:9], nrm[:, 0:tl], axis=mybir.AxisListType.X))
                        vector.wait_ge(sems.h["vch"], sems.n["vch"])
                        sems.inc("vch", vector.tensor_tensor(
                            srow[:, cell:cell + 1], srow[:, cell:cell + 1],
                            srow[:, 8:9], ALU.max))

                @block.tensor
                def _(tensor):
                    for ri, (t0, tl) in enumerate(rts):
                        for c in range(8):
                            tensor.wait_ge(sems.h["vch"], vch_base + 10 * ri + c + 1)
                            if c == 0 and ri > 0:
                                tensor.wait_ge(sems.h["sch"], sch_base + ri)
                            sems.inc("mm", tensor.matmul(
                                PM[0:1, 0:tl], ones_colb[:, :], sqs[:, 0:tl],
                                start=(c == 0), stop=(c == 7)))

                @block.scalar
                def _(scalar):
                    for ri, (t0, tl) in enumerate(rts):
                        scalar.wait_ge(sems.h["mm"], mm_base2 + 8 * (ri + 1))
                        if ri >= 1:
                            scalar.wait_ge(sems.h["vch"], vch_base + 10 * ri)
                        sems.inc("sch", scalar.copy(nrm[:, 0:tl], PM[0:1, 0:tl]))

        with nc.Block("d_m0") as block:
            @block.vector
            def _(vector):
                vector.memset(srow[:, 1:3], 0.0)

        emit_norm1("d_nq", qT, ROWS, 1)
        emit_norm1("d_nk", kT, NK, 2)

        with nc.Block("d_m") as block:
            @block.vector
            def _(vector):
                sems.chain(vector, "vch", vector.tensor_tensor(
                    srow[:, 3:4], srow[:, 1:2], srow[:, 2:3], ALU.mult))

            @block.scalar
            def _(scalar):
                scalar.wait_ge(sems.h["vch"], sems.n["vch"])
                sems.inc("sch", scalar.activation(srow[:, 4:5], srow[:, 3:4], AF.Sqrt))

            @block.vector
            def _(vector):
                vector.wait_ge(sems.h["sch"], sems.n["sch"])
                sems.chain(vector, "vch", vector.tensor_scalar(
                    srow[:, 5:6], srow[:, 4:5], -1.0, None, ALU.mult, ALU.bypass))

        with nc.Block("d_mbc") as block:
            @block.tensor
            def _(tensor):
                tensor.wait_ge(sems.h["ld"], sems.n["ld"])
                tensor.matmul(PM[:, 0:1], onesr[:, :], srow[:, 5:6], start=True, stop=True)
                tensor.matmul(PM[:, 1:2], onesr[:, :], srow[:, 0:1], start=True, stop=True)

        with nc.Block("d_mbc2") as block:
            @block.scalar
            def _(scalar):
                scalar.copy(colbuf[:, 0:2], PM[:, 0:2])

        with nc.Block("d_vf") as block:
            @block.vector
            def _(vector):
                vector.tensor_scalar(vx[:, 16 * 1040:32 * 1040],
                                     vx[:, 16 * 1040:32 * 1040],
                                     colbuf[:, 1:2], None, ALU.mult, ALU.bypass)

        all_batches = []
        for ib in range(NIB):
            chunksl = list(range(0, ib + 1)) + list(range(16, 32))
            all_batches.append([chunksl[i:i + 4] for i in range(0, len(chunksl), 4)])

        flat = []
        for ib in range(NIB):
            for bi, batch in enumerate(all_batches[ib]):
                flat.append((ib, batch, bi == len(all_batches[ib]) - 1, bi == 0))
        for hp in range(8):
            with nc.Block("d_at%d" % hp) as block:
                mm_base3 = sems.n["mm"]
                ex_base = sems.n["ex"]
                pv_base = sems.n["pv"]
                ev_base3 = sems.n["ev"]
                stn = ["st", "st2", "st3", "st4"]
                stb = [sems.n[x] for x in stn]

                @block.tensor
                def _(tensor):
                    def emit_pv(j):
                        ibj, batchj, lastbj, firstbj = flat[j]
                        tensor.wait_ge(sems.h["ex"], ex_base + j + 1)
                        if firstbj and ibj >= 1:
                            tensor.wait_ge(sems.h["ev"], ev_base3 + ibj)
                        po = (j % 2) * 1024
                        last = None
                        for k, c in enumerate(batchj):
                            lastc = lastbj and (k == len(batchj) - 1)
                            for hh, poff in ((0, 0), (1, 256)):
                                h = 2 * hp + hh
                                last = tensor.matmul(
                                    PV1[:, poff: poff + 65],
                                    ptil[:, po + hh * 512 + k * 128:
                                         po + hh * 512 + k * 128 + 128],
                                    vx[:, c * 1040 + h * 65: c * 1040 + h * 65 + 65],
                                    start=(firstbj and k == 0 and hh == 0),
                                    stop=(lastc and hh == 1))
                        sems.inc("pv", last)

                    for nx, (ib, batch, lastb, firstb) in enumerate(flat):
                        psA = SA if (nx % 2 == 0) else SC
                        psB = SB if (nx % 2 == 0) else SD
                        if nx >= 2:
                            tensor.wait_ge(sems.h["ex"], ex_base + nx - 1)
                        for k, c in enumerate(batch):
                            dg = (c == ib)
                            tensor.matmul(
                                psA[:, k * 128:k * 128 + 128],
                                kT[0:64, hp * NK + c * 128: hp * NK + c * 128 + 128],
                                qT[0:64, hp * ROWS + ib * 128: hp * ROWS + ib * 128 + 128],
                                start=True, stop=not dg, tile_position=(0, 0))
                            if dg:
                                tensor.matmul(
                                    psA[:, k * 128:k * 128 + 128],
                                    tri[:, 256:384], tri[:, 0:128],
                                    start=False, stop=True)
                            lastB = tensor.matmul(
                                psB[:, k * 128:k * 128 + 128],
                                kT[64:128, hp * NK + c * 128: hp * NK + c * 128 + 128],
                                qT[64:128, hp * ROWS + ib * 128: hp * ROWS + ib * 128 + 128],
                                start=True, stop=not dg, tile_position=(64, 0))
                            if dg:
                                lastB = tensor.matmul(
                                    psB[:, k * 128:k * 128 + 128],
                                    tri[:, 256:384], tri[:, 0:128],
                                    start=False, stop=True)
                        sems.inc("mm", lastB)
                        if nx >= 1:
                            emit_pv(nx - 1)
                    emit_pv(len(flat) - 1)

                @block.scalar
                def _(scalar):
                    for nx, (ib, batch, lastb, firstb) in enumerate(flat):
                        psA = SA if (nx % 2 == 0) else SC
                        psB = SB if (nx % 2 == 0) else SD
                        po = (nx % 2) * 1024
                        scalar.wait_ge(sems.h["mm"], mm_base3 + nx + 1)
                        if nx >= 2:
                            scalar.wait_ge(sems.h["pv"], pv_base + nx - 1)
                        scalar.activation(
                            ptil[:, po:po + 128 * len(batch)],
                            psA[:, 0:128 * len(batch)],
                            AF.Exp, bias=colbuf[:, 0:1])
                        sems.inc("ex", scalar.activation(
                            ptil[:, po + 512:po + 512 + 128 * len(batch)],
                            psB[:, 0:128 * len(batch)],
                            AF.Exp, bias=colbuf[:, 0:1]))

                @block.vector
                def _(vector):
                    npvtot = 0
                    for ib in range(NIB):
                        npvtot += len(all_batches[ib])
                        slot = (ib % 4) * 128
                        vector.wait_ge(sems.h["pv"], pv_base + npvtot)
                        if ib >= 4:
                            vector.wait_ge(sems.h[stn[ib % 4]],
                                           stb[ib % 4] + 16 * ((ib - 4) // 4 + 1))
                        for hh, poff in ((0, 0), (1, 256)):
                            sems.chain(vector, "vch", vector.reciprocal(
                                colbuf[:, 4:5], PV1[:, poff + 64: poff + 65]))
                            sems.chain(vector, "vch", vector.tensor_scalar(
                                attnb[:, slot + hh * 64: slot + hh * 64 + 64],
                                PV1[:, poff: poff + 64],
                                colbuf[:, 4:5], None, ALU.mult, ALU.bypass))
                        sems.inc("ev", vector.engine_nop())

                @block.sync
                def _(sync):
                    for ib in range(NIB):
                        slot = (ib % 4) * 128
                        sync.wait_ge(sems.h["ev"], ev_base3 + ib + 1)
                        sems.inc(stn[ib % 4], sync.dma_start(
                            out=at_h[ib * 128:(ib + 1) * 128,
                                     hp * 128:(hp + 1) * 128],
                            in_=attnb[:, slot:slot + 128]), 16)
                    for x in stn:
                        sync.wait_ge(sems.h[x], sems.n[x])





def build_attn1a():
    nc = bass.Bass("TRN2", target_bir_lowering=False)
    H = dram_attn1a(nc)
    with contextlib.ExitStack() as c0:
        sems = Sems(nc, c0, ["ld", "ldg", "ldw", "ldr", "lxa", "ev", "st", "st2", "st3", "st4", "vch", "sch", "mm", "ex", "pv", "tp"])
        emit_attn1a(nc, sems, H)
    return nc


def dram_attn1b(nc, sfx="", ext=True):
    k = lambda io: ({"kind": "External" + io} if ext else {})
    return dict(
        at=nc.dram_tensor("at" + sfx, [ROWS, 1024], BF16, **k("Input")),
        wout=nc.dram_tensor("wout" + sfx, [1024, 1024], F32, **k("Input")),
        tri=nc.dram_tensor("tri" + sfx, [128, 384], F32, **k("Input")),
        a1=nc.dram_tensor("a1" + sfx, [1024, ROWS], F32, **k("Output")),
    )


def emit_attn1b(nc, sems, H):
    at_h = H["at"]
    wout_h = H["wout"]
    tri_h = H["tri"]
    a1_h = H["a1"]

    with contextlib.ExitStack() as ctx:
        e = ctx.enter_context
        attn = e(nc.sbuf_tensor("attn_b", [128, 16 * 1024], BF16))
        attnT = e(nc.sbuf_tensor("attnT_b", [128, 8 * ROWS], BF16))
        wsb = e(nc.sbuf_tensor("wsb_b", [128, 8 * 1024], BF16))
        tri = e(nc.sbuf_tensor("tri_b", [128, 384], BF16))
        outb = e(nc.sbuf_tensor("outb_b", [128, 2 * 512], F32))
        PT = e(nc.psum_tensor("PTc", [128, 1024], BF16))
        PT2 = e(nc.psum_tensor("PTd", [128, 1024], BF16))
        SA = e(nc.psum_tensor("SAb", [128, 512], F32))
        SB = e(nc.psum_tensor("SBb", [128, 512], F32))

        with nc.Block("g_ld") as block:
            @block.gpsimd
            def _(g):
                sems.inc("ldw", g.dma_start(out=tri[:, :], in_=tri_h[:, :]), 16)
                sems.inc("ldw", g.dma_start(
                    out=attn[:, :],
                    in_=at_h.ap().rearrange("(rb p) m -> p rb m", p=128)), 16)
                sems.inc("ldw", g.dma_start(
                    out=wsb[:, :], in_=wout_h.ap().rearrange("(kc p) m -> p kc m", p=128)), 16)

        with nc.Block("g_tr") as block:
            mm_base = sems.n["mm"]
            ev_base = sems.n["ev"]
            tp_base = sems.n["tp"]
            sch_base = sems.n["sch"]
            jobs = [(rb, cc) for rb in range(16) for cc in range(8)]

            @block.tensor
            def _(tensor):
                tensor.wait_ge(sems.h["ldw"], sems.n["ldw"])
                for i, (rb, cc) in enumerate(jobs):
                    pt = PT if i % 2 == 0 else PT2
                    if i >= 2:
                        tensor.wait_ge(sems.h["sch"], sch_base + i - 1)
                    sems.inc("tp", tensor.transpose(
                        pt[:, 0:128],
                        attn[:, rb * 1024 + cc * 128: rb * 1024 + cc * 128 + 128],
                        tri[:, 0:128]))

            @block.scalar
            def _(scalar):
                for i, (rb, cc) in enumerate(jobs):
                    pt = PT if i % 2 == 0 else PT2
                    scalar.wait_ge(sems.h["tp"], tp_base + i + 1)
                    sems.inc("sch", scalar.copy(
                        attnT[:, cc * ROWS + rb * 128: cc * ROWS + rb * 128 + 128],
                        pt[:, 0:128]))

        with nc.Block("g_wo") as block:
            mm_base = sems.n["mm"]
            ev_base = sems.n["ev"]
            st_base = sems.n["st"]
            st2_base = sems.n["st2"]
            jobs = [(mc, r0) for mc in range(8) for r0 in range(0, ROWS, 512)]

            @block.tensor
            def _(tensor):
                for i, (mc, r0) in enumerate(jobs):
                    ps = SA if i % 2 == 0 else SB
                    if i >= 2:
                        tensor.wait_ge(sems.h["ev"], ev_base + i - 1)
                    last = None
                    for kc in range(8):
                        last = tensor.matmul(
                            ps[:, 0:512],
                            wsb[:, (kc * 8 + mc) * 128:(kc * 8 + mc) * 128 + 128],
                            attnT[:, kc * ROWS + r0: kc * ROWS + r0 + 512],
                            start=(kc == 0), stop=(kc == 7))
                    sems.inc("mm", last)

            @block.vector
            def _(vector):
                for i, (mc, r0) in enumerate(jobs):
                    ps = SA if i % 2 == 0 else SB
                    half = (i % 2) * 512
                    vector.wait_ge(sems.h["mm"], mm_base + i + 1)
                    if i >= 2:
                        sm2 = "st" if i % 2 == 0 else "st2"
                        sbb = st_base if i % 2 == 0 else st2_base
                        vector.wait_ge(sems.h[sm2], sbb + 16 * ((i - 2) // 2 + 1))
                    sems.inc("ev", vector.tensor_copy(outb[:, half:half + 512],
                                                      ps[:, 0:512]))

            @block.scalar
            def _(scalar):
                for i, (mc, r0) in enumerate(jobs):
                    half = (i % 2) * 512
                    sm2 = "st" if i % 2 == 0 else "st2"
                    scalar.wait_ge(sems.h["ev"], ev_base + i + 1)
                    sems.inc(sm2, scalar.dma_start(
                        out=a1_h[mc * 128:(mc + 1) * 128, r0:r0 + 512],
                        in_=outb[:, half:half + 512]), 16)
                scalar.wait_ge(sems.h["st"], sems.n["st"])
                scalar.wait_ge(sems.h["st2"], sems.n["st2"])





def build_attn1b():
    nc = bass.Bass("TRN2", target_bir_lowering=False)
    H = dram_attn1b(nc)
    with contextlib.ExitStack() as c0:
        sems = Sems(nc, c0, ["ld", "ldw", "ev", "st", "st2", "vch", "sch", "mm", "tp"])
        emit_attn1b(nc, sems, H)
    return nc


def dram_ffn1(nc, sfx="", ext=True):
    k = lambda io: ({"kind": "External" + io} if ext else {})
    return dict(
        xn=nc.dram_tensor("xn" + sfx, [1024, ROWS], F32, **k("Input")),
        x1=nc.dram_tensor("x1" + sfx, [1024, ROWS], F32, **k("Input")),
        a1=nc.dram_tensor("a1" + sfx, [1024, ROWS], F32, **k("Input")),
        wt=nc.dram_tensor("wt" + sfx, [1024, 1024], F32, **k("Input")),
        bias=nc.dram_tensor("bias" + sfx, [128, 8], F32, **k("Input")),
        isc=nc.dram_tensor("isc" + sfx, [1, 1], F32, **k("Input")),
        wsc=nc.dram_tensor("wsc" + sfx, [1, 1], F32, **k("Input")),
        out=nc.dram_tensor("out" + sfx, [1024, ROWS], F32, **k("Output")),
    )


def emit_ffn1(nc, sems, H):
    xn_h = H["xn"]
    x1_h = H["x1"]
    a1_h = H["a1"]
    wt_h = H["wt"]
    bias_h = H["bias"]
    isc_h = H["isc"]
    wsc_h = H["wsc"]
    out_h = H["out"]
    n = ROWS
    with contextlib.ExitStack() as ctx:
        e = ctx.enter_context
        xs = e(nc.sbuf_tensor("xs_q", [128, 8 * n], F32))
        x1s = e(nc.sbuf_tensor("x1s_q", [128, 8 * n], F32))
        x8 = e(nc.sbuf_tensor("x8_q", [128, 8 * n], FP8))
        w32 = e(nc.sbuf_tensor("w32_q", [128, 8 * 1024], F32))
        w8 = e(nc.sbuf_tensor("w8_q", [128, 8 * 1024], FP8))
        cols = e(nc.sbuf_tensor("cols_q", [128, 16], F32))
        srow = e(nc.sbuf_tensor("srow_q", [1, 8], F32))
        onesr = e(nc.sbuf_tensor("onesr_q", [1, 128], F32))
        PA = e(nc.psum_tensor("PA_q", [128, 1024], F32))
        PM = e(nc.psum_tensor("PM_q", [128, 512], F32))

        with nc.Block("e_ld") as block:
            @block.sync
            def _(sync):
                sems.inc("ld", sync.dma_start(
                    out=xs[:, :], in_=xn_h.ap().rearrange("(c p) n -> p c n", p=128)), 16)
                sems.inc("ld", sync.dma_start(
                    out=w32[:, :], in_=wt_h.ap().rearrange("(kc p) m -> p kc m", p=128)), 16)
                sems.inc("ld", sync.dma_start(out=cols[:, 8:16], in_=bias_h[:, :]), 16)
                sems.inc("ld", sync.dma_start(out=srow[:, 0:1], in_=isc_h[:, :]), 16)
                sems.inc("ld", sync.dma_start(out=srow[:, 1:2], in_=wsc_h[:, :]), 16)

            @block.vector
            def _(vector):
                vector.memset(onesr[:, :], 1.0)

        with nc.Block("e_sc") as block:
            @block.vector
            def _(vector):
                vector.wait_ge(sems.h["ld"], sems.n["ld"])
                sems.chain(vector, "vch", vector.reciprocal(srow[:, 2:3], srow[:, 0:1]))
                sems.chain(vector, "vch", vector.tensor_tensor(
                    srow[:, 3:4], srow[:, 0:1], srow[:, 1:2], ALU.mult))
                for c in range(8):
                    sems.chain(vector, "vch", vector.tensor_copy(
                        w8[:, c * 1024:(c + 1) * 1024], w32[:, c * 1024:(c + 1) * 1024]))

        with nc.Block("e_bc") as block:
            @block.tensor
            def _(tensor):
                tensor.matmul(PM[:, 0:1], onesr[:, :], srow[:, 2:3], start=True, stop=True)
                tensor.matmul(PM[:, 1:2], onesr[:, :], srow[:, 3:4], start=True, stop=True)

        with nc.Block("e_bc2") as block:
            @block.scalar
            def _(scalar):
                scalar.copy(cols[:, 0:2], PM[:, 0:2])

        with nc.Block("e_q") as block:
            @block.vector
            def _(vector):
                for c in range(8):
                    sl = xs[:, c * n:(c + 1) * n]
                    sems.chain(vector, "vch", vector.tensor_scalar(
                        sl, sl, cols[:, 0:1], 448.0, ALU.mult, ALU.min))
                    sems.chain(vector, "vch", vector.tensor_scalar(
                        sl, sl, -448.0, None, ALU.max, ALU.bypass))
                    sems.chain(vector, "vch", vector.tensor_copy(
                        x8[:, c * n:(c + 1) * n], sl))

        with nc.Block("e_ld2") as block:
            @block.sync
            def _(sync):
                sems.inc("ld", sync.dma_start(
                    out=x1s[:, :], in_=x1_h.ap().rearrange("(c p) n -> p c n", p=128)), 16)

        with nc.Block("e_mm") as block:
            mm_base = sems.n["mm"]
            ev_base = sems.n["ev"]
            jobs = [(mc, r0) for mc in range(8) for r0 in range(0, n, 512)]

            @block.tensor
            def _(tensor):
                for i, (mc, r0) in enumerate(jobs):
                    half = (i % 2) * 512
                    if i >= 2:
                        tensor.wait_ge(sems.h["ev"], ev_base + i - 1)
                    last = None
                    for kc in range(8):
                        last = tensor.matmul(
                            PA[:, half:half + 512],
                            w8[:, (kc * 8 + mc) * 128:(kc * 8 + mc) * 128 + 128],
                            x8[:, kc * n + r0: kc * n + r0 + 512],
                            start=(kc == 0), stop=(kc == 7))
                    sems.inc("mm", last)

            @block.vector
            def _(vector):
                vector.wait_ge(sems.h["ld"], sems.n["ld"])
                for i, (mc, r0) in enumerate(jobs):
                    half = (i % 2) * 512
                    vector.wait_ge(sems.h["mm"], mm_base + i + 1)
                    dst = x1s[:, mc * n + r0: mc * n + r0 + 512]
                    sems.chain(vector, "vch", vector.tensor_scalar(
                        PA[:, half:half + 512] if False else xs[:, mc * n + r0: mc * n + r0 + 512],
                        PA[:, half:half + 512],
                        cols[:, 1:2], cols[:, 8 + mc:9 + mc], ALU.mult, ALU.add))
                    sems.inc("ev", vector.tensor_tensor(
                        dst, dst, xs[:, mc * n + r0: mc * n + r0 + 512], ALU.add))

        with nc.Block("e_ld3") as block:
            @block.sync
            def _(sync):
                sems.inc("ld", sync.dma_start(
                    out=xs[:, :], in_=a1_h.ap().rearrange("(c p) n -> p c n", p=128)), 16)

        with nc.Block("e_out") as block:
            @block.vector
            def _(vector):
                vector.wait_ge(sems.h["ld"], sems.n["ld"])
                for c in range(8):
                    sems.inc("vch", vector.tensor_tensor(
                        x1s[:, c * n:(c + 1) * n], x1s[:, c * n:(c + 1) * n],
                        xs[:, c * n:(c + 1) * n], ALU.add))

        with nc.Block("e_st") as block:
            @block.sync
            def _(sync):
                sems.inc("st", sync.dma_start(
                    out=out_h.ap().rearrange("(c p) n -> p c n", p=128), in_=x1s[:, :]), 16)
                sync.wait_ge(sems.h["st"], sems.n["st"])



def build_ffn1():
    nc = bass.Bass("TRN2", target_bir_lowering=False)
    H = dram_ffn1(nc)
    with contextlib.ExitStack() as c0:
        sems = Sems(nc, c0, ["ld", "ev", "st", "vch", "sch", "mm"])
        emit_ffn1(nc, sems, H)
    return nc




def _tri_consts():
    a = np.arange(128)
    ident = np.eye(128, dtype=np.float32)
    low = np.where(a[None, :] < a[:, None], -1e9, 0.0).astype(np.float32)
    up = np.where(a[None, :] > a[:, None], -1e9, 0.0).astype(np.float32)
    return np.concatenate([ident, low, up], axis=1)


def _ind_consts():
    ind = np.zeros((128, 64), dtype=np.float32)
    indt = np.zeros((16, 512), dtype=np.float32)
    p = np.arange(128)
    for j in range(4):
        for hloc in range(16):
            m = (4 * j + (p // 32)) == hloc
            ind[m, 16 * j + hloc] = 1.0
            indt[hloc, 128 * j + np.where(m)[0]] = 1.0
    return ind, indt


def ffn0_attn_inputs(ins, layer, core, xa_prev_orig):
    b, h = core // 2, core % 2
    own0 = h * ROWS
    own = xa_prev_orig[b, own0:own0 + ROWS, :]
    if h == 0:
        halo = np.zeros((257, D), dtype=np.float32)
    else:
        halo = xa_prev_orig[b, own0 - 257:own0, :]
    xf = np.ascontiguousarray(np.concatenate([halo, own], axis=0).T)
    Wqkv = np.asarray(ins["L0_ffn_Wqkv"][layer])
    wq = np.ascontiguousarray(Wqkv[:, 0:1024][:, INNER_QK_COLS])
    wk = np.ascontiguousarray(Wqkv[:, 1024:2048][:, INNER_QK_COLS])
    wv = np.ascontiguousarray(Wqkv[:, 2048:3072])
    wo = np.ascontiguousarray(np.asarray(ins["L0_ffn_Wo"][layer]))
    qs = np.asarray(ins["L0_ffn_q_scale"][layer])
    ks = np.asarray(ins["L0_ffn_k_scale"][layer])
    p = np.arange(128)
    qsc = np.stack([qs[p % 32], qs[32 + p % 32], ks[p % 32], ks[32 + p % 32]],
                   axis=1).astype(np.float32)
    qsr = np.concatenate([qs, ks])[None, :].astype(np.float32)
    pos_q = np.arange(own0, own0 + ROWS)
    pos_k = np.maximum(np.arange(own0 - 256, own0 + ROWS), 0)
    tq = rope_inner(pos_q)
    tk = rope_inner(pos_k)
    hpo = np.full((1, 128), -1e9 if h == 0 else 0.0, dtype=np.float32)
    ind, indt = _ind_consts()
    return dict(xf=xf, wq=wq, wk=wk, wv=wv, wo=wo, qsc=qsc, qsr=qsr,
                rcq=tq[0], rsq=tq[1], rck=tk[2], rsk=tk[3],
                tri=_tri_consts(), hpo=hpo, ind=ind, indt=indt,
                hflag=np.full((1, 1), 0.0 if h == 0 else 1.0, dtype=np.float32))


def _colify(v, nch):
    out = np.zeros((128, nch), dtype=np.float32)
    v = np.asarray(v)
    for c in range(nch):
        seg = v[c * 128:(c + 1) * 128]
        out[0:len(seg), c] = seg
    return out


def ffn0_ffn_inputs(ins, layer, core, xa_orig):
    b, h = core // 2, core % 2
    own0 = h * ROWS
    own = xa_orig[b, own0:own0 + ROWS, :]
    halo = np.zeros((1, D), dtype=np.float32) if h == 0 else xa_orig[b, own0 - 1:own0, :]
    xa = np.ascontiguousarray(np.concatenate([halo, own], axis=0).T)
    W1 = np.asarray(ins["L0_ffn_W1"][layer])
    W2 = np.asarray(ins["L0_ffn_W2"][layer])
    b1 = np.asarray(ins["L0_ffn_b1"][layer])
    w1a = np.zeros((1024, 2816), dtype=np.float32)
    w1g = np.zeros((1024, 2816), dtype=np.float32)
    w1a[:, 0:2730] = W1[:, 0:2730]
    w1g[:, 0:2730] = W1[:, 2730:5460]
    w2 = np.zeros((2816, 1024), dtype=np.float32)
    w2[0:2730, :] = W2
    return dict(
        xa=xa, w1a=w1a, w1g=w1g, w2=w2,
        lng=_colify(ins["L0_ffn_ln_g"][layer], 8),
        lnb=_colify(ins["L0_ffn_ln_b"][layer], 8),
        b1a=_colify(b1[0:2730], 22), b1g=_colify(b1[2730:5460], 22),
        b2=_colify(ins["L0_ffn_b2"][layer], 8),
        hflag=np.full((1, 1), 0.0 if h == 0 else 1.0, dtype=np.float32))


def attn0_inputs(ins, core, xn_rho):
    b, h = core // 2, core % 2
    own0 = h * ROWS
    own = xn_rho[b, own0:own0 + ROWS, :]
    par = xn_rho[b, ROWS - own0:2 * ROWS - own0, :]
    xnk = np.ascontiguousarray(np.concatenate([own, par], axis=0).T)
    wq = np.asarray(ins["L0_attn_w_q"])
    wk = np.asarray(ins["L0_attn_w_k"])
    wv = np.asarray(ins["L0_attn_w_v"])

    def lay(w):
        out = np.zeros((64, 1024), dtype=np.float32)
        for c in range(8):
            for sub in range(2):
                H = (4 * c + 2 * sub) if c < 4 else (4 * (c - 4) + 2 * sub + 1)
                out[:, (c * 2 + sub) * 64:(c * 2 + sub) * 64 + 64] = w[:, :, H]
        return out
    wqkv = np.concatenate([lay(wq), lay(wk), lay(wv)], axis=1)
    wqkv = np.concatenate([wqkv, wqkv], axis=0)
    pos = np.concatenate([np.arange(own0, own0 + ROWS),
                          np.arange(ROWS - own0, 2 * ROWS - own0)])
    rc, rs = rope_l0attn(pos)
    vflag = np.full((1, 1), 1.0 if h == 1 else 0.0, dtype=np.float32)
    return dict(xnk=xnk, wqkv=wqkv, rc=rc, rs=rs, tri=_tri_consts(), vflag=vflag)

ALLSEMS = ["ld", "ldg", "ldw", "ldr", "ldr2", "ldx", "lwa", "lwb", "lxa", "lxb",
           "ev", "st", "st2", "st3", "st4", "vch", "sch", "mm", "ex", "pv",
           "tp", "bc_mm", "bc_ev", "sg", "sg2", "gl", "sx", "rch"] + ["lc%d" % i for i in range(8)] + \
          ["cc%d" % i for i in range(13)]

I32 = mybir.dt.int32


class _NCShim:

    def __init__(self, nc, sfx, repl):
        self._nc, self._sfx, self._repl = nc, sfx, repl

    def dram_tensor(self, name, shape, dt, **kw):
        base = name[:len(name) - len(self._sfx)] if self._sfx else name
        r = self._repl.get(base)
        if r is not None:
            return r
        return self._nc.dram_tensor(name, shape, dt, **kw)


class _UniqNC:

    def __init__(self, nc, tag):
        self._nc, self._tag = nc, tag

    def sbuf_tensor(self, name, *a, **k):
        return self._nc.sbuf_tensor(name + self._tag, *a, **k)

    def psum_tensor(self, name, *a, **k):
        return self._nc.psum_tensor(name + self._tag, *a, **k)

    def Block(self, name=None, *a, **k):
        if name is None:
            return self._nc.Block(*a, **k)
        return self._nc.Block(name + self._tag, *a, **k)

    def __getattr__(self, x):
        return getattr(self._nc, x)


def _glue(nc, sems, tag, copies, collective=None, ccname=None):
    if collective is not None:
        cin, cout = collective
        with nc.Block(tag + "_a") as block:
            @block.gpsimd
            def _(g):
                g.collective_compute(
                    "AllGather", ALU.bypass,
                    replica_groups=[[0, 1], [2, 3], [4, 5], [6, 7]],
                    ins=[cin], outs=[cout]).then_inc(sems.h[ccname], 1)
                sems.n[ccname] += 1
                g.wait_ge(sems.h[ccname], sems.n[ccname])

    if copies:
        with nc.Block(tag + "_c") as block:
            @block.sync
            def _(sync):
                for dst, src in copies:
                    sems.inc("gl", sync.dma_start(
                        out=dst, in_=src, allow_slow_non_contiguous=True), 16)
                sync.wait_ge(sems.h["gl"], sems.n["gl"])


def _ln0full(nc, sems, x0f, xn0f):
    n = ROWS
    with contextlib.ExitStack() as ctx:
        e = ctx.enter_context
        xs = e(nc.sbuf_tensor("z_xs", [128, 8 * n], F32))
        sq = e(nc.sbuf_tensor("z_sq", [128, 1024], F32))
        stat = e(nc.sbuf_tensor("z_stat", [1, 2 * n], F32))
        ones_col = e(nc.sbuf_tensor("z_oc", [128, 1], F32))
        onesr = e(nc.sbuf_tensor("z_or", [1, 128], F32))
        eps_col = e(nc.sbuf_tensor("z_eps", [1, 1], F32))
        Ab = e(nc.sbuf_tensor("z_Ab", [128, n], F32))
        Bb = e(nc.sbuf_tensor("z_Bb", [128, n], F32))
        ps = e(nc.psum_tensor("z_ps", [1, n], F32))
        pb2 = e(nc.psum_tensor("z_pb2", [128, 1024], F32))

        for hf in range(2):
            with nc.Block("z_ld%d" % hf) as block:
                @block.sync
                def _(sync):
                    sems.inc("ld", sync.dma_start(
                        out=xs[:, :],
                        in_=x0f.ap().rearrange("(c p) n -> p c n", p=128)
                        [:, :, hf * n:(hf + 1) * n]), 16)
                    sync.wait_ge(sems.h["ld"], sems.n["ld"])

                @block.vector
                def _(vector):
                    if hf == 0:
                        vector.memset(ones_col[:, :], 1.0)
                        vector.memset(onesr[:, :], 1.0)
                        vector.memset(eps_col[:, :], 1e-5)

            emit_ln(nc, sems, xs, n, sq, stat, ones_col, onesr, ps, pb2, Ab, Bb,
                    "z_ln%d" % hf, eps_col=eps_col)

            with nc.Block("z_app%d" % hf) as block:
                @block.vector
                def _(vector):
                    for c in range(8):
                        sems.chain(vector, "vch", vector.tensor_tensor(
                            xs[:, c * n:(c + 1) * n], xs[:, c * n:(c + 1) * n],
                            Ab[:, :], ALU.mult))
                        vector.tensor_tensor(
                            xs[:, c * n:(c + 1) * n], xs[:, c * n:(c + 1) * n],
                            Bb[:, :], ALU.add)

            with nc.Block("z_st%d" % hf) as block:
                @block.sync
                def _(sync):
                    sems.inc("st", sync.dma_start(
                        out=xn0f.ap().rearrange("(c p) n -> p c n", p=128)
                        [:, :, hf * n:(hf + 1) * n], in_=xs[:, :]), 16)
                    sync.wait_ge(sems.h["st"], sems.n["st"])


def _perm_copies(dst, dcol, dwid, src, scol, swid, ncols, rho_to_orig):
    out = []
    d0 = dst[0:1, 0:1]
    s0 = src[0:1, 0:1]
    for h in range(16):
        r = (h % 2) * 512 + (h // 2) * 64
        if rho_to_orig:
            dap = bass.AP(d0.tensor, d0.offset + h * dwid + dcol,
                          [[16 * dwid, 64], [1, ncols]])
            sap = bass.AP(s0.tensor, s0.offset + r * swid + scol,
                          [[swid, 64], [1, ncols]])
        else:
            dap = bass.AP(d0.tensor, d0.offset + r * dwid + dcol,
                          [[dwid, 64], [1, ncols]])
            sap = bass.AP(s0.tensor, s0.offset + h * swid + scol,
                          [[16 * swid, 64], [1, ncols]])
        out.append((dap, sap))
    return out


def build_fused(dbg=()):
    nc = bass.Bass("TRN2", target_bir_lowering=False)

    def it(name, shape, dt):
        kw = {"kind": "ExternalOutput"} if name in dbg else {}
        return nc.dram_tensor(name, shape, dt, **kw)

    x0f = nc.dram_tensor("x0f", [1024, 2 * ROWS], F32, kind="ExternalInput")
    cfg = nc.dram_tensor("cfg", [1, 1], I32, kind="ExternalInput")
    out_h = nc.dram_tensor("out", [1024, ROWS], F32, kind="ExternalOutput")
    xn0f = it("xn0f", [1024, 2 * ROWS], F32)
    a0d = it("a0d", [ROWS, 1024], BF16)
    xf0d = it("xf0d", [1024, CS], F32)
    xas = [it("xad%d" % i, [1024, ROWS], F32) for i in range(6)]
    xos = [it("xod%d" % i, [1024, ROWS], F32) for i in range(6)]
    f0r = it("f0r", [1024, ROWS], F32)
    x1d = it("x1d", [1024, ROWS], F32)
    xn1d = it("xn1d", [1024, ROWS], F32)
    xnk1 = it("xnk1", [1024, 2 * ROWS], F32)
    atd = it("atd", [ROWS, 1024], BF16)
    a1d = it("a1d", [1024, ROWS], F32)
    cch = nc.dram_tensor("cch", [1024, 257], F32)
    ccho = nc.dram_tensor("ccho", [2048, 257], F32)
    cc1 = nc.dram_tensor("cc1", [1024, 1], F32)
    cc1o = nc.dram_tensor("cc1o", [2048, 1], F32)
    ccx = nc.dram_tensor("ccx", [1024, ROWS], F32)
    ccxo = nc.dram_tensor("ccxo", [2048, ROWS], F32)

    hfl = nc.dram_tensor("hflag", [1, 1], F32, kind="ExternalInput")
    Ha0 = dram_attn0(_NCShim(nc, "", dict(xnk=xn0f, a0=a0d)))
    Hffa = []
    Hfff = []
    for l in range(6):
        sfx = "_%d" % l
        H = dram_ffn0_attn(_NCShim(nc, sfx, dict(
            xf=(xf0d if l == 0 else xos[l - 1]), xa=xas[l], hflag=hfl,
            xl=cc1)), sfx=sfx)
        if l > 0:
            H["xsrc"] = xos[l - 1]
            H["xh257"] = ccho
        Hffa.append(H)
        H = dram_ffn0_ffn(_NCShim(nc, sfx, dict(
            xa=xas[l], xo=xos[l], hflag=hfl, xt=cch)), sfx=sfx)
        H["xh"] = cc1o
        Hfff.append(H)
    Hc = dram_combine(_NCShim(nc, "", dict(
        x0=x0f, a0=a0d, f0=f0r, x1=x1d, xn=xn1d)))
    H1a = dram_attn1a(_NCShim(nc, "_x1", dict(xnk=xnk1, at=atd)), sfx="_x1")
    H1b = dram_attn1b(_NCShim(nc, "_1b", dict(at=atd, a1=a1d)), sfx="_1b")
    H1f = dram_ffn1(_NCShim(nc, "_f1", dict(
        xn=xn1d, x1=x1d, a1=a1d, out=out_h)), sfx="_f1")

    with contextlib.ExitStack() as c0:
        sems = Sems(nc, c0, ALLSEMS)
        greg = c0.enter_context(nc.gpsimd.register("greg"))
        gcfg = c0.enter_context(nc.sbuf_tensor("gcfg", [1, 1], I32))

        _ln0full(nc, sems, x0f, xn0f)
        emit_attn0(_UniqNC(nc, "_ua0"), sems, Ha0)

        _glue(nc, sems, "g_xf0",
              _perm_copies(xf0d, 0, CS, xn0f, 2 * ROWS - 257, 2 * ROWS,
                           257, True) +
              _perm_copies(xf0d, 257, CS, xn0f, 0, 2 * ROWS, ROWS, True))
        for l in range(6):
            emit_ffn0_attn(_UniqNC(nc, "_ufa%d" % l), sems, Hffa[l])
            _glue(nc, sems, "g_xg%d" % l, [],
                  collective=(cc1.ap().opt(), cc1o.ap().opt()),
                  ccname="cc%d" % (2 * l))
            emit_ffn0_ffn(_UniqNC(nc, "_uff%d" % l), sems, Hfff[l])
            if l < 5:
                _glue(nc, sems, "g_xf%d" % (l + 1), [],
                      collective=(cch.ap().opt(), ccho.ap().opt()),
                      ccname="cc%d" % (2 * l + 1))

        _glue(nc, sems, "g_f0",
              _perm_copies(f0r, 0, ROWS, xos[5], 0, ROWS, ROWS, False))
        emit_combine(_UniqNC(nc, "_uc"), sems, Hc)

        _glue(nc, sems, "g_x1", [(ccx[:, :], xn1d[:, :])])
        _glue(nc, sems, "g_x1b", [(xnk1[:, 0:ROWS], xn1d[:, :])],
              collective=(ccx.ap().opt(), ccxo.ap().opt()), ccname="cc12")
        with nc.Block("g_x1c") as block:
            @block.gpsimd
            def _(g):
                sems.inc("ldg", g.dma_start(out=gcfg[:, :], in_=cfg[:, :]), 16)
                g.wait_ge(sems.h["ldg"], sems.n["ldg"])
                g.reg_load(greg, gcfg[:1, :1])
                sems.inc("ldg", g.dma_start(
                    out=xnk1[:, ROWS:2 * ROWS],
                    in_=bass.AP(ccxo, greg, [[ROWS, 1024], [1, ROWS]])), 16)
                g.wait_ge(sems.h["ldg"], sems.n["ldg"])

        emit_attn1a(_UniqNC(nc, "_u1a"), sems, H1a)
        emit_attn1b(_UniqNC(nc, "_u1b"), sems, H1b)
        emit_ffn1(_UniqNC(nc, "_uf1"), sems, H1f)
    return nc


_W_KEYS = dict(
    attn0=("wqkv", "rc", "rs", "tri", "vflag"),
    ffn0_attn=("wq", "wk", "wv", "wo", "qsc", "qsr", "rcq", "rsq", "rck", "rsk",
               "tri", "hpo", "ind", "indt"),
    ffn0_ffn=("w1a", "w1g", "w2", "lng", "lnb", "b1a", "b1g", "b2"),
    attn1a=("wqkv", "rtab", "tri", "vflag"),
    attn1b=("wout", "tri"),
    ffn1=("wt", "bias", "isc", "wsc"),
)


def fused_inputs(ins, core, x0):
    b, h = core // 2, core % 2
    own0 = h * ROWS
    rows_of = x0[b][:, P2C]
    x0f = np.ascontiguousarray(np.concatenate(
        [rows_of[own0:own0 + ROWS], rows_of[ROWS - own0:2 * ROWS - own0]]).T)
    m = dict(x0f=x0f,
             cfg=np.array([[(1 - h) * 1024 * ROWS]], dtype=np.int32),
             hflag=np.full((1, 1), 1.0 if h == 1 else 0.0, dtype=np.float32))
    dummy = np.zeros((B, T, D), dtype=np.float32)
    src = attn0_inputs(ins, core, dummy)
    for k in _W_KEYS["attn0"]:
        m[k] = src[k]
    for l in range(6):
        src = ffn0_attn_inputs(ins, l, core, dummy)
        for k in _W_KEYS["ffn0_attn"]:
            m[k + "_%d" % l] = src[k]
        src = ffn0_ffn_inputs(ins, l, core, dummy)
        for k in _W_KEYS["ffn0_ffn"]:
            m[k + "_%d" % l] = src[k]
    src = attn1a_inputs(ins, core, dummy)
    for k in _W_KEYS["attn1a"]:
        m[k + "_x1"] = src[k]
    src = attn1b_inputs(ins, core, np.zeros((ROWS, 1024), dtype=np.float32))
    for k in _W_KEYS["attn1b"]:
        m[k + "_1b"] = src[k]
    src = ffn1_inputs(ins, core, None, None, None)
    for k in _W_KEYS["ffn1"]:
        m[k + "_f1"] = src[k]
    return m

def _l1_cols():
    q, k, v = [], [], []
    for h in range(16):
        base = 384 * (h // 2) + 64 * (h % 2)
        q += list(range(base, base + 64))
        k += list(range(base + 128, base + 192))
        v += list(range(base + 256, base + 320))
    return np.array(q), np.array(k), np.array(v)


def attn1a_inputs(ins, core, xn1_rho):
    b, h = core // 2, core % 2
    own0 = h * ROWS
    own = xn1_rho[b, own0:own0 + ROWS, :]
    par = xn1_rho[b, ROWS - own0:2 * ROWS - own0, :]
    xnk = np.ascontiguousarray(np.concatenate([own, par], axis=0).T)
    qc, kc, vc = _l1_cols()
    w = np.asarray(ins["L1_attn_w_qkv"])[P2C, :]
    wqkv = np.ascontiguousarray(np.concatenate(
        [w[:, qc], w[:, kc], w[:, vc]], axis=1))
    pos = np.concatenate([np.arange(own0, own0 + ROWS),
                          np.arange(ROWS - own0, 2 * ROWS - own0)]).astype(np.float64)
    j = np.arange(16, dtype=np.float64)
    inv = (1.0 / 10000.0) ** (2.0 * j / 32.0)
    ang = pos[:, None] * inv[None, :]
    rtab = np.concatenate([np.tile(np.cos(ang), (1, 16)),
                           np.tile(np.sin(ang), (1, 16))], axis=1).astype(np.float32)
    vflag = np.full((1, 1), 1.0 if h == 1 else 0.0, dtype=np.float32)
    return dict(xnk=xnk, wqkv=wqkv, rtab=rtab, tri=_tri_consts(), vflag=vflag)


def attn1b_inputs(ins, core, at):
    wout = np.ascontiguousarray(np.asarray(ins["L1_attn_w_out"])[:, P2C])
    return dict(at=at, wout=wout, tri=_tri_consts())


def ffn1_inputs(ins, core, xn1_sh, x1_sh, a1_sh):
    w = np.asarray(ins["L1_ffn_weight_fp8"])
    wt = np.ascontiguousarray(w[np.ix_(P2C, P2C)].T)
    bias = _colify(np.asarray(ins["L1_ffn_bias"])[P2C], 8)
    return dict(xn=xn1_sh, x1=x1_sh, a1=a1_sh, wt=wt, bias=bias,
                isc=np.asarray(ins["L1_ffn_input_scale"], dtype=np.float32).reshape(1, 1),
                wsc=np.asarray(ins["L1_ffn_weight_scale"], dtype=np.float32).reshape(1, 1))


def _gather_rho(shards):
    out = np.empty((B, T, D), dtype=np.float32)
    for core in range(NCORE):
        b, h = core // 2, core % 2
        out[b, h * ROWS:(h + 1) * ROWS, :] = np.asarray(shards[core]).T
    return out


_NCS = {}


def _get_nc(name):
    if name not in _NCS:
        _NCS[name] = globals()["build_" + name]()
    return _NCS[name]


def _run(name, in_maps):
    nc = _get_nc(name)
    return run_bass_kernel_spmd(nc, in_maps, core_ids=list(range(NCORE))).results


def kernel(**inputs):
    ins = inputs
    x0 = np.asarray(ins["L0_attn_x"], dtype=np.float32)
    res = _run("fused", [fused_inputs(ins, c, x0) for c in range(NCORE)])
    return unshard_rows([r["out"] for r in res], P2C)


def kernel_unfused(**inputs):
    ins = inputs
    x0 = np.asarray(ins["L0_attn_x"], dtype=np.float32)

    res = _run("ln0", [{"x": s} for s in shard_rows(x0, P2C)])
    xn0_rho = _gather_rho([r["xn"] for r in res])
    xn0_orig = xn0_rho[:, :, C2P]

    res = _run("attn0", [attn0_inputs(ins, c, xn0_rho) for c in range(NCORE)])
    a0_sh = [np.asarray(r["a0"]) for r in res]

    x = xn0_orig
    for layer in range(6):
        res = _run("ffn0_attn", [ffn0_attn_inputs(ins, layer, c, x) for c in range(NCORE)])
        xa = _gather_rho([r["xa"] for r in res])
        res = _run("ffn0_ffn", [ffn0_ffn_inputs(ins, layer, c, xa) for c in range(NCORE)])
        x = _gather_rho([r["xo"] for r in res])
    f0_rho = x[:, :, P2C]

    x0_sh = shard_rows(x0, P2C)
    f0_sh = shard_rows(f0_rho)
    res = _run("combine", [dict(x0=x0_sh[c], a0=a0_sh[c], f0=f0_sh[c])
                           for c in range(NCORE)])
    x1_sh = [np.asarray(r["x1"]) for r in res]
    xn1_sh = [np.asarray(r["xn"]) for r in res]
    xn1_rho = _gather_rho(xn1_sh)

    res = _run("attn1a", [attn1a_inputs(ins, c, xn1_rho) for c in range(NCORE)])
    ats = [np.asarray(r["at"]) for r in res]
    res = _run("attn1b", [attn1b_inputs(ins, c, ats[c]) for c in range(NCORE)])
    a1_sh = [np.asarray(r["a1"]) for r in res]

    res = _run("ffn1", [ffn1_inputs(ins, c, xn1_sh[c], x1_sh[c], a1_sh[c])
                        for c in range(NCORE)])
    out = unshard_rows([r["out"] for r in res], P2C)
    return out


# version 10
# speedup vs baseline: 1.0135x; 1.0135x over previous
import contextlib
import math
import numpy as np
import concourse.bass as bass
import concourse.mybir as mybir
from concourse.bass_utils import run_bass_kernel_spmd

F32 = mybir.dt.float32
BF16 = mybir.dt.bfloat16
FP8 = mybir.dt.float8e4
AF = mybir.ActivationFunctionType
ALU = mybir.AluOpType

B, T, D = 4, 4096, 1024
NCORE = 8
ROWS = T // 2


def _p2c():
    p = np.arange(1024)
    hp = p // 512
    j = (p % 512) // 64
    q = p % 64
    return q * 16 + (2 * j + hp)

P2C = _p2c()
C2P = np.argsort(P2C)


def _l1_qk_cols():
    cols = []
    for h in range(16):
        cols += [h * 64 + 2 * j for j in range(16)]
    for h in range(16):
        cols += [h * 64 + 2 * j + 1 for j in range(16)]
    for h in range(16):
        cols += [h * 64 + d for d in range(32, 64)]
    return np.array(cols)

L1_QK_COLS = _l1_qk_cols()


def _inner_qk_cols():
    cols = []
    for h in range(16):
        cols += [h * 64 + d for d in range(32)]
    for h in range(16):
        cols += [h * 64 + d for d in range(32, 64)]
    return np.array(cols)

INNER_QK_COLS = _inner_qk_cols()


def rope_l0attn(pos):
    p = np.arange(512)
    i = 8 * (p % 64) + p // 64
    inv = (1.0 / 10000.0) ** (i.astype(np.float64) / 512.0)
    ang = np.asarray(pos, dtype=np.float64)[None, :] * inv[:, None]
    return np.cos(ang).astype(np.float32), np.sin(ang).astype(np.float32)


def rope_inner(pos):
    p = np.arange(512)
    i = (p % 32).astype(np.float64)
    inv = (1.0 / 10000.0) ** (2.0 * i / 64.0)
    bs = (2.0 * i + 0.4 * 64.0) / (1.4 * 64.0)
    t = np.asarray(pos, dtype=np.float64)
    ang = t[None, :] * inv[:, None]
    power = (t[None, :] - T // 2) / 128.0
    scale = bs[:, None] ** power
    c, s = np.cos(ang), np.sin(ang)
    return ((8.0 * c * scale).astype(np.float32), (8.0 * s * scale).astype(np.float32),
            (c / scale).astype(np.float32), (s / scale).astype(np.float32))


def rope_l1(pos):
    p = np.arange(256)
    j = (p % 16).astype(np.float64)
    inv = (1.0 / 10000.0) ** (2.0 * j / 32.0)
    ang = np.asarray(pos, dtype=np.float64)[None, :] * inv[:, None]
    return np.cos(ang).astype(np.float32), np.sin(ang).astype(np.float32)


class Sems:
    def __init__(self, nc, ctx, names):
        self.h = {}
        self.n = {}
        for nm in names:
            self.h[nm] = ctx.enter_context(nc.semaphore(nm))
            self.n[nm] = 0

    def inc(self, name, instr, k=1):
        instr.then_inc(self.h[name], k)
        self.n[name] += k
        return self.n[name]

    def chain(self, eng, name, instr):
        c = self.inc(name, instr)
        eng.wait_ge(self.h[name], c)


def chunked(total, step):
    out = []
    o = 0
    while o < total:
        out.append((o, min(step, total - o)))
        o += step
    return out



def emit_row_bcast(nc, sems, onesr, jobs, pb2, tag):
    base_ev = sems.n["bc_ev"]
    with nc.Block(tag) as block:
        @block.tensor
        def _(tensor):
            for i, (src, dst) in enumerate(jobs):
                if i >= 2:
                    tensor.wait_ge(sems.h["bc_ev"], base_ev + i - 1)
                L = src.shape[-1]
                sems.inc("bc_mm", tensor.matmul(
                    pb2[:, (i % 2) * 512:(i % 2) * 512 + L], onesr[:, :], src,
                    start=True, stop=True))

        @block.scalar
        def _(scalar):
            base = sems.n["bc_mm"] - len(jobs)
            for i, (src, dst) in enumerate(jobs):
                L = src.shape[-1]
                scalar.wait_ge(sems.h["bc_mm"], base + i + 1)
                sems.inc("bc_ev", scalar.copy(
                    dst, pb2[:, (i % 2) * 512:(i % 2) * 512 + L]))


def emit_ln(nc, sems, xs, n, sq, stat, ones_col, onesr, ps, pb2, Ab, Bb,
            tag, x_off=0, chunk_stride=None, eps_col=None, chunk_ready=None):
    cs = chunk_stride if chunk_stride is not None else n
    offs = x_off if callable(x_off) else (lambda c: c * cs + x_off)
    M = stat[:, 0:n]
    W = stat[:, n:2 * n]

    with nc.Block(tag + "_s") as block:
        mm_base = sems.n["mm"]
        sch_base = sems.n["sch"]
        vch_base = sems.n["vch"]
        jobs = [(c, t0, tl) for c in range(8) for (t0, tl) in chunked(n, 512)]

        @block.scalar
        def _(scalar):
            for j, (c, t0, tl) in enumerate(jobs):
                if chunk_ready is not None and t0 == 0:
                    scalar.wait_ge(*chunk_ready[c])
                if j >= 2:
                    scalar.wait_ge(sems.h["mm"], mm_base + j)
                sems.inc("sch", scalar.activation(
                    sq[:, (j % 2) * 512:(j % 2) * 512 + tl],
                    xs[:, offs(c) + t0: offs(c) + t0 + tl], AF.Square))

        @block.tensor
        def _(tensor):
            last = None
            for c in range(8):
                if chunk_ready is not None:
                    tensor.wait_ge(*chunk_ready[c])
                for t0, tl in chunked(n, 512):
                    last = tensor.matmul(ps[:, t0:t0 + tl], ones_col[:, :],
                                         xs[:, offs(c) + t0: offs(c) + t0 + tl],
                                         start=(c == 0), stop=(c == 7))
            sems.inc("mm", last)
            for j, (c, t0, tl) in enumerate(jobs):
                tensor.wait_ge(sems.h["sch"], sch_base + j + 1)
                if j == 0:
                    tensor.wait_ge(sems.h["vch"], vch_base + 1)
                sems.inc("mm", tensor.matmul(
                    ps[:, t0:t0 + tl], ones_col[:, :],
                    sq[:, (j % 2) * 512:(j % 2) * 512 + tl],
                    start=(c == 0), stop=(c == 7)))

        @block.vector
        def _(vector):
            vector.wait_ge(sems.h["mm"], mm_base + 1)
            sems.inc("vch", vector.tensor_scalar_mul(M, ps[:, 0:n], 1.0 / 1024.0))

    with nc.Block(tag + "_statmath") as block:
        @block.vector
        def _(vector):
            sems.chain(vector, "vch", vector.tensor_scalar_mul(W, ps[:, 0:n], 1.0 / 1024.0))
            sems.chain(vector, "vch", vector.tensor_tensor(Ab[0:1, 0:n], M, M, ALU.mult))
            sems.inc("ev", vector.tensor_tensor(W, W, Ab[0:1, 0:n], ALU.subtract))

        @block.scalar
        def _(scalar):
            scalar.wait_ge(sems.h["ev"], sems.n["ev"])
            sems.chain(scalar, "sch", scalar.activation(W, W, AF.Ln, bias=eps_col[:1, :]))
            sems.inc("sch", scalar.activation(W, W, AF.Exp, scale=-0.5))

        @block.vector
        def _(vector):
            vector.wait_ge(sems.h["sch"], sems.n["sch"])
            sems.chain(vector, "vch", vector.tensor_tensor(M, M, W, ALU.mult))
            vector.tensor_scalar_mul(M, M, -1.0)

    jobs = []
    for t0, tl in chunked(n, 512):
        jobs.append((W[:, t0:t0 + tl], Ab[:, t0:t0 + tl]))
        jobs.append((M[:, t0:t0 + tl], Bb[:, t0:t0 + tl]))
    emit_row_bcast(nc, sems, onesr, jobs, pb2, tag + "_bc")


def build_ln0():
    nc = bass.Bass("TRN2", target_bir_lowering=False)
    x_h = nc.dram_tensor("x", [1024, ROWS], F32, kind="ExternalInput")
    xn_h = nc.dram_tensor("xn", [1024, ROWS], F32, kind="ExternalOutput")
    n = ROWS
    with contextlib.ExitStack() as ctx:
        e = ctx.enter_context
        xs = e(nc.sbuf_tensor("xs", [128, 8 * n], F32))
        sq = e(nc.sbuf_tensor("sq", [128, 1024], F32))
        stat = e(nc.sbuf_tensor("stat", [1, 2 * n], F32))
        ones_col = e(nc.sbuf_tensor("ones_col", [128, 1], F32))
        onesr = e(nc.sbuf_tensor("onesr", [1, 128], F32))
        eps_col = e(nc.sbuf_tensor("eps_col", [1, 1], F32))
        Ab = e(nc.sbuf_tensor("Ab", [128, n], F32))
        Bb = e(nc.sbuf_tensor("Bb", [128, n], F32))
        xn = xs
        ps = e(nc.psum_tensor("ps", [1, n], F32))
        pb2 = e(nc.psum_tensor("pb2", [128, 1024], F32))
        sems = Sems(nc, ctx, ["ld", "ev", "st", "bc_mm", "bc_ev", "vch", "sch", "mm"])

        with nc.Block("ld") as block:
            @block.sync
            def _(sync):
                sems.inc("ld", sync.dma_start(
                    out=xs[:, :],
                    in_=x_h.ap().rearrange("(c p) n -> p c n", p=128)), 16)
                sync.wait_ge(sems.h["ld"], sems.n["ld"])

            @block.vector
            def _(vector):
                vector.memset(ones_col[:, :], 1.0)
                vector.memset(onesr[:, :], 1.0)
                vector.memset(eps_col[:, :], 1e-5)

        emit_ln(nc, sems, xs, n, sq, stat, ones_col, onesr, ps, pb2, Ab, Bb,
                "ln0", eps_col=eps_col)

        with nc.Block("apply") as block:
            @block.vector
            def _(vector):
                for c in range(8):
                    sems.chain(vector, "vch", vector.tensor_tensor(
                        xn[:, c * n:(c + 1) * n], xs[:, c * n:(c + 1) * n],
                        Ab[:, :], ALU.mult))
                    vector.tensor_tensor(xn[:, c * n:(c + 1) * n],
                                         xn[:, c * n:(c + 1) * n], Bb[:, :], ALU.add)

        with nc.Block("store") as block:
            @block.sync
            def _(sync):
                sems.inc("st", sync.dma_start(
                    out=xn_h.ap().rearrange("(c p) n -> p c n", p=128),
                    in_=xn[:, :]), 16)
                sync.wait_ge(sems.h["st"], sems.n["st"])
    return nc



def shard_rows(x_btd, perm=None):
    shards = []
    for core in range(NCORE):
        b, h = core // 2, core % 2
        rows = x_btd[b, h * ROWS:(h + 1) * ROWS, :]
        if perm is not None:
            rows = rows[:, perm]
        shards.append(np.ascontiguousarray(rows.T))
    return shards


def unshard_rows(shards, perm=None):
    out = np.empty((B, T, D), dtype=shards[0].dtype)
    for core in range(NCORE):
        b, h = core // 2, core % 2
        rows = np.asarray(shards[core]).T
        if perm is not None:
            inv = np.argsort(perm)
            rows = rows[:, inv]
        out[b, h * ROWS:(h + 1) * ROWS, :] = rows
    return out


def kernel(**inputs):
    raise NotImplementedError("assembled after stages are unit-tested")

CS = 257 + ROWS
NKV = 18
NIB = 16


def _qoff(c, r):
    return c * CS + (257 if c < 4 else 256) + r


def _kvoff(c, r):
    return c * CS + (1 if c < 4 else 0) + r


def _rope_pos_q():
    return None


def dram_ffn0_attn(nc, sfx="", ext=True):
    k = lambda io: ({"kind": "External" + io} if ext else {})
    return dict(
        xf=nc.dram_tensor("xf" + sfx, [1024, CS], F32, **k("Input")),
        wq=nc.dram_tensor("wq" + sfx, [1024, 1024], F32, **k("Input")),
        wk=nc.dram_tensor("wk" + sfx, [1024, 1024], F32, **k("Input")),
        wv=nc.dram_tensor("wv" + sfx, [1024, 1024], F32, **k("Input")),
        wo=nc.dram_tensor("wo" + sfx, [1024, 1024], F32, **k("Input")),
        qsc=nc.dram_tensor("qsc" + sfx, [128, 4], F32, **k("Input")),
        qsr=nc.dram_tensor("qsr" + sfx, [1, 128], F32, **k("Input")),
        rcq=nc.dram_tensor("rcq" + sfx, [512, ROWS], F32, **k("Input")),
        rsq=nc.dram_tensor("rsq" + sfx, [512, ROWS], F32, **k("Input")),
        rck=nc.dram_tensor("rck" + sfx, [512, 2304], F32, **k("Input")),
        rsk=nc.dram_tensor("rsk" + sfx, [512, 2304], F32, **k("Input")),
        tri=nc.dram_tensor("tri" + sfx, [128, 384], F32, **k("Input")),
        hpo=nc.dram_tensor("hpo" + sfx, [1, 128], F32, **k("Input")),
        ind=nc.dram_tensor("ind" + sfx, [128, 64], F32, **k("Input")),
        indt=nc.dram_tensor("indt" + sfx, [16, 512], F32, **k("Input")),
        xa=nc.dram_tensor("xa" + sfx, [1024, ROWS], F32, **k("Output")),
        hflag=nc.dram_tensor("hflag" + sfx, [1, 1], F32, **k("Input")),
        xl=nc.dram_tensor("xl" + sfx, [1024, 1], F32, **k("Output")),
    )


def emit_ffn0_attn(nc, sems, H):
    xf_h = H["xf"]
    wq_h = H["wq"]
    wk_h = H["wk"]
    wv_h = H["wv"]
    wo_h = H["wo"]
    qsc_h = H["qsc"]
    qsr_h = H["qsr"]
    rcq_h = H["rcq"]
    rsq_h = H["rsq"]
    rck_h = H["rck"]
    rsk_h = H["rsk"]
    tri_h = H["tri"]
    hpo_h = H["hpo"]
    ind_h = H["ind"]
    indt_h = H["indt"]
    xa_h = H["xa"]
    xl_h = H["xl"]
    xsrc_h = H.get("xsrc")
    xh_h = H.get("xh257")
    hflag_h = H["hflag"]

    with contextlib.ExitStack() as ctx:
        e = ctx.enter_context
        xb = e(nc.sbuf_tensor("xb_s", [128, 8 * CS], BF16))
        wsb = e(nc.sbuf_tensor("wsb_s", [128, 64 * 128], BF16))
        qtil = e(nc.sbuf_tensor("qtil_s", [128, 8 * ROWS], BF16))
        ktil = e(nc.sbuf_tensor("ktil_s", [128, 8 * 2304], BF16))
        vx = e(nc.sbuf_tensor("vx_s", [128, NKV * 1040], BF16))
        sqs = e(nc.sbuf_tensor("sqs_s", [128, 512], BF16))
        rstd = e(nc.sbuf_tensor("rstd_s", [16, 512], F32))
        rsthi = e(nc.sbuf_tensor("rsthi_s", [16, 512], BF16))
        rstlo = e(nc.sbuf_tensor("rstlo_s", [16, 512], BF16))
        bsc = e(nc.sbuf_tensor("bsc_s", [128, 1024], F32))
        outb = bsc
        rta = e(nc.sbuf_tensor("rta_s", [128, 2 * 512], F32))
        xff = rta
        rtb = e(nc.sbuf_tensor("rtb_s", [128, 2 * 512], F32))
        colbuf = e(nc.sbuf_tensor("colbuf_s", [128, 8], F32))
        srow = e(nc.sbuf_tensor("srow_s", [1, 160], F32))
        qsc = e(nc.sbuf_tensor("qsc_s", [128, 4], F32))
        tri = e(nc.sbuf_tensor("tri_s", [128, 384], BF16))
        hpo = e(nc.sbuf_tensor("hpo_s", [1, 128], BF16))
        ind = e(nc.sbuf_tensor("ind_s", [128, 64], BF16))
        indt = e(nc.sbuf_tensor("indt_s", [16, 512], BF16))
        onesr = e(nc.sbuf_tensor("onesr_s", [1, 128], F32))
        onesrb = e(nc.sbuf_tensor("onesrb_s", [1, 128], BF16))
        ones_col = e(nc.sbuf_tensor("ones_col_s", [128, 1], F32))
        ptil = e(nc.sbuf_tensor("ptil_s", [128, 4 * 384], BF16))
        attn = xb
        rdc = e(nc.sbuf_tensor("rdc_s", [128, 4], F32))
        attnT = qtil
        PA = e(nc.psum_tensor("PA", [128, 1024], F32))
        PB = e(nc.psum_tensor("PB", [128, 1024], F32))
        PC = e(nc.psum_tensor("PC", [128, 1024], F32))
        PT1 = e(nc.psum_tensor("PT1", [128, 1024], BF16))
        PT2 = e(nc.psum_tensor("PT2", [128, 1024], BF16))

        with nc.Block("a_ld") as block:
            @block.sync
            def _(sync):
                sems.inc("ld", sync.dma_start(out=srow[:, 0:128], in_=qsr_h[:, :]), 16)
                sems.inc("ld", sync.dma_start(out=qsc[:, :], in_=qsc_h[:, :]), 16)
                sems.inc("ld", sync.dma_start(out=srow[:, 140:141], in_=hflag_h[:, :]), 16)

            @block.gpsimd
            def _(g):
                if xsrc_h is None:
                    sems.inc("ldx", g.dma_start(
                        out=xb[:, :],
                        in_=xf_h.ap().rearrange("(c p) n -> p c n", p=128)), 16)
                else:
                    x0v = xb[:, 0:1]
                    sems.inc("ldx", g.dma_start(
                        out=bass.AP(x0v.tensor, x0v.offset,
                                    [list(x0v.ap[0]), [CS, 8], [1, 257]]),
                        in_=xh_h.ap().rearrange("(c p) n -> p c n", p=128)
                        [:, 0:8, :]), 16)
                    sems.inc("ldx", g.dma_start(
                        out=bass.AP(x0v.tensor, x0v.offset + 257,
                                    [list(x0v.ap[0]), [CS, 8], [1, ROWS]]),
                        in_=xsrc_h.ap().rearrange("(c p) n -> p c n", p=128)), 16)
                sems.inc("ldw", g.dma_start(out=tri[:, :], in_=tri_h[:, :]), 16)
                sems.inc("ldw", g.dma_start(out=hpo[:, :], in_=hpo_h[:, :]), 16)
                sems.inc("ldw", g.dma_start(out=ind[:, :], in_=ind_h[:, :]), 16)
                sems.inc("ldw", g.dma_start(out=indt[:, :], in_=indt_h[:, :]), 16)
                sems.inc("ldw", g.dma_start(
                    out=wsb[:, :], in_=wq_h.ap().rearrange("(kc p) m -> p kc m", p=128)), 16)

            @block.vector
            def _(vector):
                vector.memset(onesr[:, :], 1.0)
                vector.memset(onesrb[:, :], 1.0)
                vector.memset(ones_col[:, :], 1.0)
                vector.memset(colbuf[:, 7:8], 1e-24)
                sems.chain(vector, "vch", vector.memset(vx[:, :], 0.0))
                v0 = vx[:, 0:1]
                for rb in range(NKV):
                    vector.memset(bass.AP(v0.tensor, v0.offset + rb * 1040 + 64,
                                          [list(v0.ap[0]), [65, 16], [1, 1]]), 1.0)

        with nc.Block("a_hbc") as block:
            @block.tensor
            def _(tensor):
                tensor.wait_ge(sems.h["ld"], sems.n["ld"])
                tensor.matmul(PC[:, 0:1], onesr[:, :], srow[:, 140:141],
                              start=True, stop=True)

        with nc.Block("a_hbc2") as block:
            @block.scalar
            def _(scalar):
                scalar.copy(colbuf[:, 2:3], PC[:, 0:1])

        with nc.Block("a_hz") as block:
            @block.vector
            def _(vector):
                vector.wait_ge(sems.h["ldx"], sems.n["ldx"])
                x0v = xb[:, 0:1]
                hv = bass.AP(x0v.tensor, x0v.offset, [list(x0v.ap[0]), [CS, 8], [1, 257]])
                vector.tensor_scalar(hv, hv, colbuf[:, 2:3], None, ALU.mult, ALU.bypass)

        def emit_proj_B(tag, out_tile, nrows, offfn):
            rts = chunked(nrows, 512)
            jobs = [(mc, r0, rl) for mc in range(8) for (r0, rl) in rts]
            mm_base = sems.n["mm"]
            ev_base = sems.n["ev"]
            with nc.Block(tag) as block:
                @block.tensor
                def _(tensor):
                    tensor.wait_ge(sems.h["ldw"], sems.n["ldw"])
                    tensor.wait_ge(sems.h["ldx"], sems.n["ldx"])
                    for i, (mc, r0, rl) in enumerate(jobs):
                        half = (i % 2) * 512
                        if i >= 2:
                            tensor.wait_ge(sems.h["ev"], ev_base + i - 1)
                        last = None
                        for kc in range(8):
                            last = tensor.matmul(
                                PA[:, half:half + rl],
                                wsb[:, (kc * 8 + mc) * 128:(kc * 8 + mc) * 128 + 128],
                                xb[:, offfn(kc, r0):offfn(kc, r0) + rl],
                                start=(kc == 0), stop=(kc == 7))
                        sems.inc("mm", last)

                @block.scalar
                def _(scalar):
                    for i, (mc, r0, rl) in enumerate(jobs):
                        half = (i % 2) * 512
                        scalar.wait_ge(sems.h["mm"], mm_base + i + 1)
                        sems.inc("ev", scalar.copy(
                            out_tile[:, mc * nrows + r0: mc * nrows + r0 + rl],
                            PA[:, half:half + rl]))

        def emit_proj_A(tag, nrows, offfn):
            rbs = nrows // 128
            jobs = [(rb, n0) for rb in range(rbs) for n0 in (0, 512)]
            mm_base = sems.n["mm"]
            ev_base = sems.n["ev"]
            with nc.Block(tag) as block:
                @block.tensor
                def _(tensor):
                    tensor.wait_ge(sems.h["ldw"], sems.n["ldw"])
                    for i, (rb, n0) in enumerate(jobs):
                        half = (i % 2) * 512
                        if i >= 2:
                            tensor.wait_ge(sems.h["ev"], ev_base + i - 1)
                        last = None
                        for kc in range(8):
                            last = tensor.matmul(
                                PA[:, half:half + 512],
                                xb[:, offfn(kc, rb * 128):offfn(kc, rb * 128) + 128],
                                wsb[:, kc * 1024 + n0: kc * 1024 + n0 + 512],
                                start=(kc == 0), stop=(kc == 7))
                        sems.inc("mm", last)

                @block.scalar
                def _(scalar):
                    for i, (rb, n0) in enumerate(jobs):
                        half = (i % 2) * 512
                        scalar.wait_ge(sems.h["mm"], mm_base + i + 1)
                        v0 = vx[:, 0:1]
                        dst = bass.AP(v0.tensor, v0.offset + rb * 1040 + (n0 // 64) * 65,
                                      [list(v0.ap[0]), [65, 8], [1, 64]])
                        sems.inc("ev", scalar.copy(dst, PA[:, half:half + 512]))

        emit_proj_B("a_projq", qtil, ROWS, _qoff)
        with nc.Block("a_ldk") as block:
            @block.gpsimd
            def _(g):
                sems.inc("ldw", g.dma_start(
                    out=wsb[:, :], in_=wk_h.ap().rearrange("(kc p) m -> p kc m", p=128)), 16)
        emit_proj_B("a_projk", ktil, 2304, _kvoff)
        with nc.Block("a_ldv") as block:
            @block.gpsimd
            def _(g):
                sems.inc("ldw", g.dma_start(
                    out=wsb[:, :], in_=wv_h.ap().rearrange("(kc p) m -> p kc m", p=128)), 16)
        emit_proj_A("a_projv", 2304, _kvoff)
        with nc.Block("a_ldo") as block:
            @block.gpsimd
            def _(g):
                sems.inc("ldw", g.dma_start(
                    out=wsb[:, :], in_=wo_h.ap().rearrange("(kc p) m -> p kc m", p=128)), 16)

        def emit_l2norm(tag, tile, nrows, sc_x1, sc_x2):
            for ri, (r0, rl) in enumerate(chunked(nrows, 512)):
                with nc.Block(tag + "_n%d" % ri) as block:
                    vch_base = sems.n["vch"]
                    mm_base = sems.n["mm"]
                    sch_base = sems.n["sch"]
                    ev_base = sems.n["ev"]

                    @block.vector
                    def _(vector):
                        vector.wait_ge(sems.h["ld"], sems.n["ld"])
                        for c in range(8):
                            if c:
                                vector.wait_ge(sems.h["mm"], mm_base + c)
                            sems.inc("vch", vector.tensor_tensor(
                                sqs[:, 0:rl], tile[:, c * nrows + r0:c * nrows + r0 + rl],
                                tile[:, c * nrows + r0:c * nrows + r0 + rl], ALU.mult))
                        vector.wait_ge(sems.h["sch"], sch_base + 3)
                        sems.inc("vch", vector.tensor_tensor(
                            rstlo[:, 0:rl], rstd[:, 0:rl], rsthi[:, 0:rl], ALU.subtract))
                        for c in range(8):
                            sc = sc_x2 if c >= 4 else sc_x1
                            half = (c % 2) * 512
                            vector.wait_ge(sems.h["mm"], mm_base + 8 + c + 1)
                            sems.chain(vector, "vch", vector.tensor_scalar(
                                bsc[:, half:half + rl], PA[:, half:half + rl], sc,
                                None, ALU.mult, ALU.bypass))
                            sems.inc("ev", vector.tensor_tensor(
                                tile[:, c * nrows + r0:c * nrows + r0 + rl],
                                tile[:, c * nrows + r0:c * nrows + r0 + rl],
                                bsc[:, half:half + rl], ALU.mult))

                    @block.tensor
                    def _(tensor):
                        for c in range(8):
                            tensor.wait_ge(sems.h["vch"], vch_base + c + 1)
                            sems.inc("mm", tensor.matmul(
                                PB[0:16, 0:rl], ind[:, (c % 4) * 16:(c % 4) * 16 + 16],
                                sqs[:, 0:rl], start=(c == 0), stop=(c == 7)))
                        for c in range(8):
                            tensor.wait_ge(sems.h["vch"], vch_base + 9)
                            half = (c % 2) * 512
                            if c >= 2:
                                tensor.wait_ge(sems.h["ev"], ev_base + c - 1)
                            tensor.matmul(
                                PA[:, half:half + rl],
                                indt[:, (c % 4) * 128:(c % 4) * 128 + 128],
                                rsthi[:, 0:rl], start=True, stop=False)
                            sems.inc("mm", tensor.matmul(
                                PA[:, half:half + rl],
                                indt[:, (c % 4) * 128:(c % 4) * 128 + 128],
                                rstlo[:, 0:rl], start=False, stop=True))

                    @block.scalar
                    def _(scalar):
                        scalar.wait_ge(sems.h["mm"], mm_base + 8)
                        sems.chain(scalar, "sch", scalar.activation(
                            rstd[:, 0:rl], PB[0:16, 0:rl], AF.Ln,
                            bias=colbuf[0:16, 7:8]))
                        sems.chain(scalar, "sch", scalar.activation(
                            rstd[:, 0:rl], rstd[:, 0:rl], AF.Exp, scale=-0.5))
                        sems.inc("sch", scalar.copy(rsthi[:, 0:rl], rstd[:, 0:rl]))

        emit_l2norm("a_nq", qtil, ROWS, qsc[:, 0:1], qsc[:, 1:2])
        emit_l2norm("a_nk", ktil, 2304, qsc[:, 2:3], qsc[:, 3:4])

        with nc.Block("a_m") as block:
            @block.vector
            def _(vector):
                vector.wait_ge(sems.h["ld"], sems.n["ld"])
                sems.chain(vector, "vch", vector.reduce_max(
                    srow[:, 128:129], srow[:, 0:64], axis=mybir.AxisListType.X,
                    apply_absolute_value=True))
                sems.chain(vector, "vch", vector.reduce_max(
                    srow[:, 129:130], srow[:, 64:128], axis=mybir.AxisListType.X,
                    apply_absolute_value=True))
                sems.chain(vector, "vch", vector.tensor_tensor(
                    srow[:, 130:131], srow[:, 128:129], srow[:, 129:130], ALU.mult))
                sems.chain(vector, "vch", vector.tensor_scalar(
                    srow[:, 130:131], srow[:, 130:131], -8.0, None, ALU.mult, ALU.bypass))

        with nc.Block("a_mbc") as block:
            @block.tensor
            def _(tensor):
                sems.inc("mm", tensor.matmul(PB[:, 0:1], onesr[:, :],
                                             srow[:, 130:131], start=True, stop=True))

        with nc.Block("a_mbc2") as block:
            @block.scalar
            def _(scalar):
                scalar.copy(colbuf[:, 0:1], PB[:, 0:1])

        rope_jobs = []
        for c in range(4):
            for (r0, rl) in chunked(ROWS, 512):
                rope_jobs.append((qtil, ROWS, rcq_h, rsq_h, c, r0, rl))
            for (r0, rl) in chunked(2304, 512):
                rope_jobs.append((ktil, 2304, rck_h, rsk_h, c, r0, rl))
        RPC = len(rope_jobs) // 4
        ldr_base = sems.n["ldr"]
        ldr2_base = sems.n["ldr2"]
        rch_base = sems.n["rch"]

        def rope_stream(sync, j0, j1):
            for i in range(j0, j1):
                tile, nrows, ch, sh, c, r0, rl = rope_jobs[i]
                half = (i % 2) * 512
                sm = "ldr" if i % 2 == 0 else "ldr2"
                if i >= 2:
                    sync.wait_ge(sems.h["rch"], rch_base + 6 * (i - 1))
                sems.inc(sm, sync.dma_start(
                    out=rta[:, half:half + rl],
                    in_=ch[c * 128:(c + 1) * 128, r0:r0 + rl]), 16)
                sems.inc(sm, sync.dma_start(
                    out=rtb[:, half:half + rl],
                    in_=sh[c * 128:(c + 1) * 128, r0:r0 + rl]), 16)

        def rope_exec(vector, j0, j1):
            for i in range(j0, j1):
                tile, nrows, ch, sh, c, r0, rl = rope_jobs[i]
                half = (i % 2) * 512
                sm = "ldr" if i % 2 == 0 else "ldr2"
                sb = ldr_base if i % 2 == 0 else ldr2_base
                vector.wait_ge(sems.h[sm], sb + 32 * (i // 2 + 1))
                x1 = tile[:, c * nrows + r0: c * nrows + r0 + rl]
                x2 = tile[:, (4 + c) * nrows + r0: (4 + c) * nrows + r0 + rl]
                co = rta[:, half:half + rl]
                si = rtb[:, half:half + rl]
                t1 = bsc[:, 0:rl]
                t2 = bsc[:, 512:512 + rl]
                sems.chain(vector, "rch", vector.tensor_tensor(t1, x1, si, ALU.mult))
                sems.chain(vector, "rch", vector.tensor_tensor(t2, x2, si, ALU.mult))
                sems.chain(vector, "rch", vector.tensor_tensor(x1, x1, co, ALU.mult))
                sems.chain(vector, "rch", vector.tensor_tensor(x2, x2, co, ALU.mult))
                sems.chain(vector, "rch", vector.tensor_tensor(x1, x1, t2, ALU.subtract))
                sems.chain(vector, "rch", vector.tensor_tensor(x2, x2, t1, ALU.add))

        with nc.Block("a_rp0") as block:
            @block.sync
            def _(sync):
                rope_stream(sync, 0, RPC)

            @block.vector
            def _(vector):
                rope_exec(vector, 0, RPC)

        SP = [PA[:, 0:512], PA[:, 512:1024], PB[:, 0:512], PB[:, 512:1024]]
        for c in range(4):
            mm_base = sems.n["mm"]
            ex_base = sems.n["ex"]
            pv_base = sems.n["pv"]
            ev_base = sems.n["ev"]
            with nc.Block("a_att%d" % c) as block:
                if c < 3:
                    @block.sync
                    def _(sync):
                        rope_stream(sync, (c + 1) * RPC, (c + 2) * RPC)

                @block.tensor
                def _(tensor):
                    tensor.wait_ge(sems.h["rch"], rch_base + 6 * (c + 1) * RPC)
                    for ib in range(NIB + 1):
                        if ib > 0:
                            tensor.wait_ge(sems.h["ex"], ex_base + ib)
                        if ib < NIB:
                            last = None
                            poisons = [0, 1] if ib == 0 else ([0] if ib == 1 else [])
                            for hh in range(4):
                                b32 = 32 * hh
                                tp = (b32, 0)
                                for sub in range(3):
                                    bl = ib + sub
                                    tensor.matmul(
                                        SP[hh][:, sub * 128:sub * 128 + 128],
                                        ktil[b32:b32 + 32, c * 2304 + bl * 128: c * 2304 + bl * 128 + 128],
                                        qtil[b32:b32 + 32, c * ROWS + ib * 128: c * ROWS + ib * 128 + 128],
                                        start=(sub == 0), stop=False, tile_position=tp)
                                    tensor.matmul(
                                        SP[hh][:, sub * 128:sub * 128 + 128],
                                        ktil[b32:b32 + 32, (4 + c) * 2304 + bl * 128: (4 + c) * 2304 + bl * 128 + 128],
                                        qtil[b32:b32 + 32, (4 + c) * ROWS + ib * 128: (4 + c) * ROWS + ib * 128 + 128],
                                        start=False, stop=False, tile_position=tp)
                                tensor.matmul(
                                    SP[hh][:, 0:128], tri[:, 128:256], tri[:, 0:128],
                                    start=False, stop=False)
                                for sub in poisons:
                                    tensor.matmul(
                                        SP[hh][:, sub * 128:sub * 128 + 128],
                                        hpo[:, :], onesrb[:, :],
                                        start=False, stop=False)
                                last = tensor.matmul(
                                    SP[hh][:, 256:384], tri[:, 256:384], tri[:, 0:128],
                                    start=False, stop=True)
                            sems.inc("mm", last)
                        if ib > 0:
                            jb = ib - 1
                            if jb >= 2:
                                tensor.wait_ge(sems.h["ev"], ev_base + jb - 1)
                            half = (jb % 2) * 512
                            last = None
                            for hh in range(4):
                                for sub in range(3):
                                    last = tensor.matmul(
                                        PC[:, half + hh * 128: half + hh * 128 + 65],
                                        ptil[:, hh * 384 + sub * 128: hh * 384 + sub * 128 + 128],
                                        vx[:, (jb + sub) * 1040 + (4 * c + hh) * 65:
                                           (jb + sub) * 1040 + (4 * c + hh) * 65 + 65],
                                        start=(sub == 0), stop=(sub == 2))
                            sems.inc("pv", last)

                @block.scalar
                def _(scalar):
                    for ib in range(NIB):
                        scalar.wait_ge(sems.h["mm"], mm_base + ib + 1)
                        if ib > 0:
                            scalar.wait_ge(sems.h["pv"], pv_base + ib)
                        last = None
                        for hh in range(4):
                            last = scalar.activation(
                                ptil[:, hh * 384: hh * 384 + 384], SP[hh][:, 0:384],
                                AF.Exp, bias=colbuf[:, 0:1])
                        sems.inc("ex", last)

                @block.vector
                def _(vector):
                    if c < 3:
                        rope_exec(vector, (c + 1) * RPC, (c + 2) * RPC)
                    for jb in range(NIB):
                        vector.wait_ge(sems.h["pv"], pv_base + jb + 1)
                        half = (jb % 2) * 512
                        for hh in range(4):
                            sems.chain(vector, "vch", vector.reciprocal(
                                rdc[:, hh:hh + 1],
                                PC[:, half + hh * 128 + 64: half + hh * 128 + 65]))
                            sems.chain(vector, "vch", vector.tensor_scalar(
                                attn[:, jb * 1024 + (4 * c + hh) * 64:
                                     jb * 1024 + (4 * c + hh) * 64 + 64],
                                PC[:, half + hh * 128: half + hh * 128 + 64],
                                rdc[:, hh:hh + 1], None, ALU.mult, ALU.bypass))
                        sems.inc("ev", vector.engine_nop())

        with nc.Block("a_tr") as block:
            mm_base = sems.n["mm"]
            ev_base = sems.n["ev"]
            jobs = [(rb, cc) for rb in range(NIB) for cc in range(8)]

            @block.tensor
            def _(tensor):
                for i, (rb, cc) in enumerate(jobs):
                    pt = PT1 if i % 2 == 0 else PT2
                    if i >= 2:
                        tensor.wait_ge(sems.h["ev"], ev_base + i - 1)
                    sems.inc("mm", tensor.transpose(
                        pt[:, 0:128],
                        attn[:, rb * 1024 + cc * 128: rb * 1024 + cc * 128 + 128],
                        tri[:, 0:128]))

            @block.scalar
            def _(scalar):
                for i, (rb, cc) in enumerate(jobs):
                    pt = PT1 if i % 2 == 0 else PT2
                    scalar.wait_ge(sems.h["mm"], mm_base + i + 1)
                    sems.inc("ev", scalar.copy(
                        attnT[:, cc * ROWS + rb * 128: cc * ROWS + rb * 128 + 128],
                        pt[:, 0:128]))

        with nc.Block("a_wo") as block:
            mm_base = sems.n["mm"]
            ev_base = sems.n["ev"]
            lw0 = {"lwa": sems.n["lwa"], "lwb": sems.n["lwb"]}
            st0 = {"st": sems.n["st"], "st2": sems.n["st2"]}
            sx0 = sems.n["sx"]
            jobs = [(mc, r0) for mc in range(8) for r0 in range(0, ROWS, 512)]

            @block.sync
            def _(sync):
                for i, (mc, r0) in enumerate(jobs):
                    half = (i % 2) * 512
                    sm = "lwa" if i % 2 == 0 else "lwb"
                    if i >= 2:
                        sync.wait_ge(sems.h["ev"], ev_base + i - 1)
                    src = (xf_h[mc * 128:(mc + 1) * 128, 257 + r0: 257 + r0 + 512]
                           if xsrc_h is None else
                           xsrc_h[mc * 128:(mc + 1) * 128, r0:r0 + 512])
                    sems.inc(sm, sync.dma_start(
                        out=xff[:, half:half + 512], in_=src), 16)

            @block.tensor
            def _(tensor):
                tensor.wait_ge(sems.h["ldw"], sems.n["ldw"])
                for i, (mc, r0) in enumerate(jobs):
                    half = (i % 2) * 512
                    if i >= 2:
                        tensor.wait_ge(sems.h["ev"], ev_base + i - 1)
                    last = None
                    for kc in range(8):
                        last = tensor.matmul(
                            PA[:, half:half + 512],
                            wsb[:, (kc * 8 + mc) * 128:(kc * 8 + mc) * 128 + 128],
                            attnT[:, kc * ROWS + r0: kc * ROWS + r0 + 512],
                            start=(kc == 0), stop=(kc == 7))
                    sems.inc("mm", last)

            @block.vector
            def _(vector):
                for i, (mc, r0) in enumerate(jobs):
                    half = (i % 2) * 512
                    sm = "lwa" if i % 2 == 0 else "lwb"
                    vector.wait_ge(sems.h["mm"], mm_base + i + 1)
                    vector.wait_ge(sems.h[sm], lw0[sm] + 16 * (i // 2 + 1))
                    if i >= 2:
                        sm2 = "st" if i % 2 == 0 else "st2"
                        vector.wait_ge(sems.h[sm2], st0[sm2] + 16 * ((i - 2) // 2 + 1))
                        if (i - 2) % 4 == 3:
                            vector.wait_ge(sems.h["sx"], sx0 + 16 * ((i - 1) // 4))
                    sems.inc("ev", vector.scalar_tensor_tensor(
                        outb[:, half:half + 512], PA[:, half:half + 512], 1.0,
                        xff[:, half:half + 512], ALU.mult, ALU.add))

            @block.scalar
            def _(scalar):
                for i, (mc, r0) in enumerate(jobs):
                    half = (i % 2) * 512
                    sm2 = "st" if i % 2 == 0 else "st2"
                    scalar.wait_ge(sems.h["ev"], ev_base + i + 1)
                    sems.inc(sm2, scalar.dma_start(
                        out=xa_h[mc * 128:(mc + 1) * 128, r0:r0 + 512],
                        in_=outb[:, half:half + 512]), 16)
                    if r0 == 1536:
                        sems.inc("sx", scalar.dma_start(
                            out=xl_h[mc * 128:(mc + 1) * 128, :],
                            in_=outb[:, half + 511:half + 512]), 16)
                scalar.wait_ge(sems.h["st"], sems.n["st"])
                scalar.wait_ge(sems.h["st2"], sems.n["st2"])
                scalar.wait_ge(sems.h["sx"], sems.n["sx"])



def build_ffn0_attn():
    nc = bass.Bass("TRN2", target_bir_lowering=False)
    H = dram_ffn0_attn(nc)
    with contextlib.ExitStack() as c0:
        sems = Sems(nc, c0, ["ld", "ldw", "ldr", "ldr2", "ldx", "lwa", "lwb", "ev", "st", "st2", "vch", "sch", "mm", "ex", "pv", "sx", "rch"])
        emit_ffn0_attn(nc, sems, H)
    return nc


CS2 = 1 + ROWS
NH1 = 22


def _foff(c, r):
    return c * CS2 + (1 if c < 4 else 0) + r


def dram_ffn0_ffn(nc, sfx="", ext=True):
    k = lambda io: ({"kind": "External" + io} if ext else {})
    return dict(
        xa=nc.dram_tensor("xa" + sfx, [1024, CS2], F32, **k("Input")),
        w1a=nc.dram_tensor("w1a" + sfx, [1024, 2816], F32, **k("Input")),
        w1g=nc.dram_tensor("w1g" + sfx, [1024, 2816], F32, **k("Input")),
        w2=nc.dram_tensor("w2" + sfx, [2816, 1024], F32, **k("Input")),
        lng=nc.dram_tensor("lng" + sfx, [128, 8], F32, **k("Input")),
        lnb=nc.dram_tensor("lnb" + sfx, [128, 8], F32, **k("Input")),
        b1a=nc.dram_tensor("b1a" + sfx, [128, 22], F32, **k("Input")),
        b1g=nc.dram_tensor("b1g" + sfx, [128, 22], F32, **k("Input")),
        b2=nc.dram_tensor("b2" + sfx, [128, 8], F32, **k("Input")),
        xo=nc.dram_tensor("xo" + sfx, [1024, ROWS], F32, **k("Output")),
        hflag=nc.dram_tensor("hflag" + sfx, [1, 1], F32, **k("Input")),
        xt=nc.dram_tensor("xt" + sfx, [1024, 257], F32, **k("Output")),
    )


def emit_ffn0_ffn(nc, sems, H):
    xa_h = H["xa"]
    xt_h = H["xt"]
    xh_h = H.get("xh")
    w1a_h = H["w1a"]
    w1g_h = H["w1g"]
    w2_h = H["w2"]
    lng_h = H["lng"]
    lnb_h = H["lnb"]
    b1a_h = H["b1a"]
    b1g_h = H["b1g"]
    b2_h = H["b2"]
    xo_h = H["xo"]
    hflag_h = H["hflag"]

    n = ROWS
    with contextlib.ExitStack() as ctx:
        e = ctx.enter_context
        xs = e(nc.sbuf_tensor("xs_s", [128, 8 * CS2], BF16))
        xnb = e(nc.sbuf_tensor("xnb_s", [128, 8 * n], BF16))
        wsb = e(nc.sbuf_tensor("wsb_s", [128, NH1 * 8 * 128], BF16))
        hb = e(nc.sbuf_tensor("hb_s", [128, NH1 * 1024], BF16))
        gb = e(nc.sbuf_tensor("gb_s", [128, 1024], BF16))
        sq = e(nc.sbuf_tensor("sq_s", [128, 1024], BF16))
        stat = e(nc.sbuf_tensor("stat_s", [1, 2 * n], F32))
        fsc = e(nc.sbuf_tensor("fsc_s", [128, 512], F32))
        eps_col = e(nc.sbuf_tensor("eps_s", [1, 1], F32))
        ones_colb = e(nc.sbuf_tensor("onescb_s", [128, 1], BF16))
        Ab = e(nc.sbuf_tensor("Ab_s", [128, n], F32))
        Bb = e(nc.sbuf_tensor("Bb_s", [128, n], F32))
        cols = e(nc.sbuf_tensor("cols_s", [128, 80], F32))
        xff = e(nc.sbuf_tensor("xff_s", [128, 1024], F32))
        outb = e(nc.sbuf_tensor("outb_s", [128, 1024], F32))
        ones_col = e(nc.sbuf_tensor("ones_col_s", [128, 1], F32))
        onesr = e(nc.sbuf_tensor("onesr_s", [1, 128], F32))
        PS = e(nc.psum_tensor("PS", [1, n], F32))
        PB2 = e(nc.psum_tensor("PB2", [128, 1024], F32))
        PA = e(nc.psum_tensor("PA", [128, 1024], F32))
        LNG = cols[:, 0:8]
        LNB = cols[:, 8:16]
        B1A = cols[:, 16:38]
        B1G = cols[:, 38:60]

        ld_a0B = [None]
        ld_a0A_l = [None]
        ldg_c = [None] * 8
        ldg_h = [None]
        with nc.Block("f_ld") as block:
            @block.sync
            def _(sync):
                sems.inc("ld", sync.dma_start(out=cols[:, 0:8], in_=lng_h[:, :]), 16)
                sems.inc("ld", sync.dma_start(out=cols[:, 8:16], in_=lnb_h[:, :]), 16)
                sems.inc("ld", sync.dma_start(out=cols[:, 16:38], in_=b1a_h[:, :]), 16)
                sems.inc("ld", sync.dma_start(out=cols[:, 38:60], in_=b1g_h[:, :]), 16)
                sems.inc("ld", sync.dma_start(out=cols[0:1, 70:71], in_=hflag_h[:, :]), 16)

            @block.gpsimd
            def _(g):
                if xh_h is None:
                    for c in range(8):
                        sm = "lc%d" % c
                        sems.inc(sm, g.dma_start(
                            out=xs[:, c * CS2:(c + 1) * CS2],
                            in_=xa_h[c * 128:(c + 1) * 128, :]), 16)
                        ldg_c[c] = (sems.h[sm], sems.n[sm])
                    ldg_h[0] = sems.n["ldg"]
                else:
                    x0v = xs[:, 0:1]
                    sems.inc("ldg", g.dma_start(
                        out=bass.AP(x0v.tensor, x0v.offset,
                                    [list(x0v.ap[0]), [CS2, 8], [1, 1]]),
                        in_=xh_h.ap().rearrange("(c p) n -> p c n", p=128)
                        [:, 0:8, :], allow_slow_non_contiguous=True), 16)
                    ldg_h[0] = sems.n["ldg"]
                    for c in range(8):
                        sm = "lc%d" % c
                        sems.inc(sm, g.dma_start(
                            out=xs[:, c * CS2 + 1:(c + 1) * CS2],
                            in_=xa_h[c * 128:(c + 1) * 128, :]), 16)
                        ldg_c[c] = (sems.h[sm], sems.n[sm])
                w1av0 = w1a_h.ap().rearrange("(kc p) m -> p kc m", p=128)
                sems.inc("ldw", g.dma_start(
                    out=wsb[:, 0:11 * 8 * 128], in_=w1av0[:, :, 0:11 * 128]), 16)
                ld_a0A = sems.n["ldw"]
                sems.inc("ldw", g.dma_start(
                    out=wsb[:, 11 * 8 * 128:22 * 8 * 128],
                    in_=w1av0[:, :, 11 * 128:22 * 128]), 16)
                ld_a0B[0] = sems.n["ldw"]
                ld_a0A_l[0] = ld_a0A

            @block.vector
            def _(vector):
                vector.memset(ones_col[:, :], 1.0)
                vector.memset(ones_colb[:, :], 1.0)
                vector.memset(onesr[:, :], 1.0)
                vector.memset(eps_col[:, :], 1e-5)

        with nc.Block("f_ld2") as block:
            @block.sync
            def _(sync):
                sems.inc("ld", sync.dma_start(out=cols[:, 60:68], in_=b2_h[:, :]), 16)

        with nc.Block("f_hbc") as block:
            @block.tensor
            def _(tensor):
                tensor.wait_ge(sems.h["ld"], sems.n["ld"])
                tensor.matmul(PB2[:, 0:1], onesr[:, :], cols[0:1, 70:71],
                              start=True, stop=True)

        with nc.Block("f_hbc2") as block:
            @block.scalar
            def _(scalar):
                scalar.copy(cols[:, 71:72], PB2[:, 0:1])

        with nc.Block("f_hz") as block:
            @block.vector
            def _(vector):
                if xh_h is not None:
                    vector.wait_ge(sems.h["ldg"], ldg_h[0])
                else:
                    vector.wait_ge(*ldg_c[7])
                x0v = xs[:, 0:1]
                hv = bass.AP(x0v.tensor, x0v.offset, [list(x0v.ap[0]), [CS2, 8], [1, 1]])
                vector.tensor_scalar(hv, hv, cols[:, 71:72], None, ALU.mult, ALU.bypass)

        with nc.Block("f_wait") as block:
            @block.tensor
            def _(tensor):
                tensor.wait_ge(sems.h["ld"], sems.n["ld"])
        emit_ln(nc, sems, xs, n, sq, stat, ones_colb, onesr, PS, PB2, Ab, Bb,
                "f_ln", x_off=_foff0, chunk_stride=None, eps_col=eps_col,
                chunk_ready=ldg_c)

        HJ = 11
        HW = HJ * 8 * 128
        R1 = wsb[:, 0:HW]
        R2 = wsb[:, HW:2 * HW]
        R3 = xs[:, 0:HW]
        w1av = w1a_h.ap().rearrange("(kc p) m -> p kc m", p=128)
        w1gv = w1g_h.ap().rearrange("(kc p) m -> p kc m", p=128)
        w2v = w2_h.ap().rearrange("(kc p) m -> p kc m", p=128)
        w1aA, w1aB = w1av[:, :, 0:HJ * 128], w1av[:, :, HJ * 128:NH1 * 128]
        w1gA, w1gB = w1gv[:, :, 0:HJ * 128], w1gv[:, :, HJ * 128:NH1 * 128]
        w2A, w2B = w2v[:, 0:HJ, :], w2v[:, HJ:NH1, :]

        def wload(g, region, src):
            sems.inc("ldw", g.dma_start(out=region, in_=src), 16)
            return sems.n["ldw"]

        regs = {"a": (R1, R2), "g": (R3, R1), "w": (R2, R3)}
        lc = {("a", 0): [ld_a0A_l[0], ld_a0B[0]]}

        app_cnt = {}

        def emit_app(vector, tiles):
            vector.wait_ge(sems.h["ld"], sems.n["ld"])
            for t0 in tiles:
                for c in range(8):
                    sems.chain(vector, "vch", vector.tensor_tensor(
                        fsc[:, 0:512], xs[:, _foff(c, t0):_foff(c, t0) + 512],
                        Ab[:, t0:t0 + 512], ALU.mult))
                    sems.chain(vector, "vch", vector.tensor_tensor(
                        fsc[:, 0:512], fsc[:, 0:512], Bb[:, t0:t0 + 512],
                        ALU.add))
                    sems.chain(vector, "vch", vector.tensor_scalar(
                        xnb[:, c * n + t0:c * n + t0 + 512], fsc[:, 0:512],
                        cols[:, c:c + 1], cols[:, 8 + c:9 + c],
                        ALU.mult, ALU.add))

        def emit_w1_pass(rg, kind, prefetch, app_tiles=None):
            R0 = rg * 1024
            regA, regB = regs[kind]
            jobs = ([(mc, n0) for mc in range(HJ) for n0 in (0, 512)] +
                    [(mc, n0) for mc in range(HJ, NH1) for n0 in (0, 512)])
            mm_base = sems.n["mm"]
            ex_base = sems.n["ex"]
            ev_base = sems.n["ev"]
            cA, cB = lc[(kind, rg)]
            out_counts = []
            vch0 = sems.n["vch"]
            app_end = None
            if app_tiles:
                for i, t0 in enumerate(app_tiles):
                    app_cnt[t0 // 512] = vch0 + 24 * (i + 1)
                app_end = vch0 + 24 * len(app_tiles)
            with nc.Block("f_%s%d" % (kind, rg)) as block:
                @block.gpsimd
                def _(g):
                    for region, src, mode in prefetch:
                        if mode == "app":
                            g.wait_ge(sems.h["vch"], app_end)
                        elif mode is True:
                            g.wait_ge(sems.h["mm"], mm_base + 2 * HJ)
                        out_counts.append(wload(g, region, src))

                @block.tensor
                def _(tensor):
                    if cA is not None:
                        tensor.wait_ge(sems.h["ldw"], cA)
                    cBr = (out_counts[cB[1]] if isinstance(cB, tuple)
                           else cB)
                    gate = "ex" if kind == "g" else "ev"
                    gbase = ex_base if kind == "g" else ev_base
                    for i, (mc, n0) in enumerate(jobs):
                        ti = (R0 + n0) // 512
                        if kind == "a" and ti in app_cnt:
                            tensor.wait_ge(sems.h["vch"], app_cnt[ti])
                        if i == 2 * HJ and cBr is not None:
                            tensor.wait_ge(sems.h["ldw"], cBr)
                        half = (i % 2) * 512
                        if i >= 2:
                            tensor.wait_ge(sems.h[gate], gbase + i - 1)
                        reg = regA if i < 2 * HJ else regB
                        mcl = mc if mc < HJ else mc - HJ
                        last = None
                        for kc in range(8):
                            last = tensor.matmul(
                                PA[:, half:half + 512],
                                reg[:, (kc * HJ + mcl) * 128:(kc * HJ + mcl) * 128 + 128],
                                xnb[:, kc * n + R0 + n0: kc * n + R0 + n0 + 512],
                                start=(kc == 0), stop=(kc == 7))
                        sems.inc("mm", last)

                if kind == "a":
                    @block.vector
                    def _(vector):
                        if app_tiles:
                            emit_app(vector, app_tiles)
                        for i, (mc, n0) in enumerate(jobs):
                            half = (i % 2) * 512
                            vector.wait_ge(sems.h["mm"], mm_base + i + 1)
                            sems.inc("ev", vector.tensor_scalar(
                                hb[:, mc * 1024 + n0: mc * 1024 + n0 + 512],
                                PA[:, half:half + 512],
                                cols[:, 16 + mc:17 + mc], None, ALU.add, ALU.bypass))
                else:
                    @block.scalar
                    def _(scalar):
                        for i, (mc, n0) in enumerate(jobs):
                            half = (i % 2) * 512
                            scalar.wait_ge(sems.h["mm"], mm_base + i + 1)
                            if i >= 2:
                                scalar.wait_ge(sems.h["ev"], ev_base + i - 1)
                            sems.inc("ex", scalar.activation(
                                gb[:, half:half + 512], PA[:, half:half + 512],
                                AF.Gelu, bias=cols[:, 38 + mc:39 + mc]))

                    @block.vector
                    def _(vector):
                        if app_tiles:
                            emit_app(vector, app_tiles)
                        for i, (mc, n0) in enumerate(jobs):
                            half = (i % 2) * 512
                            vector.wait_ge(sems.h["ex"], ex_base + i + 1)
                            sems.inc("ev", vector.tensor_tensor(
                                hb[:, mc * 1024 + n0: mc * 1024 + n0 + 512],
                                hb[:, mc * 1024 + n0: mc * 1024 + n0 + 512],
                                gb[:, half:half + 512], ALU.mult))
            return out_counts

        def emit_w2_pass(rg, prefetch):
            R0 = rg * 1024
            regA, regB = regs["w"]
            jobs2 = [(mc, n0) for mc in range(8) for n0 in (0, 512)]
            mm_base = sems.n["mm"]
            ev_base = sems.n["ev"]
            lwa_base = sems.n["lwa"]
            lwb_base = sems.n["lwb"]
            st_base = sems.n["st"]
            st2_base = sems.n["st2"]
            sx_base = sems.n["sx"]
            cA, cB = lc[("w", rg)]
            out_counts = []
            with nc.Block("f_w2%d" % rg) as block:
                @block.gpsimd
                def _(g):
                    for region, src, after_A in prefetch:
                        out_counts.append(wload(g, region, src))

                @block.sync
                def _(sync):
                    for i, (mc, n0) in enumerate(jobs2):
                        half = (i % 2) * 512
                        sm = "lwa" if i % 2 == 0 else "lwb"
                        if i >= 2:
                            sync.wait_ge(sems.h["ev"], ev_base + i - 1)
                        a0_ = 0 if xh_h is not None else 1
                        sems.inc(sm, sync.dma_start(
                            out=xff[:, half:half + 512],
                            in_=xa_h[mc * 128:(mc + 1) * 128,
                                     a0_ + R0 + n0: a0_ + R0 + n0 + 512]), 16)

                @block.tensor
                def _(tensor):
                    tensor.wait_ge(sems.h["ldw"], cB)
                    for i, (mc, n0) in enumerate(jobs2):
                        half = (i % 2) * 512
                        if i >= 2:
                            tensor.wait_ge(sems.h["ev"], ev_base + i - 1)
                        last = None
                        for kc in range(NH1):
                            reg = regA if kc < HJ else regB
                            kcl = kc if kc < HJ else kc - HJ
                            last = tensor.matmul(
                                PA[:, half:half + 512],
                                reg[:, (kcl * 8 + mc) * 128:(kcl * 8 + mc) * 128 + 128],
                                hb[:, kc * 1024 + n0: kc * 1024 + n0 + 512],
                                start=(kc == 0), stop=(kc == NH1 - 1))
                        sems.inc("mm", last)

                @block.vector
                def _(vector):
                    for i, (mc, n0) in enumerate(jobs2):
                        half = (i % 2) * 512
                        sm = "lwa" if i % 2 == 0 else "lwb"
                        vector.wait_ge(sems.h["mm"], mm_base + i + 1)
                        lbase = lwa_base if i % 2 == 0 else lwb_base
                        vector.wait_ge(sems.h[sm], lbase + 16 * (i // 2 + 1))
                        if i >= 2:
                            sm2 = "st" if i % 2 == 0 else "st2"
                            sbase = st_base if i % 2 == 0 else st2_base
                            vector.wait_ge(sems.h[sm2], sbase + 16 * ((i - 2) // 2 + 1))
                            if rg == 1 and i % 2 == 1:
                                vector.wait_ge(sems.h["sx"],
                                               sx_base + 16 * ((i - 1) // 2))
                        sems.chain(vector, "vch", vector.scalar_tensor_tensor(
                            outb[:, half:half + 512], PA[:, half:half + 512],
                            cols[:, 60 + mc:61 + mc], xff[:, half:half + 512],
                            ALU.add, ALU.add))
                        sems.inc("ev", vector.engine_nop())

                @block.scalar
                def _(scalar):
                    for i, (mc, n0) in enumerate(jobs2):
                        half = (i % 2) * 512
                        sm2 = "st" if i % 2 == 0 else "st2"
                        scalar.wait_ge(sems.h["ev"], ev_base + i + 1)
                        sems.inc(sm2, scalar.dma_start(
                            out=xo_h[mc * 128:(mc + 1) * 128, R0 + n0: R0 + n0 + 512],
                            in_=outb[:, half:half + 512]), 16)
                        if rg == 1 and n0 == 512:
                            sems.inc("sx", scalar.dma_start(
                                out=xt_h[mc * 128:(mc + 1) * 128, :],
                                in_=outb[:, half + 255:half + 512]), 16)
                    scalar.wait_ge(sems.h["st"], sems.n["st"])
                    scalar.wait_ge(sems.h["st2"], sems.n["st2"])
                    scalar.wait_ge(sems.h["sx"], sems.n["sx"])
            return out_counts

        with nc.Block("f_app") as block:
            @block.vector
            def _(vector):
                emit_app(vector, [0, 512, 1024, 1536])

        c = emit_w1_pass(0, "a", [(R3, w1gA, False), (R1, w1gB, True)])
        lc[("g", 0)] = c
        c = emit_w1_pass(0, "g", [(R2, w2A, False), (R3, w2B, True)])
        lc[("w", 0)] = c
        c = emit_w2_pass(0, [(R1, w1aA, False)])
        lc[("a", 1)] = [c[0], ("self", 0)]
        c = emit_w1_pass(1, "a", [(R2, w1aB, False), (R3, w1gA, False),
                                  (R1, w1gB, True)])
        lc[("g", 1)] = [c[1], c[2]]
        c = emit_w1_pass(1, "g", [(R2, w2A, False), (R3, w2B, True)])
        lc[("w", 1)] = c
        emit_w2_pass(1, [])


def _foff0(c):
    return _foff(c, 0)



def build_ffn0_ffn():
    nc = bass.Bass("TRN2", target_bir_lowering=False)
    H = dram_ffn0_ffn(nc)
    with contextlib.ExitStack() as c0:
        sems = Sems(nc, c0, ["ld", "ldg", "ldw", "lwa", "lwb", "ev", "st", "st2", "vch", "sch", "mm", "ex", "bc_mm", "bc_ev", "sx"] + ["lc%d" % i for i in range(8)])
        emit_ffn0_ffn(nc, sems, H)
    return nc


def dram_attn0(nc, sfx="", ext=True):
    k = lambda io: ({"kind": "External" + io} if ext else {})
    return dict(
        xnk=nc.dram_tensor("xnk" + sfx, [1024, 2 * ROWS], F32, **k("Input")),
        wqkv=nc.dram_tensor("wqkv" + sfx, [128, 3 * 1024], F32, **k("Input")),
        rc=nc.dram_tensor("rc" + sfx, [512, 2 * ROWS], F32, **k("Input")),
        rs=nc.dram_tensor("rs" + sfx, [512, 2 * ROWS], F32, **k("Input")),
        tri=nc.dram_tensor("tri" + sfx, [128, 384], F32, **k("Input")),
        vflag=nc.dram_tensor("vflag" + sfx, [1, 1], F32, **k("Input")),
        a0=nc.dram_tensor("a0" + sfx, [ROWS, 1024], BF16, **k("Output")),
    )


def emit_attn0(nc, sems, H):
    xnk_h = H["xnk"]
    wqkv_h = H["wqkv"]
    rc_h = H["rc"]
    rs_h = H["rs"]
    tri_h = H["tri"]
    vflag_h = H["vflag"]
    a0_h = H["a0"]

    NK = 2 * ROWS
    with contextlib.ExitStack() as ctx:
        e = ctx.enter_context
        xtl = e(nc.sbuf_tensor("xtl_s", [128, 8 * 512], BF16))
        wsb = e(nc.sbuf_tensor("wsb_a", [128, 3 * 1024], BF16))
        qhat = e(nc.sbuf_tensor("qhat_a", [128, 8 * ROWS], BF16))
        khat = e(nc.sbuf_tensor("khat_a", [128, 8 * NK], BF16))
        vx = e(nc.sbuf_tensor("vx_a", [128, 32 * 1025], BF16))
        vtmp = e(nc.sbuf_tensor("vtmp_a", [128, 2 * 8 * 512], BF16))
        rta = e(nc.sbuf_tensor("rta_a", [128, 2 * 512], F32))
        rtb = e(nc.sbuf_tensor("rtb_a", [128, 2 * 512], F32))
        bsc = e(nc.sbuf_tensor("bsc_a", [128, 1024], F32))
        sqs = e(nc.sbuf_tensor("sqs_a", [128, 512], BF16))
        nrm = e(nc.sbuf_tensor("nrm_a", [1, 512], F32))
        srow = e(nc.sbuf_tensor("srow_a", [1, 16], F32))
        colbuf = e(nc.sbuf_tensor("colbuf_a", [128, 8], F32))
        tri = e(nc.sbuf_tensor("tri_a", [128, 384], BF16))
        onesr = e(nc.sbuf_tensor("onesr_a", [1, 128], F32))
        ones_colb = e(nc.sbuf_tensor("onescb_a", [128, 1], BF16))
        ptil = e(nc.sbuf_tensor("ptil_a", [128, 512], BF16))
        SA = e(nc.psum_tensor("SA", [128, 512], F32))
        SB = e(nc.psum_tensor("SB", [128, 512], F32))
        PT = e(nc.psum_tensor("PT", [128, 1024], BF16))
        PT2 = e(nc.psum_tensor("PT2", [128, 1024], BF16))
        PV1 = e(nc.psum_tensor("PV1", [128, 1536], F32))
        PM = e(nc.psum_tensor("PM", [128, 512], F32))

        with nc.Block("b_ld") as block:
            @block.sync
            def _(sync):
                sems.inc("ld", sync.dma_start(out=srow[:, 0:1], in_=vflag_h[:, :]), 16)

            @block.gpsimd
            def _(g):
                sems.inc("ldw", g.dma_start(out=tri[:, :], in_=tri_h[:, :]), 16)
                sems.inc("ldw", g.dma_start(out=wsb[:, :], in_=wqkv_h[:, :]), 16)

            @block.vector
            def _(vector):
                vector.memset(onesr[:, :], 1.0)
                vector.memset(ones_colb[:, :], 1.0)
                v0 = vx[:, 0:1]
                sems.chain(vector, "vch", vector.memset(vx[:, :], 0.0))
                vector.memset(bass.AP(v0.tensor, v0.offset + 1024,
                                      [list(v0.ap[0]), [1025, 32], [1, 1]]), 1.0)

        for rt in range(9):
            with nc.Block("b_pj%d" % rt) as block:
                half = 0
                pvh = ((rt - 1) % 2) * 8 * 512
                mm_base = sems.n["mm"]
                ev_base = sems.n["ev"]
                tp_base = sems.n["tp"]
                ev2_base = sems.n["sch"]
                lxa_base = sems.n["lxa"]
                projs = ([0, 1, 2] if rt < 4 else [1, 2]) if rt < 8 else []

                @block.gpsimd
                def _(g):
                    if rt < 8:
                        sems.inc("lxa", g.dma_start(
                            out=xtl[:, 0:8 * 512],
                            in_=xnk_h.ap().rearrange("(c p) n -> p c n", p=128)
                            [:, :, rt * 512:(rt + 1) * 512]), 16)

                @block.tensor
                def _(tensor):
                    if projs:
                        tensor.wait_ge(sems.h["ldw"], sems.n["ldw"])
                        tensor.wait_ge(sems.h["lxa"], lxa_base + 16)
                    nmm = 0
                    for pi, pj in enumerate(projs):
                        for c in range(8):
                            ps = SA if nmm % 2 == 0 else SB
                            nmm += 1
                            if nmm > 2:
                                tensor.wait_ge(sems.h["ev"], ev_base + nmm - 2)
                            last = None
                            for sub in range(2):
                                base = sub * 64
                                last = tensor.matmul(
                                    ps[base:base + 64, 0:512],
                                    wsb[base:base + 64,
                                        pj * 1024 + (c * 2 + sub) * 64:
                                        pj * 1024 + (c * 2 + sub) * 64 + 64],
                                    xtl[base:base + 64, half + c * 512: half + c * 512 + 512],
                                    start=True, stop=True, tile_position=(base, base))
                            sems.inc("mm", last)
                    if rt >= 1:
                        ntp = 0
                        for jb4 in range(4):
                            for cc in range(8):
                                pt = PT if ntp % 2 == 0 else PT2
                                ntp += 1
                                if ntp > 2:
                                    tensor.wait_ge(sems.h["sch"], ev2_base + ntp - 2)
                                sems.inc("tp", tensor.transpose(
                                    pt[:, 0:128],
                                    vtmp[:, pvh + cc * 512 + jb4 * 128:
                                         pvh + cc * 512 + jb4 * 128 + 128],
                                    tri[:, 0:128]))

                @block.scalar
                def _(scalar):
                    nmm = 0
                    for pi, pj in enumerate(projs):
                        for c in range(8):
                            ps = SA if nmm % 2 == 0 else SB
                            nmm += 1
                            scalar.wait_ge(sems.h["mm"], mm_base + nmm)
                            if pj == 0:
                                dst = qhat[:, c * ROWS + rt * 512: c * ROWS + rt * 512 + 512]
                                sc = 1.0 / 32.0
                            elif pj == 1:
                                dst = khat[:, c * NK + rt * 512: c * NK + rt * 512 + 512]
                                sc = 1.0
                            else:
                                dst = vtmp[:, (rt % 2) * 8 * 512 + c * 512:
                                           (rt % 2) * 8 * 512 + c * 512 + 512]
                                sc = 1.0
                            sems.inc("ev", scalar.activation(dst, ps[:, 0:512],
                                                             AF.Copy, scale=sc))
                    if rt >= 1:
                        ntp = 0
                        for jb4 in range(4):
                            for cc in range(8):
                                pt = PT if ntp % 2 == 0 else PT2
                                ntp += 1
                                scalar.wait_ge(sems.h["tp"], tp_base + ntp)
                                jb = (rt - 1) * 4 + jb4
                                sems.inc("sch", scalar.copy(
                                    vx[:, jb * 1025 + cc * 128: jb * 1025 + cc * 128 + 128],
                                    pt[:, 0:128]))

        def emit_norm2(tag, tile, nrows, cell):
            rts = chunked(nrows, 512)
            with nc.Block(tag) as block:
                vch_base = sems.n["vch"]
                mm_base2 = sems.n["mm"]
                sch_base = sems.n["sch"]

                @block.vector
                def _(vector):
                    for ri, (t0, tl) in enumerate(rts):
                        for c in range(8):
                            j = ri * 8 + c
                            if j:
                                vector.wait_ge(sems.h["mm"], mm_base2 + j)
                            sems.inc("vch", vector.tensor_tensor(
                                sqs[:, 0:tl],
                                tile[:, c * nrows + t0: c * nrows + t0 + tl],
                                tile[:, c * nrows + t0: c * nrows + t0 + tl], ALU.mult))
                        vector.wait_ge(sems.h["sch"], sch_base + ri + 1)
                        sems.inc("vch", vector.reduce_max(
                            srow[:, 8:9], nrm[:, 0:tl], axis=mybir.AxisListType.X))
                        vector.wait_ge(sems.h["vch"], sems.n["vch"])
                        sems.inc("vch", vector.tensor_tensor(
                            srow[:, cell:cell + 1], srow[:, cell:cell + 1],
                            srow[:, 8:9], ALU.max))

                @block.tensor
                def _(tensor):
                    for ri, (t0, tl) in enumerate(rts):
                        for c in range(8):
                            tensor.wait_ge(sems.h["vch"], vch_base + 10 * ri + c + 1)
                            if c == 0 and ri > 0:
                                tensor.wait_ge(sems.h["sch"], sch_base + ri)
                            sems.inc("mm", tensor.matmul(
                                PM[0:1, 0:tl], ones_colb[:, :], sqs[:, 0:tl],
                                start=(c == 0), stop=(c == 7)))

                @block.scalar
                def _(scalar):
                    for ri, (t0, tl) in enumerate(rts):
                        scalar.wait_ge(sems.h["mm"], mm_base2 + 8 * (ri + 1))
                        if ri >= 1:
                            scalar.wait_ge(sems.h["vch"], vch_base + 10 * ri)
                        sems.inc("sch", scalar.copy(nrm[:, 0:tl], PM[0:1, 0:tl]))

        with nc.Block("b_m0") as block:
            @block.vector
            def _(vector):
                vector.memset(srow[:, 1:3], 0.0)

        emit_norm2("b_nq", qhat, ROWS, 1)
        emit_norm2("b_nk", khat, NK, 2)

        with nc.Block("b_m") as block:
            @block.vector
            def _(vector):
                sems.chain(vector, "vch", vector.tensor_tensor(
                    srow[:, 3:4], srow[:, 1:2], srow[:, 2:3], ALU.mult))

            @block.scalar
            def _(scalar):
                scalar.wait_ge(sems.h["vch"], sems.n["vch"])
                sems.inc("sch", scalar.activation(srow[:, 4:5], srow[:, 3:4], AF.Sqrt))

            @block.vector
            def _(vector):
                vector.wait_ge(sems.h["sch"], sems.n["sch"])
                sems.chain(vector, "vch", vector.tensor_scalar(
                    srow[:, 5:6], srow[:, 4:5], -1.0, None, ALU.mult, ALU.bypass))

        with nc.Block("b_mbc") as block:
            @block.tensor
            def _(tensor):
                tensor.matmul(PM[:, 0:1], onesr[:, :], srow[:, 5:6], start=True, stop=True)

        with nc.Block("b_mbc2") as block:
            @block.scalar
            def _(scalar):
                scalar.copy(colbuf[:, 0:1], PM[:, 0:1])

        def emit_rope_a(tag, tile, nrows):
            jobs = [(c, t0, tl) for c in range(4) for (t0, tl) in chunked(nrows, 512)]
            with nc.Block(tag) as block:
                ldr_base = sems.n["ldr"]
                ldr2_base = sems.n["ldr2"]
                vch_base0 = sems.n["vch"]

                @block.sync
                def _(sync):
                    for i, (c, t0, tl) in enumerate(jobs):
                        half = (i % 2) * 512
                        sm = "ldr" if i % 2 == 0 else "ldr2"
                        if i >= 2:
                            sync.wait_ge(sems.h["vch"], vch_base0 + 6 * (i - 1))
                        sems.inc(sm, sync.dma_start(
                            out=rta[:, half:half + tl],
                            in_=rc_h[c * 128:(c + 1) * 128, t0:t0 + tl]), 16)
                        sems.inc(sm, sync.dma_start(
                            out=rtb[:, half:half + tl],
                            in_=rs_h[c * 128:(c + 1) * 128, t0:t0 + tl]), 16)

                @block.vector
                def _(vector):
                    for i, (c, t0, tl) in enumerate(jobs):
                        half = (i % 2) * 512
                        sm = "ldr" if i % 2 == 0 else "ldr2"
                        sb = ldr_base if i % 2 == 0 else ldr2_base
                        vector.wait_ge(sems.h[sm], sb + 32 * (i // 2 + 1))
                        x1 = tile[:, c * nrows + t0: c * nrows + t0 + tl]
                        x2 = tile[:, (4 + c) * nrows + t0: (4 + c) * nrows + t0 + tl]
                        co = rta[:, half:half + tl]
                        si = rtb[:, half:half + tl]
                        t1 = bsc[:, 0:tl]
                        t2 = bsc[:, 512:512 + tl]
                        sems.chain(vector, "vch", vector.tensor_tensor(t1, x1, si, ALU.mult))
                        sems.chain(vector, "vch", vector.tensor_tensor(t2, x2, si, ALU.mult))
                        sems.chain(vector, "vch", vector.tensor_tensor(x1, x1, co, ALU.mult))
                        sems.chain(vector, "vch", vector.tensor_tensor(x2, x2, co, ALU.mult))
                        sems.chain(vector, "vch", vector.tensor_tensor(x1, x1, t2, ALU.subtract))
                        sems.chain(vector, "vch", vector.tensor_tensor(x2, x2, t1, ALU.add))

        emit_rope_a("b_ropeq", qhat, ROWS)
        emit_rope_a("b_ropek", khat, NK)

        with nc.Block("b_vf") as block:
            @block.tensor
            def _(tensor):
                tensor.wait_ge(sems.h["ld"], sems.n["ld"])
                tensor.matmul(PM[:, 1:2], onesr[:, :], srow[:, 0:1], start=True, stop=True)

        with nc.Block("b_vf2") as block:
            @block.scalar
            def _(scalar):
                scalar.copy(colbuf[:, 1:2], PM[:, 1:2])

        with nc.Block("b_vf3") as block:
            @block.vector
            def _(vector):
                vector.tensor_scalar(vx[:, 16 * 1025:32 * 1025],
                                     vx[:, 16 * 1025:32 * 1025],
                                     colbuf[:, 1:2], None, ALU.mult, ALU.bypass)

        with nc.Block("b_att") as block:
            mm_base3 = sems.n["mm"]
            ex_base = sems.n["ex"]
            pv_base = sems.n["pv"]
            ev_base3 = sems.n["ev"]
            st_base = sems.n["sg"]
            st2_base = sems.n["sg2"]
            all_batches = []
            for ib in range(NIB):
                chunksl = list(range(0, ib + 1)) + list(range(16, 32))
                all_batches.append([chunksl[i:i + 4] for i in range(0, len(chunksl), 4)])

            flat = []
            for ib in range(NIB):
                for bi, batch in enumerate(all_batches[ib]):
                    flat.append((ib, batch, bi == len(all_batches[ib]) - 1,
                                 bi == 0))

            @block.tensor
            def _(tensor):
                def emit_pv(j):
                    ibj, batchj, lastbj, firstbj = flat[j]
                    tensor.wait_ge(sems.h["ex"], ex_base + j + 1)
                    if firstbj and ibj >= 1:
                        tensor.wait_ge(sems.h["ev"], ev_base3 + ibj)
                    pslot = ptil if j % 2 == 0 else sqs
                    last = None
                    for k, c in enumerate(batchj):
                        lastc = lastbj and (k == len(batchj) - 1)
                        for n0, nl in ((0, 512), (512, 512), (1024, 1)):
                            last = tensor.matmul(
                                PV1[:, n0:n0 + nl],
                                pslot[:, k * 128:k * 128 + 128],
                                vx[:, c * 1025 + n0: c * 1025 + n0 + nl],
                                start=(firstbj and k == 0), stop=lastc)
                    sems.inc("pv", last)

                for nx, (ib, batch, lastb, firstb) in enumerate(flat):
                    ps = SA if (nx % 2 == 0) else SB
                    if nx >= 2:
                        tensor.wait_ge(sems.h["ex"], ex_base + nx - 1)
                    last = None
                    for k, c in enumerate(batch):
                        for cc in range(8):
                            last = tensor.matmul(
                                ps[:, k * 128:k * 128 + 128],
                                khat[:, cc * NK + c * 128: cc * NK + c * 128 + 128],
                                qhat[:, cc * ROWS + ib * 128: cc * ROWS + ib * 128 + 128],
                                start=(cc == 0), stop=(cc == 7 and c != ib))
                        if c == ib:
                            last = tensor.matmul(
                                ps[:, k * 128:k * 128 + 128],
                                tri[:, 256:384], tri[:, 0:128],
                                start=False, stop=True)
                    sems.inc("mm", last)
                    if nx >= 1:
                        emit_pv(nx - 1)
                emit_pv(len(flat) - 1)

            @block.scalar
            def _(scalar):
                for nx, (ib, batch, lastb, firstb) in enumerate(flat):
                    ps = SA if (nx % 2 == 0) else SB
                    pslot = ptil if nx % 2 == 0 else sqs
                    scalar.wait_ge(sems.h["mm"], mm_base3 + nx + 1)
                    if nx >= 2:
                        scalar.wait_ge(sems.h["pv"], pv_base + nx - 1)
                    sems.inc("ex", scalar.activation(
                        pslot[:, 0:128 * len(batch)],
                        ps[:, 0:128 * len(batch)],
                        AF.Exp, bias=colbuf[:, 0:1]))

            @block.vector
            def _(vector):
                npvtot = 0
                for ib in range(NIB):
                    npvtot += len(all_batches[ib])
                    ob = rta if ib % 2 == 0 else rtb
                    vector.wait_ge(sems.h["pv"], pv_base + npvtot)
                    sems.chain(vector, "vch", vector.reciprocal(
                        colbuf[:, 2:3], PV1[:, 1024:1025]))
                    if ib >= 2:
                        sm2 = "sg" if ib % 2 == 0 else "sg2"
                        sbb = st_base if ib % 2 == 0 else st2_base
                        vector.wait_ge(sems.h[sm2], sbb + 16 * ((ib - 2) // 2 + 1))
                    sems.chain(vector, "vch", vector.tensor_scalar(
                        ob[:, 0:1024], PV1[:, 0:1024], colbuf[:, 2:3],
                        None, ALU.mult, ALU.bypass))
                    sems.inc("ev", vector.engine_nop())

            @block.gpsimd
            def _(g):
                for ib in range(NIB):
                    ob = rta if ib % 2 == 0 else rtb
                    sm2 = "sg" if ib % 2 == 0 else "sg2"
                    g.wait_ge(sems.h["ev"], ev_base3 + ib + 1)
                    sems.inc(sm2, g.dma_start(
                        out=a0_h[ib * 128:(ib + 1) * 128, :],
                        in_=ob[:, 0:1024]), 16)
                g.wait_ge(sems.h["sg"], sems.n["sg"])
                g.wait_ge(sems.h["sg2"], sems.n["sg2"])



def build_attn0():
    nc = bass.Bass("TRN2", target_bir_lowering=False)
    H = dram_attn0(nc)
    with contextlib.ExitStack() as c0:
        sems = Sems(nc, c0, ["ld", "ldg", "ldw", "ldr", "ldr2", "lxa", "lxb", "ev", "st", "st2", "vch", "sch", "mm", "ex", "pv", "tp", "sg", "sg2"])
        emit_attn0(nc, sems, H)
    return nc


def dram_combine(nc, sfx="", ext=True):
    k = lambda io: ({"kind": "External" + io} if ext else {})
    return dict(
        x0=nc.dram_tensor("x0" + sfx, [1024, ROWS], F32, **k("Input")),
        a0=nc.dram_tensor("a0" + sfx, [ROWS, 1024], BF16, **k("Input")),
        f0=nc.dram_tensor("f0" + sfx, [1024, ROWS], F32, **k("Input")),
        x1=nc.dram_tensor("x1" + sfx, [1024, ROWS], F32, **k("Output")),
        xn=nc.dram_tensor("xn" + sfx, [1024, ROWS], F32, **k("Output")),
    )


def emit_combine(nc, sems, H):
    x0_h = H["x0"]
    a0_h = H["a0"]
    f0_h = H["f0"]
    x1_h = H["x1"]
    xn_h = H["xn"]
    n = ROWS
    with contextlib.ExitStack() as ctx:
        e = ctx.enter_context
        xs = e(nc.sbuf_tensor("xs", [128, 8 * n], F32))
        ya = e(nc.sbuf_tensor("ya", [128, 8 * n], F32))
        sq = e(nc.sbuf_tensor("sq", [128, 1024], F32))
        stat = e(nc.sbuf_tensor("stat", [1, 2 * n], F32))
        ones_col = e(nc.sbuf_tensor("ones_col", [128, 1], F32))
        onesr = e(nc.sbuf_tensor("onesr", [1, 128], F32))
        eps_col = e(nc.sbuf_tensor("eps_col", [1, 1], F32))
        Ab = e(nc.sbuf_tensor("Ab", [128, n], F32))
        Bb = e(nc.sbuf_tensor("Bb", [128, n], F32))
        ps = e(nc.psum_tensor("ps", [1, n], F32))
        pb2 = e(nc.psum_tensor("pb2", [128, 1024], F32))

        yab = e(nc.sbuf_tensor("yab", [128, 8 * n], BF16))
        with nc.Block("c_ld") as block:
            @block.sync
            def _(sync):
                sems.inc("ld", sync.dma_start(
                    out=xs[:, :], in_=x0_h.ap().rearrange("(c p) n -> p c n", p=128)
                    [:, :, 0:n]), 16)
                for c in range(8):
                    sems.inc("ld", sync.dma_start(
                        out=yab[:, c * n:(c + 1) * n],
                        in_=a0_h[:, c * 128:(c + 1) * 128], transpose=True), 16)

            @block.vector
            def _(vector):
                vector.memset(ones_col[:, :], 1.0)
                vector.memset(onesr[:, :], 1.0)
                vector.memset(eps_col[:, :], 1e-5)

        with nc.Block("c_add1") as block:
            @block.vector
            def _(vector):
                vector.wait_ge(sems.h["ld"], sems.n["ld"])
                for c in range(8):
                    sems.inc("vch", vector.tensor_tensor(
                        xs[:, c * n:(c + 1) * n], xs[:, c * n:(c + 1) * n],
                        yab[:, c * n:(c + 1) * n], ALU.add))

        with nc.Block("c_ld2") as block:
            @block.sync
            def _(sync):
                sems.inc("ld", sync.dma_start(
                    out=ya[:, :], in_=f0_h.ap().rearrange("(c p) n -> p c n", p=128)), 16)

        with nc.Block("c_add2") as block:
            @block.vector
            def _(vector):
                vector.wait_ge(sems.h["ld"], sems.n["ld"])
                for c in range(8):
                    sems.inc("vch", vector.tensor_tensor(
                        xs[:, c * n:(c + 1) * n], xs[:, c * n:(c + 1) * n],
                        ya[:, c * n:(c + 1) * n], ALU.add))

        with nc.Block("c_st1") as block:
            @block.sync
            def _(sync):
                sems.inc("st", sync.dma_start(
                    out=x1_h.ap().rearrange("(c p) n -> p c n", p=128), in_=xs[:, :]), 16)

        emit_ln(nc, sems, xs, n, sq, stat, ones_col, onesr, ps, pb2, Ab, Bb,
                "c_ln", eps_col=eps_col)

        with nc.Block("c_app") as block:
            @block.vector
            def _(vector):
                for c in range(8):
                    sems.chain(vector, "vch", vector.tensor_tensor(
                        ya[:, c * n:(c + 1) * n], xs[:, c * n:(c + 1) * n],
                        Ab[:, :], ALU.mult))
                    vector.tensor_tensor(ya[:, c * n:(c + 1) * n],
                                         ya[:, c * n:(c + 1) * n], Bb[:, :], ALU.add)

        with nc.Block("c_st2") as block:
            @block.sync
            def _(sync):
                sems.inc("st", sync.dma_start(
                    out=xn_h.ap().rearrange("(c p) n -> p c n", p=128), in_=ya[:, :]), 16)
                sync.wait_ge(sems.h["st"], sems.n["st"])





def build_combine():
    nc = bass.Bass("TRN2", target_bir_lowering=False)
    H = dram_combine(nc)
    with contextlib.ExitStack() as c0:
        sems = Sems(nc, c0, ["ld", "ev", "st", "bc_mm", "bc_ev", "vch", "sch", "mm"])
        emit_combine(nc, sems, H)
    return nc


def dram_attn1a(nc, sfx="", ext=True):
    k = lambda io: ({"kind": "External" + io} if ext else {})
    return dict(
        xnk=nc.dram_tensor("xnk" + sfx, [1024, 2 * ROWS], F32, **k("Input")),
        wqkv=nc.dram_tensor("wqkv" + sfx, [1024, 3 * 1024], F32, **k("Input")),
        rtab=nc.dram_tensor("rtab" + sfx, [2 * ROWS, 512], F32, **k("Input")),
        tri=nc.dram_tensor("tri" + sfx, [128, 384], F32, **k("Input")),
        vflag=nc.dram_tensor("vflag" + sfx, [1, 1], F32, **k("Input")),
        at=nc.dram_tensor("at" + sfx, [ROWS, 1024], BF16, **k("Output")),
    )


def emit_attn1a(nc, sems, H):
    xnk_h = H["xnk"]
    wqkv_h = H["wqkv"]
    rtab_h = H["rtab"]
    tri_h = H["tri"]
    vflag_h = H["vflag"]
    at_h = H["at"]

    NK = 2 * ROWS
    NRB = 32
    with contextlib.ExitStack() as ctx:
        e = ctx.enter_context
        xtl = e(nc.sbuf_tensor("xtl_1", [128, 8 * 128], BF16))
        wsb = e(nc.sbuf_tensor("wsb_1", [128, 8 * 1024], BF16))
        qT = e(nc.sbuf_tensor("qT_1", [128, 8 * ROWS], BF16))
        kT = e(nc.sbuf_tensor("kT_1", [128, 8 * NK], BF16))
        vx = e(nc.sbuf_tensor("vx_1", [128, NRB * 1040], BF16))
        rowsc = e(nc.sbuf_tensor("rowsc_1", [128, 1024], BF16))
        rtab = e(nc.sbuf_tensor("rtab_1", [128, 512], F32))
        bsc = e(nc.sbuf_tensor("bsc_1", [128, 512], F32))
        nrm = e(nc.sbuf_tensor("nrm_1", [1, 512], F32))
        srow = e(nc.sbuf_tensor("srow_1", [1, 16], F32))
        colbuf = e(nc.sbuf_tensor("colbuf_1", [128, 8], F32))
        tri = e(nc.sbuf_tensor("tri_1", [128, 384], BF16))
        onesr = e(nc.sbuf_tensor("onesr_1", [1, 128], F32))
        ones_colb = e(nc.sbuf_tensor("onescb_1", [128, 1], BF16))
        ptil = e(nc.sbuf_tensor("ptil_1", [128, 2 * 1024], BF16))
        attnb = e(nc.sbuf_tensor("attnb_1", [128, 4 * 128], BF16))
        sqs = ptil
        SA = e(nc.psum_tensor("SA1", [128, 512], F32))
        SB = e(nc.psum_tensor("SB1", [128, 512], F32))
        SC = e(nc.psum_tensor("SC1", [128, 512], F32))
        SD = e(nc.psum_tensor("SD1", [128, 512], F32))
        PM = e(nc.psum_tensor("PM1", [128, 512], F32))
        PT = e(nc.psum_tensor("PTa", [128, 1024], BF16))
        PT2 = e(nc.psum_tensor("PTb", [128, 1024], BF16))
        PV1 = e(nc.psum_tensor("PVa", [128, 512], F32))

        with nc.Block("d_ld") as block:
            @block.sync
            def _(sync):
                sems.inc("ld", sync.dma_start(out=srow[:, 0:1], in_=vflag_h[:, :]), 16)

            @block.gpsimd
            def _(g):
                sems.inc("ldw", g.dma_start(out=tri[:, :], in_=tri_h[:, :]), 16)

            @block.vector
            def _(vector):
                vector.memset(onesr[:, :], 1.0)
                vector.memset(ones_colb[:, :], 1.0)
                v0 = vx[:, 0:1]
                sems.chain(vector, "vch", vector.memset(vx[:, :], 0.0))
                vector.memset(bass.AP(v0.tensor, v0.offset + 64,
                                      [list(v0.ap[0]), [1040, NRB], [65, 16], [1, 1]]),
                              1.0)

        for pj in range(3):
            with nc.Block("d_w%d" % pj) as block:
                @block.gpsimd
                def _(g):
                    sems.inc("ldw", g.dma_start(
                        out=wsb[:, :], in_=wqkv_h.ap()
                        .rearrange("(kc p) m -> p kc m", p=128)
                        [:, :, pj * 1024:(pj + 1) * 1024]), 16)

            nrb = 16 if pj == 0 else NRB
            for rb in range(nrb):
                with nc.Block("d_p%d_%d" % (pj, rb)) as block:
                    mm_base = sems.n["mm"]
                    ev_base = sems.n["ev"]
                    tp_base = sems.n["tp"]
                    sch_base = sems.n["sch"]
                    vch_base = sems.n["vch"]
                    ldr_base = sems.n["ldr"]
                    lxa_base = sems.n["lxa"]

                    @block.gpsimd
                    def _(g):
                        sems.inc("lxa", g.dma_start(
                            out=xtl[:, :].rearrange("p (c m) -> p c m", c=8),
                            in_=xnk_h.ap().rearrange("(c p) n -> p c n", p=128)
                            [:, :, rb * 128:(rb + 1) * 128]), 16)

                    @block.sync
                    def _(sync):
                        if pj < 2:
                            sems.inc("ldr", sync.dma_start(
                                out=rtab[:, :],
                                in_=rtab_h[rb * 128:(rb + 1) * 128, :]), 16)

                    @block.tensor
                    def _(tensor):
                        tensor.wait_ge(sems.h["ldw"], sems.n["ldw"])
                        tensor.wait_ge(sems.h["lxa"], lxa_base + 16)
                        for i, n0 in enumerate((0, 512)):
                            ps = SA if i == 0 else SB
                            last = None
                            for kc in range(8):
                                last = tensor.matmul(
                                    ps[:, 0:512],
                                    xtl[:, kc * 128:(kc + 1) * 128],
                                    wsb[:, kc * 1024 + n0: kc * 1024 + n0 + 512],
                                    start=(kc == 0), stop=(kc == 7))
                            sems.inc("mm", last)
                        if pj == 2:
                            return
                        tensor.wait_ge(sems.h["vch"], vch_base + 6)
                        for cc in range(8):
                            pt = PT if cc % 2 == 0 else PT2
                            if cc >= 2:
                                tensor.wait_ge(sems.h["sch"], sch_base + cc - 1)
                            sems.inc("tp", tensor.transpose(
                                pt[:, 0:128],
                                rowsc[:, cc * 128:(cc + 1) * 128],
                                tri[:, 0:128]))

                    @block.scalar
                    def _(scalar):
                        for i, n0 in enumerate((0, 512)):
                            ps = SA if i == 0 else SB
                            scalar.wait_ge(sems.h["mm"], mm_base + i + 1)
                            if pj == 2:
                                v0 = vx[:, 0:1]
                                dst = bass.AP(v0.tensor,
                                              v0.offset + rb * 1040 + (n0 // 64) * 65,
                                              [list(v0.ap[0]), [65, 8], [1, 64]])
                                sems.inc("ev", scalar.copy(dst, ps[:, 0:512]))
                            else:
                                sems.inc("ev", scalar.activation(
                                    rowsc[:, n0:n0 + 512], ps[:, 0:512], AF.Copy,
                                    scale=(0.125 if pj == 0 else 1.0)))
                        if pj == 2:
                            return
                        dstT = qT if pj == 0 else kT
                        nlen = ROWS if pj == 0 else NK
                        for cc in range(8):
                            pt = PT if cc % 2 == 0 else PT2
                            scalar.wait_ge(sems.h["tp"], tp_base + cc + 1)
                            sems.inc("sch", scalar.copy(
                                dstT[:, cc * nlen + rb * 128: cc * nlen + rb * 128 + 128],
                                pt[:, 0:128]))

                    @block.vector
                    def _(vector):
                        if pj == 2:
                            return
                        vector.wait_ge(sems.h["ev"], ev_base + 2)
                        vector.wait_ge(sems.h["ldr"], ldr_base + 16)
                        r0 = rowsc[:, 0:1]
                        p0 = list(r0.ap[0])
                        ev_ = bass.AP(r0.tensor, r0.offset, [p0, [64, 16], [2, 16]])
                        od = bass.AP(r0.tensor, r0.offset + 1, [p0, [64, 16], [2, 16]])
                        co = rtab[:, 0:256].rearrange("p (h i) -> p h i", h=16)
                        si = rtab[:, 256:512].rearrange("p (h i) -> p h i", h=16)
                        t1 = bsc[:, 0:256].rearrange("p (h i) -> p h i", h=16)
                        t2 = bsc[:, 256:512].rearrange("p (h i) -> p h i", h=16)
                        sems.chain(vector, "vch", vector.tensor_tensor(t1, ev_, si, ALU.mult))
                        sems.chain(vector, "vch", vector.tensor_tensor(t2, od, si, ALU.mult))
                        sems.chain(vector, "vch", vector.tensor_tensor(ev_, ev_, co, ALU.mult))
                        sems.chain(vector, "vch", vector.tensor_tensor(od, od, co, ALU.mult))
                        sems.chain(vector, "vch", vector.tensor_tensor(ev_, ev_, t2, ALU.subtract))
                        sems.chain(vector, "vch", vector.tensor_tensor(od, od, t1, ALU.add))

        def emit_norm1(tag, tile, nrows, cell):
            rts = chunked(nrows, 512)
            with nc.Block(tag) as block:
                vch_base = sems.n["vch"]
                mm_base2 = sems.n["mm"]
                sch_base = sems.n["sch"]

                @block.vector
                def _(vector):
                    for ri, (t0, tl) in enumerate(rts):
                        for c in range(8):
                            j = ri * 8 + c
                            if j:
                                vector.wait_ge(sems.h["mm"], mm_base2 + j)
                            sems.inc("vch", vector.tensor_tensor(
                                sqs[:, 0:tl],
                                tile[:, c * nrows + t0: c * nrows + t0 + tl],
                                tile[:, c * nrows + t0: c * nrows + t0 + tl], ALU.mult))
                        vector.wait_ge(sems.h["sch"], sch_base + ri + 1)
                        sems.inc("vch", vector.reduce_max(
                            srow[:, 8:9], nrm[:, 0:tl], axis=mybir.AxisListType.X))
                        vector.wait_ge(sems.h["vch"], sems.n["vch"])
                        sems.inc("vch", vector.tensor_tensor(
                            srow[:, cell:cell + 1], srow[:, cell:cell + 1],
                            srow[:, 8:9], ALU.max))

                @block.tensor
                def _(tensor):
                    for ri, (t0, tl) in enumerate(rts):
                        for c in range(8):
                            tensor.wait_ge(sems.h["vch"], vch_base + 10 * ri + c + 1)
                            if c == 0 and ri > 0:
                                tensor.wait_ge(sems.h["sch"], sch_base + ri)
                            sems.inc("mm", tensor.matmul(
                                PM[0:1, 0:tl], ones_colb[:, :], sqs[:, 0:tl],
                                start=(c == 0), stop=(c == 7)))

                @block.scalar
                def _(scalar):
                    for ri, (t0, tl) in enumerate(rts):
                        scalar.wait_ge(sems.h["mm"], mm_base2 + 8 * (ri + 1))
                        if ri >= 1:
                            scalar.wait_ge(sems.h["vch"], vch_base + 10 * ri)
                        sems.inc("sch", scalar.copy(nrm[:, 0:tl], PM[0:1, 0:tl]))

        with nc.Block("d_m0") as block:
            @block.vector
            def _(vector):
                vector.memset(srow[:, 1:3], 0.0)

        emit_norm1("d_nq", qT, ROWS, 1)
        emit_norm1("d_nk", kT, NK, 2)

        with nc.Block("d_m") as block:
            @block.vector
            def _(vector):
                sems.chain(vector, "vch", vector.tensor_tensor(
                    srow[:, 3:4], srow[:, 1:2], srow[:, 2:3], ALU.mult))

            @block.scalar
            def _(scalar):
                scalar.wait_ge(sems.h["vch"], sems.n["vch"])
                sems.inc("sch", scalar.activation(srow[:, 4:5], srow[:, 3:4], AF.Sqrt))

            @block.vector
            def _(vector):
                vector.wait_ge(sems.h["sch"], sems.n["sch"])
                sems.chain(vector, "vch", vector.tensor_scalar(
                    srow[:, 5:6], srow[:, 4:5], -1.0, None, ALU.mult, ALU.bypass))

        with nc.Block("d_mbc") as block:
            @block.tensor
            def _(tensor):
                tensor.wait_ge(sems.h["ld"], sems.n["ld"])
                tensor.matmul(PM[:, 0:1], onesr[:, :], srow[:, 5:6], start=True, stop=True)
                tensor.matmul(PM[:, 1:2], onesr[:, :], srow[:, 0:1], start=True, stop=True)

        with nc.Block("d_mbc2") as block:
            @block.scalar
            def _(scalar):
                scalar.copy(colbuf[:, 0:2], PM[:, 0:2])

        with nc.Block("d_vf") as block:
            @block.vector
            def _(vector):
                vector.tensor_scalar(vx[:, 16 * 1040:32 * 1040],
                                     vx[:, 16 * 1040:32 * 1040],
                                     colbuf[:, 1:2], None, ALU.mult, ALU.bypass)

        all_batches = []
        for ib in range(NIB):
            chunksl = list(range(0, ib + 1)) + list(range(16, 32))
            all_batches.append([chunksl[i:i + 4] for i in range(0, len(chunksl), 4)])

        flat = []
        for ib in range(NIB):
            for bi, batch in enumerate(all_batches[ib]):
                flat.append((ib, batch, bi == len(all_batches[ib]) - 1, bi == 0))
        for hp in range(8):
            with nc.Block("d_at%d" % hp) as block:
                mm_base3 = sems.n["mm"]
                ex_base = sems.n["ex"]
                pv_base = sems.n["pv"]
                ev_base3 = sems.n["ev"]
                stn = ["st", "st2", "st3", "st4"]
                stb = [sems.n[x] for x in stn]

                @block.tensor
                def _(tensor):
                    def emit_pv(j):
                        ibj, batchj, lastbj, firstbj = flat[j]
                        tensor.wait_ge(sems.h["ex"], ex_base + j + 1)
                        if firstbj and ibj >= 1:
                            tensor.wait_ge(sems.h["ev"], ev_base3 + ibj)
                        po = (j % 2) * 1024
                        last = None
                        for k, c in enumerate(batchj):
                            lastc = lastbj and (k == len(batchj) - 1)
                            for hh, poff in ((0, 0), (1, 256)):
                                h = 2 * hp + hh
                                last = tensor.matmul(
                                    PV1[:, poff: poff + 65],
                                    ptil[:, po + hh * 512 + k * 128:
                                         po + hh * 512 + k * 128 + 128],
                                    vx[:, c * 1040 + h * 65: c * 1040 + h * 65 + 65],
                                    start=(firstbj and k == 0 and hh == 0),
                                    stop=(lastc and hh == 1))
                        sems.inc("pv", last)

                    for nx, (ib, batch, lastb, firstb) in enumerate(flat):
                        psA = SA if (nx % 2 == 0) else SC
                        psB = SB if (nx % 2 == 0) else SD
                        if nx >= 2:
                            tensor.wait_ge(sems.h["ex"], ex_base + nx - 1)
                        for k, c in enumerate(batch):
                            dg = (c == ib)
                            tensor.matmul(
                                psA[:, k * 128:k * 128 + 128],
                                kT[0:64, hp * NK + c * 128: hp * NK + c * 128 + 128],
                                qT[0:64, hp * ROWS + ib * 128: hp * ROWS + ib * 128 + 128],
                                start=True, stop=not dg, tile_position=(0, 0))
                            if dg:
                                tensor.matmul(
                                    psA[:, k * 128:k * 128 + 128],
                                    tri[:, 256:384], tri[:, 0:128],
                                    start=False, stop=True)
                            lastB = tensor.matmul(
                                psB[:, k * 128:k * 128 + 128],
                                kT[64:128, hp * NK + c * 128: hp * NK + c * 128 + 128],
                                qT[64:128, hp * ROWS + ib * 128: hp * ROWS + ib * 128 + 128],
                                start=True, stop=not dg, tile_position=(64, 0))
                            if dg:
                                lastB = tensor.matmul(
                                    psB[:, k * 128:k * 128 + 128],
                                    tri[:, 256:384], tri[:, 0:128],
                                    start=False, stop=True)
                        sems.inc("mm", lastB)
                        if nx >= 1:
                            emit_pv(nx - 1)
                    emit_pv(len(flat) - 1)

                @block.scalar
                def _(scalar):
                    for nx, (ib, batch, lastb, firstb) in enumerate(flat):
                        psA = SA if (nx % 2 == 0) else SC
                        psB = SB if (nx % 2 == 0) else SD
                        po = (nx % 2) * 1024
                        scalar.wait_ge(sems.h["mm"], mm_base3 + nx + 1)
                        if nx >= 2:
                            scalar.wait_ge(sems.h["pv"], pv_base + nx - 1)
                        scalar.activation(
                            ptil[:, po:po + 128 * len(batch)],
                            psA[:, 0:128 * len(batch)],
                            AF.Exp, bias=colbuf[:, 0:1])
                        sems.inc("ex", scalar.activation(
                            ptil[:, po + 512:po + 512 + 128 * len(batch)],
                            psB[:, 0:128 * len(batch)],
                            AF.Exp, bias=colbuf[:, 0:1]))

                @block.vector
                def _(vector):
                    npvtot = 0
                    for ib in range(NIB):
                        npvtot += len(all_batches[ib])
                        slot = (ib % 4) * 128
                        vector.wait_ge(sems.h["pv"], pv_base + npvtot)
                        if ib >= 4:
                            vector.wait_ge(sems.h[stn[ib % 4]],
                                           stb[ib % 4] + 16 * ((ib - 4) // 4 + 1))
                        for hh, poff in ((0, 0), (1, 256)):
                            sems.chain(vector, "vch", vector.reciprocal(
                                colbuf[:, 4:5], PV1[:, poff + 64: poff + 65]))
                            sems.chain(vector, "vch", vector.tensor_scalar(
                                attnb[:, slot + hh * 64: slot + hh * 64 + 64],
                                PV1[:, poff: poff + 64],
                                colbuf[:, 4:5], None, ALU.mult, ALU.bypass))
                        sems.inc("ev", vector.engine_nop())

                @block.sync
                def _(sync):
                    for ib in range(NIB):
                        slot = (ib % 4) * 128
                        sync.wait_ge(sems.h["ev"], ev_base3 + ib + 1)
                        sems.inc(stn[ib % 4], sync.dma_start(
                            out=at_h[ib * 128:(ib + 1) * 128,
                                     hp * 128:(hp + 1) * 128],
                            in_=attnb[:, slot:slot + 128]), 16)
                    for x in stn:
                        sync.wait_ge(sems.h[x], sems.n[x])





def build_attn1a():
    nc = bass.Bass("TRN2", target_bir_lowering=False)
    H = dram_attn1a(nc)
    with contextlib.ExitStack() as c0:
        sems = Sems(nc, c0, ["ld", "ldg", "ldw", "ldr", "lxa", "ev", "st", "st2", "st3", "st4", "vch", "sch", "mm", "ex", "pv", "tp"])
        emit_attn1a(nc, sems, H)
    return nc


def dram_attn1b(nc, sfx="", ext=True):
    k = lambda io: ({"kind": "External" + io} if ext else {})
    return dict(
        at=nc.dram_tensor("at" + sfx, [ROWS, 1024], BF16, **k("Input")),
        wout=nc.dram_tensor("wout" + sfx, [1024, 1024], F32, **k("Input")),
        tri=nc.dram_tensor("tri" + sfx, [128, 384], F32, **k("Input")),
        a1=nc.dram_tensor("a1" + sfx, [1024, ROWS], F32, **k("Output")),
    )


def emit_attn1b(nc, sems, H):
    at_h = H["at"]
    wout_h = H["wout"]
    tri_h = H["tri"]
    a1_h = H["a1"]

    with contextlib.ExitStack() as ctx:
        e = ctx.enter_context
        attn = e(nc.sbuf_tensor("attn_b", [128, 16 * 1024], BF16))
        attnT = e(nc.sbuf_tensor("attnT_b", [128, 8 * ROWS], BF16))
        wsb = e(nc.sbuf_tensor("wsb_b", [128, 8 * 1024], BF16))
        tri = e(nc.sbuf_tensor("tri_b", [128, 384], BF16))
        outb = e(nc.sbuf_tensor("outb_b", [128, 2 * 512], F32))
        PT = e(nc.psum_tensor("PTc", [128, 1024], BF16))
        PT2 = e(nc.psum_tensor("PTd", [128, 1024], BF16))
        SA = e(nc.psum_tensor("SAb", [128, 512], F32))
        SB = e(nc.psum_tensor("SBb", [128, 512], F32))

        with nc.Block("g_ld") as block:
            @block.gpsimd
            def _(g):
                sems.inc("ldw", g.dma_start(out=tri[:, :], in_=tri_h[:, :]), 16)
                sems.inc("ldw", g.dma_start(
                    out=attn[:, :],
                    in_=at_h.ap().rearrange("(rb p) m -> p rb m", p=128)), 16)
                sems.inc("ldw", g.dma_start(
                    out=wsb[:, :], in_=wout_h.ap().rearrange("(kc p) m -> p kc m", p=128)), 16)

        with nc.Block("g_tr") as block:
            mm_base = sems.n["mm"]
            ev_base = sems.n["ev"]
            tp_base = sems.n["tp"]
            sch_base = sems.n["sch"]
            jobs = [(rb, cc) for rb in range(16) for cc in range(8)]

            @block.tensor
            def _(tensor):
                tensor.wait_ge(sems.h["ldw"], sems.n["ldw"])
                for i, (rb, cc) in enumerate(jobs):
                    pt = PT if i % 2 == 0 else PT2
                    if i >= 2:
                        tensor.wait_ge(sems.h["sch"], sch_base + i - 1)
                    sems.inc("tp", tensor.transpose(
                        pt[:, 0:128],
                        attn[:, rb * 1024 + cc * 128: rb * 1024 + cc * 128 + 128],
                        tri[:, 0:128]))

            @block.scalar
            def _(scalar):
                for i, (rb, cc) in enumerate(jobs):
                    pt = PT if i % 2 == 0 else PT2
                    scalar.wait_ge(sems.h["tp"], tp_base + i + 1)
                    sems.inc("sch", scalar.copy(
                        attnT[:, cc * ROWS + rb * 128: cc * ROWS + rb * 128 + 128],
                        pt[:, 0:128]))

        with nc.Block("g_wo") as block:
            mm_base = sems.n["mm"]
            ev_base = sems.n["ev"]
            st_base = sems.n["st"]
            st2_base = sems.n["st2"]
            jobs = [(mc, r0) for mc in range(8) for r0 in range(0, ROWS, 512)]

            @block.tensor
            def _(tensor):
                for i, (mc, r0) in enumerate(jobs):
                    ps = SA if i % 2 == 0 else SB
                    if i >= 2:
                        tensor.wait_ge(sems.h["ev"], ev_base + i - 1)
                    last = None
                    for kc in range(8):
                        last = tensor.matmul(
                            ps[:, 0:512],
                            wsb[:, (kc * 8 + mc) * 128:(kc * 8 + mc) * 128 + 128],
                            attnT[:, kc * ROWS + r0: kc * ROWS + r0 + 512],
                            start=(kc == 0), stop=(kc == 7))
                    sems.inc("mm", last)

            @block.vector
            def _(vector):
                for i, (mc, r0) in enumerate(jobs):
                    ps = SA if i % 2 == 0 else SB
                    half = (i % 2) * 512
                    vector.wait_ge(sems.h["mm"], mm_base + i + 1)
                    if i >= 2:
                        sm2 = "st" if i % 2 == 0 else "st2"
                        sbb = st_base if i % 2 == 0 else st2_base
                        vector.wait_ge(sems.h[sm2], sbb + 16 * ((i - 2) // 2 + 1))
                    sems.inc("ev", vector.tensor_copy(outb[:, half:half + 512],
                                                      ps[:, 0:512]))

            @block.scalar
            def _(scalar):
                for i, (mc, r0) in enumerate(jobs):
                    half = (i % 2) * 512
                    sm2 = "st" if i % 2 == 0 else "st2"
                    scalar.wait_ge(sems.h["ev"], ev_base + i + 1)
                    sems.inc(sm2, scalar.dma_start(
                        out=a1_h[mc * 128:(mc + 1) * 128, r0:r0 + 512],
                        in_=outb[:, half:half + 512]), 16)
                scalar.wait_ge(sems.h["st"], sems.n["st"])
                scalar.wait_ge(sems.h["st2"], sems.n["st2"])





def build_attn1b():
    nc = bass.Bass("TRN2", target_bir_lowering=False)
    H = dram_attn1b(nc)
    with contextlib.ExitStack() as c0:
        sems = Sems(nc, c0, ["ld", "ldw", "ev", "st", "st2", "vch", "sch", "mm", "tp"])
        emit_attn1b(nc, sems, H)
    return nc


def dram_ffn1(nc, sfx="", ext=True):
    k = lambda io: ({"kind": "External" + io} if ext else {})
    return dict(
        xn=nc.dram_tensor("xn" + sfx, [1024, ROWS], F32, **k("Input")),
        x1=nc.dram_tensor("x1" + sfx, [1024, ROWS], F32, **k("Input")),
        a1=nc.dram_tensor("a1" + sfx, [1024, ROWS], F32, **k("Input")),
        wt=nc.dram_tensor("wt" + sfx, [1024, 1024], F32, **k("Input")),
        bias=nc.dram_tensor("bias" + sfx, [128, 8], F32, **k("Input")),
        isc=nc.dram_tensor("isc" + sfx, [1, 1], F32, **k("Input")),
        wsc=nc.dram_tensor("wsc" + sfx, [1, 1], F32, **k("Input")),
        out=nc.dram_tensor("out" + sfx, [1024, ROWS], F32, **k("Output")),
    )


def emit_ffn1(nc, sems, H):
    xn_h = H["xn"]
    x1_h = H["x1"]
    a1_h = H["a1"]
    wt_h = H["wt"]
    bias_h = H["bias"]
    isc_h = H["isc"]
    wsc_h = H["wsc"]
    out_h = H["out"]
    n = ROWS
    with contextlib.ExitStack() as ctx:
        e = ctx.enter_context
        xs = e(nc.sbuf_tensor("xs_q", [128, 8 * n], F32))
        x1s = e(nc.sbuf_tensor("x1s_q", [128, 8 * n], F32))
        x8 = e(nc.sbuf_tensor("x8_q", [128, 8 * n], FP8))
        w32 = e(nc.sbuf_tensor("w32_q", [128, 8 * 1024], F32))
        w8 = e(nc.sbuf_tensor("w8_q", [128, 8 * 1024], FP8))
        cols = e(nc.sbuf_tensor("cols_q", [128, 16], F32))
        srow = e(nc.sbuf_tensor("srow_q", [1, 8], F32))
        onesr = e(nc.sbuf_tensor("onesr_q", [1, 128], F32))
        PA = e(nc.psum_tensor("PA_q", [128, 1024], F32))
        PM = e(nc.psum_tensor("PM_q", [128, 512], F32))

        with nc.Block("e_ld") as block:
            @block.sync
            def _(sync):
                sems.inc("ld", sync.dma_start(
                    out=xs[:, :], in_=xn_h.ap().rearrange("(c p) n -> p c n", p=128)), 16)
                sems.inc("ld", sync.dma_start(
                    out=w32[:, :], in_=wt_h.ap().rearrange("(kc p) m -> p kc m", p=128)), 16)
                sems.inc("ld", sync.dma_start(out=cols[:, 8:16], in_=bias_h[:, :]), 16)
                sems.inc("ld", sync.dma_start(out=srow[:, 0:1], in_=isc_h[:, :]), 16)
                sems.inc("ld", sync.dma_start(out=srow[:, 1:2], in_=wsc_h[:, :]), 16)

            @block.vector
            def _(vector):
                vector.memset(onesr[:, :], 1.0)

        with nc.Block("e_sc") as block:
            @block.vector
            def _(vector):
                vector.wait_ge(sems.h["ld"], sems.n["ld"])
                sems.chain(vector, "vch", vector.reciprocal(srow[:, 2:3], srow[:, 0:1]))
                sems.chain(vector, "vch", vector.tensor_tensor(
                    srow[:, 3:4], srow[:, 0:1], srow[:, 1:2], ALU.mult))
                for c in range(8):
                    sems.chain(vector, "vch", vector.tensor_copy(
                        w8[:, c * 1024:(c + 1) * 1024], w32[:, c * 1024:(c + 1) * 1024]))

        with nc.Block("e_bc") as block:
            @block.tensor
            def _(tensor):
                tensor.matmul(PM[:, 0:1], onesr[:, :], srow[:, 2:3], start=True, stop=True)
                tensor.matmul(PM[:, 1:2], onesr[:, :], srow[:, 3:4], start=True, stop=True)

        with nc.Block("e_bc2") as block:
            @block.scalar
            def _(scalar):
                scalar.copy(cols[:, 0:2], PM[:, 0:2])

        with nc.Block("e_q") as block:
            @block.vector
            def _(vector):
                for c in range(8):
                    sl = xs[:, c * n:(c + 1) * n]
                    sems.chain(vector, "vch", vector.tensor_scalar(
                        sl, sl, cols[:, 0:1], 448.0, ALU.mult, ALU.min))
                    sems.chain(vector, "vch", vector.tensor_scalar(
                        sl, sl, -448.0, None, ALU.max, ALU.bypass))
                    sems.chain(vector, "vch", vector.tensor_copy(
                        x8[:, c * n:(c + 1) * n], sl))

        with nc.Block("e_ld2") as block:
            @block.sync
            def _(sync):
                sems.inc("ld", sync.dma_start(
                    out=x1s[:, :], in_=x1_h.ap().rearrange("(c p) n -> p c n", p=128)), 16)

        with nc.Block("e_mm") as block:
            mm_base = sems.n["mm"]
            ev_base = sems.n["ev"]
            jobs = [(mc, r0) for mc in range(8) for r0 in range(0, n, 512)]

            @block.tensor
            def _(tensor):
                for i, (mc, r0) in enumerate(jobs):
                    half = (i % 2) * 512
                    if i >= 2:
                        tensor.wait_ge(sems.h["ev"], ev_base + i - 1)
                    last = None
                    for kc in range(8):
                        last = tensor.matmul(
                            PA[:, half:half + 512],
                            w8[:, (kc * 8 + mc) * 128:(kc * 8 + mc) * 128 + 128],
                            x8[:, kc * n + r0: kc * n + r0 + 512],
                            start=(kc == 0), stop=(kc == 7))
                    sems.inc("mm", last)

            @block.vector
            def _(vector):
                vector.wait_ge(sems.h["ld"], sems.n["ld"])
                for i, (mc, r0) in enumerate(jobs):
                    half = (i % 2) * 512
                    vector.wait_ge(sems.h["mm"], mm_base + i + 1)
                    dst = x1s[:, mc * n + r0: mc * n + r0 + 512]
                    sems.chain(vector, "vch", vector.tensor_scalar(
                        PA[:, half:half + 512] if False else xs[:, mc * n + r0: mc * n + r0 + 512],
                        PA[:, half:half + 512],
                        cols[:, 1:2], cols[:, 8 + mc:9 + mc], ALU.mult, ALU.add))
                    sems.inc("ev", vector.tensor_tensor(
                        dst, dst, xs[:, mc * n + r0: mc * n + r0 + 512], ALU.add))

        with nc.Block("e_ld3") as block:
            @block.sync
            def _(sync):
                sems.inc("ld", sync.dma_start(
                    out=xs[:, :], in_=a1_h.ap().rearrange("(c p) n -> p c n", p=128)), 16)

        with nc.Block("e_out") as block:
            @block.vector
            def _(vector):
                vector.wait_ge(sems.h["ld"], sems.n["ld"])
                for c in range(8):
                    sems.inc("vch", vector.tensor_tensor(
                        x1s[:, c * n:(c + 1) * n], x1s[:, c * n:(c + 1) * n],
                        xs[:, c * n:(c + 1) * n], ALU.add))

        with nc.Block("e_st") as block:
            @block.sync
            def _(sync):
                sems.inc("st", sync.dma_start(
                    out=out_h.ap().rearrange("(c p) n -> p c n", p=128), in_=x1s[:, :]), 16)
                sync.wait_ge(sems.h["st"], sems.n["st"])



def build_ffn1():
    nc = bass.Bass("TRN2", target_bir_lowering=False)
    H = dram_ffn1(nc)
    with contextlib.ExitStack() as c0:
        sems = Sems(nc, c0, ["ld", "ev", "st", "vch", "sch", "mm"])
        emit_ffn1(nc, sems, H)
    return nc




def _tri_consts():
    a = np.arange(128)
    ident = np.eye(128, dtype=np.float32)
    low = np.where(a[None, :] < a[:, None], -1e9, 0.0).astype(np.float32)
    up = np.where(a[None, :] > a[:, None], -1e9, 0.0).astype(np.float32)
    return np.concatenate([ident, low, up], axis=1)


def _ind_consts():
    ind = np.zeros((128, 64), dtype=np.float32)
    indt = np.zeros((16, 512), dtype=np.float32)
    p = np.arange(128)
    for j in range(4):
        for hloc in range(16):
            m = (4 * j + (p // 32)) == hloc
            ind[m, 16 * j + hloc] = 1.0
            indt[hloc, 128 * j + np.where(m)[0]] = 1.0
    return ind, indt


def ffn0_attn_inputs(ins, layer, core, xa_prev_orig):
    b, h = core // 2, core % 2
    own0 = h * ROWS
    own = xa_prev_orig[b, own0:own0 + ROWS, :]
    if h == 0:
        halo = np.zeros((257, D), dtype=np.float32)
    else:
        halo = xa_prev_orig[b, own0 - 257:own0, :]
    xf = np.ascontiguousarray(np.concatenate([halo, own], axis=0).T)
    Wqkv = np.asarray(ins["L0_ffn_Wqkv"][layer])
    wq = np.ascontiguousarray(Wqkv[:, 0:1024][:, INNER_QK_COLS])
    wk = np.ascontiguousarray(Wqkv[:, 1024:2048][:, INNER_QK_COLS])
    wv = np.ascontiguousarray(Wqkv[:, 2048:3072])
    wo = np.ascontiguousarray(np.asarray(ins["L0_ffn_Wo"][layer]))
    qs = np.asarray(ins["L0_ffn_q_scale"][layer])
    ks = np.asarray(ins["L0_ffn_k_scale"][layer])
    p = np.arange(128)
    qsc = np.stack([qs[p % 32], qs[32 + p % 32], ks[p % 32], ks[32 + p % 32]],
                   axis=1).astype(np.float32)
    qsr = np.concatenate([qs, ks])[None, :].astype(np.float32)
    pos_q = np.arange(own0, own0 + ROWS)
    pos_k = np.maximum(np.arange(own0 - 256, own0 + ROWS), 0)
    tq = rope_inner(pos_q)
    tk = rope_inner(pos_k)
    hpo = np.full((1, 128), -1e9 if h == 0 else 0.0, dtype=np.float32)
    ind, indt = _ind_consts()
    return dict(xf=xf, wq=wq, wk=wk, wv=wv, wo=wo, qsc=qsc, qsr=qsr,
                rcq=tq[0], rsq=tq[1], rck=tk[2], rsk=tk[3],
                tri=_tri_consts(), hpo=hpo, ind=ind, indt=indt,
                hflag=np.full((1, 1), 0.0 if h == 0 else 1.0, dtype=np.float32))


def _colify(v, nch):
    out = np.zeros((128, nch), dtype=np.float32)
    v = np.asarray(v)
    for c in range(nch):
        seg = v[c * 128:(c + 1) * 128]
        out[0:len(seg), c] = seg
    return out


def ffn0_ffn_inputs(ins, layer, core, xa_orig):
    b, h = core // 2, core % 2
    own0 = h * ROWS
    own = xa_orig[b, own0:own0 + ROWS, :]
    halo = np.zeros((1, D), dtype=np.float32) if h == 0 else xa_orig[b, own0 - 1:own0, :]
    xa = np.ascontiguousarray(np.concatenate([halo, own], axis=0).T)
    W1 = np.asarray(ins["L0_ffn_W1"][layer])
    W2 = np.asarray(ins["L0_ffn_W2"][layer])
    b1 = np.asarray(ins["L0_ffn_b1"][layer])
    w1a = np.zeros((1024, 2816), dtype=np.float32)
    w1g = np.zeros((1024, 2816), dtype=np.float32)
    w1a[:, 0:2730] = W1[:, 0:2730]
    w1g[:, 0:2730] = W1[:, 2730:5460]
    w2 = np.zeros((2816, 1024), dtype=np.float32)
    w2[0:2730, :] = W2
    return dict(
        xa=xa, w1a=w1a, w1g=w1g, w2=w2,
        lng=_colify(ins["L0_ffn_ln_g"][layer], 8),
        lnb=_colify(ins["L0_ffn_ln_b"][layer], 8),
        b1a=_colify(b1[0:2730], 22), b1g=_colify(b1[2730:5460], 22),
        b2=_colify(ins["L0_ffn_b2"][layer], 8),
        hflag=np.full((1, 1), 0.0 if h == 0 else 1.0, dtype=np.float32))


def attn0_inputs(ins, core, xn_rho):
    b, h = core // 2, core % 2
    own0 = h * ROWS
    own = xn_rho[b, own0:own0 + ROWS, :]
    par = xn_rho[b, ROWS - own0:2 * ROWS - own0, :]
    xnk = np.ascontiguousarray(np.concatenate([own, par], axis=0).T)
    wq = np.asarray(ins["L0_attn_w_q"])
    wk = np.asarray(ins["L0_attn_w_k"])
    wv = np.asarray(ins["L0_attn_w_v"])

    def lay(w):
        out = np.zeros((64, 1024), dtype=np.float32)
        for c in range(8):
            for sub in range(2):
                H = (4 * c + 2 * sub) if c < 4 else (4 * (c - 4) + 2 * sub + 1)
                out[:, (c * 2 + sub) * 64:(c * 2 + sub) * 64 + 64] = w[:, :, H]
        return out
    wqkv = np.concatenate([lay(wq), lay(wk), lay(wv)], axis=1)
    wqkv = np.concatenate([wqkv, wqkv], axis=0)
    pos = np.concatenate([np.arange(own0, own0 + ROWS),
                          np.arange(ROWS - own0, 2 * ROWS - own0)])
    rc, rs = rope_l0attn(pos)
    vflag = np.full((1, 1), 1.0 if h == 1 else 0.0, dtype=np.float32)
    return dict(xnk=xnk, wqkv=wqkv, rc=rc, rs=rs, tri=_tri_consts(), vflag=vflag)

ALLSEMS = ["ld", "ldg", "ldw", "ldr", "ldr2", "ldx", "lwa", "lwb", "lxa", "lxb",
           "ev", "st", "st2", "st3", "st4", "vch", "sch", "mm", "ex", "pv",
           "tp", "bc_mm", "bc_ev", "sg", "sg2", "gl", "sx", "rch"] + ["lc%d" % i for i in range(8)] + \
          ["cc%d" % i for i in range(13)]

I32 = mybir.dt.int32


class _NCShim:

    def __init__(self, nc, sfx, repl):
        self._nc, self._sfx, self._repl = nc, sfx, repl

    def dram_tensor(self, name, shape, dt, **kw):
        base = name[:len(name) - len(self._sfx)] if self._sfx else name
        r = self._repl.get(base)
        if r is not None:
            return r
        return self._nc.dram_tensor(name, shape, dt, **kw)


class _UniqNC:

    def __init__(self, nc, tag):
        self._nc, self._tag = nc, tag

    def sbuf_tensor(self, name, *a, **k):
        return self._nc.sbuf_tensor(name + self._tag, *a, **k)

    def psum_tensor(self, name, *a, **k):
        return self._nc.psum_tensor(name + self._tag, *a, **k)

    def Block(self, name=None, *a, **k):
        if name is None:
            return self._nc.Block(*a, **k)
        return self._nc.Block(name + self._tag, *a, **k)

    def __getattr__(self, x):
        return getattr(self._nc, x)


def _glue(nc, sems, tag, copies, collective=None, ccname=None):
    if collective is not None:
        cin, cout = collective
        with nc.Block(tag + "_a") as block:
            @block.gpsimd
            def _(g):
                g.collective_compute(
                    "AllGather", ALU.bypass,
                    replica_groups=[[0, 1], [2, 3], [4, 5], [6, 7]],
                    ins=[cin], outs=[cout]).then_inc(sems.h[ccname], 1)
                sems.n[ccname] += 1
                g.wait_ge(sems.h[ccname], sems.n[ccname])

    if copies:
        with nc.Block(tag + "_c") as block:
            @block.sync
            def _(sync):
                for dst, src in copies:
                    sems.inc("gl", sync.dma_start(
                        out=dst, in_=src, allow_slow_non_contiguous=True), 16)
                sync.wait_ge(sems.h["gl"], sems.n["gl"])


def _ln0full(nc, sems, x0f, xn0f):
    n = ROWS
    with contextlib.ExitStack() as ctx:
        e = ctx.enter_context
        xs = e(nc.sbuf_tensor("z_xs", [128, 8 * n], F32))
        sq = e(nc.sbuf_tensor("z_sq", [128, 1024], F32))
        stat = e(nc.sbuf_tensor("z_stat", [1, 2 * n], F32))
        ones_col = e(nc.sbuf_tensor("z_oc", [128, 1], F32))
        onesr = e(nc.sbuf_tensor("z_or", [1, 128], F32))
        eps_col = e(nc.sbuf_tensor("z_eps", [1, 1], F32))
        Ab = e(nc.sbuf_tensor("z_Ab", [128, n], F32))
        Bb = e(nc.sbuf_tensor("z_Bb", [128, n], F32))
        ps = e(nc.psum_tensor("z_ps", [1, n], F32))
        pb2 = e(nc.psum_tensor("z_pb2", [128, 1024], F32))

        for hf in range(2):
            with nc.Block("z_ld%d" % hf) as block:
                @block.sync
                def _(sync):
                    sems.inc("ld", sync.dma_start(
                        out=xs[:, :],
                        in_=x0f.ap().rearrange("(c p) n -> p c n", p=128)
                        [:, :, hf * n:(hf + 1) * n]), 16)
                    sync.wait_ge(sems.h["ld"], sems.n["ld"])

                @block.vector
                def _(vector):
                    if hf == 0:
                        vector.memset(ones_col[:, :], 1.0)
                        vector.memset(onesr[:, :], 1.0)
                        vector.memset(eps_col[:, :], 1e-5)

            emit_ln(nc, sems, xs, n, sq, stat, ones_col, onesr, ps, pb2, Ab, Bb,
                    "z_ln%d" % hf, eps_col=eps_col)

            with nc.Block("z_app%d" % hf) as block:
                @block.vector
                def _(vector):
                    for c in range(8):
                        sems.chain(vector, "vch", vector.tensor_tensor(
                            xs[:, c * n:(c + 1) * n], xs[:, c * n:(c + 1) * n],
                            Ab[:, :], ALU.mult))
                        vector.tensor_tensor(
                            xs[:, c * n:(c + 1) * n], xs[:, c * n:(c + 1) * n],
                            Bb[:, :], ALU.add)

            with nc.Block("z_st%d" % hf) as block:
                @block.sync
                def _(sync):
                    sems.inc("st", sync.dma_start(
                        out=xn0f.ap().rearrange("(c p) n -> p c n", p=128)
                        [:, :, hf * n:(hf + 1) * n], in_=xs[:, :]), 16)
                    sync.wait_ge(sems.h["st"], sems.n["st"])


def _perm_copies(dst, dcol, dwid, src, scol, swid, ncols, rho_to_orig):
    out = []
    d0 = dst[0:1, 0:1]
    s0 = src[0:1, 0:1]
    for h in range(16):
        r = (h % 2) * 512 + (h // 2) * 64
        if rho_to_orig:
            dap = bass.AP(d0.tensor, d0.offset + h * dwid + dcol,
                          [[16 * dwid, 64], [1, ncols]])
            sap = bass.AP(s0.tensor, s0.offset + r * swid + scol,
                          [[swid, 64], [1, ncols]])
        else:
            dap = bass.AP(d0.tensor, d0.offset + r * dwid + dcol,
                          [[dwid, 64], [1, ncols]])
            sap = bass.AP(s0.tensor, s0.offset + h * swid + scol,
                          [[16 * swid, 64], [1, ncols]])
        out.append((dap, sap))
    return out


def build_fused(dbg=()):
    nc = bass.Bass("TRN2", target_bir_lowering=False)

    def it(name, shape, dt):
        kw = {"kind": "ExternalOutput"} if name in dbg else {}
        return nc.dram_tensor(name, shape, dt, **kw)

    x0f = nc.dram_tensor("x0f", [1024, 2 * ROWS], F32, kind="ExternalInput")
    cfg = nc.dram_tensor("cfg", [1, 1], I32, kind="ExternalInput")
    out_h = nc.dram_tensor("out", [1024, ROWS], F32, kind="ExternalOutput")
    xn0f = it("xn0f", [1024, 2 * ROWS], F32)
    a0d = it("a0d", [ROWS, 1024], BF16)
    xf0d = it("xf0d", [1024, CS], F32)
    xas = [it("xad%d" % i, [1024, ROWS], F32) for i in range(6)]
    xos = [it("xod%d" % i, [1024, ROWS], F32) for i in range(6)]
    f0r = it("f0r", [1024, ROWS], F32)
    x1d = it("x1d", [1024, ROWS], F32)
    xn1d = it("xn1d", [1024, ROWS], F32)
    xnk1 = it("xnk1", [1024, 2 * ROWS], F32)
    atd = it("atd", [ROWS, 1024], BF16)
    a1d = it("a1d", [1024, ROWS], F32)
    cch = nc.dram_tensor("cch", [1024, 257], F32)
    ccho = nc.dram_tensor("ccho", [2048, 257], F32)
    cc1 = nc.dram_tensor("cc1", [1024, 1], F32)
    cc1o = nc.dram_tensor("cc1o", [2048, 1], F32)
    ccx = nc.dram_tensor("ccx", [1024, ROWS], F32)
    ccxo = nc.dram_tensor("ccxo", [2048, ROWS], F32)

    hfl = nc.dram_tensor("hflag", [1, 1], F32, kind="ExternalInput")
    Ha0 = dram_attn0(_NCShim(nc, "", dict(xnk=xn0f, a0=a0d)))
    Hffa = []
    Hfff = []
    for l in range(6):
        sfx = "_%d" % l
        H = dram_ffn0_attn(_NCShim(nc, sfx, dict(
            xf=(xf0d if l == 0 else xos[l - 1]), xa=xas[l], hflag=hfl,
            xl=cc1)), sfx=sfx)
        if l > 0:
            H["xsrc"] = xos[l - 1]
            H["xh257"] = ccho
        Hffa.append(H)
        H = dram_ffn0_ffn(_NCShim(nc, sfx, dict(
            xa=xas[l], xo=xos[l], hflag=hfl, xt=cch)), sfx=sfx)
        H["xh"] = cc1o
        Hfff.append(H)
    Hc = dram_combine(_NCShim(nc, "", dict(
        x0=x0f, a0=a0d, f0=f0r, x1=x1d, xn=xn1d)))
    H1a = dram_attn1a(_NCShim(nc, "_x1", dict(xnk=xnk1, at=atd)), sfx="_x1")
    H1b = dram_attn1b(_NCShim(nc, "_1b", dict(at=atd, a1=a1d)), sfx="_1b")
    H1f = dram_ffn1(_NCShim(nc, "_f1", dict(
        xn=xn1d, x1=x1d, a1=a1d, out=out_h)), sfx="_f1")

    with contextlib.ExitStack() as c0:
        sems = Sems(nc, c0, ALLSEMS)
        greg = c0.enter_context(nc.gpsimd.register("greg"))
        gcfg = c0.enter_context(nc.sbuf_tensor("gcfg", [1, 1], I32))

        _ln0full(nc, sems, x0f, xn0f)
        emit_attn0(_UniqNC(nc, "_ua0"), sems, Ha0)

        _glue(nc, sems, "g_xf0",
              _perm_copies(xf0d, 0, CS, xn0f, 2 * ROWS - 257, 2 * ROWS,
                           257, True) +
              _perm_copies(xf0d, 257, CS, xn0f, 0, 2 * ROWS, ROWS, True))
        for l in range(6):
            emit_ffn0_attn(_UniqNC(nc, "_ufa%d" % l), sems, Hffa[l])
            _glue(nc, sems, "g_xg%d" % l, [],
                  collective=(cc1.ap().opt(), cc1o.ap().opt()),
                  ccname="cc%d" % (2 * l))
            emit_ffn0_ffn(_UniqNC(nc, "_uff%d" % l), sems, Hfff[l])
            if l < 5:
                _glue(nc, sems, "g_xf%d" % (l + 1), [],
                      collective=(cch.ap().opt(), ccho.ap().opt()),
                      ccname="cc%d" % (2 * l + 1))

        _glue(nc, sems, "g_f0",
              _perm_copies(f0r, 0, ROWS, xos[5], 0, ROWS, ROWS, False))
        emit_combine(_UniqNC(nc, "_uc"), sems, Hc)

        _glue(nc, sems, "g_x1", [(ccx[:, :], xn1d[:, :])])
        _glue(nc, sems, "g_x1b", [(xnk1[:, 0:ROWS], xn1d[:, :])],
              collective=(ccx.ap().opt(), ccxo.ap().opt()), ccname="cc12")
        with nc.Block("g_x1c") as block:
            @block.gpsimd
            def _(g):
                sems.inc("ldg", g.dma_start(out=gcfg[:, :], in_=cfg[:, :]), 16)
                g.wait_ge(sems.h["ldg"], sems.n["ldg"])
                g.reg_load(greg, gcfg[:1, :1])
                sems.inc("ldg", g.dma_start(
                    out=xnk1[:, ROWS:2 * ROWS],
                    in_=bass.AP(ccxo, greg, [[ROWS, 1024], [1, ROWS]])), 16)
                g.wait_ge(sems.h["ldg"], sems.n["ldg"])

        emit_attn1a(_UniqNC(nc, "_u1a"), sems, H1a)
        emit_attn1b(_UniqNC(nc, "_u1b"), sems, H1b)
        emit_ffn1(_UniqNC(nc, "_uf1"), sems, H1f)
    return nc


_W_KEYS = dict(
    attn0=("wqkv", "rc", "rs", "tri", "vflag"),
    ffn0_attn=("wq", "wk", "wv", "wo", "qsc", "qsr", "rcq", "rsq", "rck", "rsk",
               "tri", "hpo", "ind", "indt"),
    ffn0_ffn=("w1a", "w1g", "w2", "lng", "lnb", "b1a", "b1g", "b2"),
    attn1a=("wqkv", "rtab", "tri", "vflag"),
    attn1b=("wout", "tri"),
    ffn1=("wt", "bias", "isc", "wsc"),
)


def fused_inputs(ins, core, x0):
    b, h = core // 2, core % 2
    own0 = h * ROWS
    rows_of = x0[b][:, P2C]
    x0f = np.ascontiguousarray(np.concatenate(
        [rows_of[own0:own0 + ROWS], rows_of[ROWS - own0:2 * ROWS - own0]]).T)
    m = dict(x0f=x0f,
             cfg=np.array([[(1 - h) * 1024 * ROWS]], dtype=np.int32),
             hflag=np.full((1, 1), 1.0 if h == 1 else 0.0, dtype=np.float32))
    dummy = np.zeros((B, T, D), dtype=np.float32)
    src = attn0_inputs(ins, core, dummy)
    for k in _W_KEYS["attn0"]:
        m[k] = src[k]
    for l in range(6):
        src = ffn0_attn_inputs(ins, l, core, dummy)
        for k in _W_KEYS["ffn0_attn"]:
            m[k + "_%d" % l] = src[k]
        src = ffn0_ffn_inputs(ins, l, core, dummy)
        for k in _W_KEYS["ffn0_ffn"]:
            m[k + "_%d" % l] = src[k]
    src = attn1a_inputs(ins, core, dummy)
    for k in _W_KEYS["attn1a"]:
        m[k + "_x1"] = src[k]
    src = attn1b_inputs(ins, core, np.zeros((ROWS, 1024), dtype=np.float32))
    for k in _W_KEYS["attn1b"]:
        m[k + "_1b"] = src[k]
    src = ffn1_inputs(ins, core, None, None, None)
    for k in _W_KEYS["ffn1"]:
        m[k + "_f1"] = src[k]
    return m

def _l1_cols():
    q, k, v = [], [], []
    for h in range(16):
        base = 384 * (h // 2) + 64 * (h % 2)
        q += list(range(base, base + 64))
        k += list(range(base + 128, base + 192))
        v += list(range(base + 256, base + 320))
    return np.array(q), np.array(k), np.array(v)


def attn1a_inputs(ins, core, xn1_rho):
    b, h = core // 2, core % 2
    own0 = h * ROWS
    own = xn1_rho[b, own0:own0 + ROWS, :]
    par = xn1_rho[b, ROWS - own0:2 * ROWS - own0, :]
    xnk = np.ascontiguousarray(np.concatenate([own, par], axis=0).T)
    qc, kc, vc = _l1_cols()
    w = np.asarray(ins["L1_attn_w_qkv"])[P2C, :]
    wqkv = np.ascontiguousarray(np.concatenate(
        [w[:, qc], w[:, kc], w[:, vc]], axis=1))
    pos = np.concatenate([np.arange(own0, own0 + ROWS),
                          np.arange(ROWS - own0, 2 * ROWS - own0)]).astype(np.float64)
    j = np.arange(16, dtype=np.float64)
    inv = (1.0 / 10000.0) ** (2.0 * j / 32.0)
    ang = pos[:, None] * inv[None, :]
    rtab = np.concatenate([np.tile(np.cos(ang), (1, 16)),
                           np.tile(np.sin(ang), (1, 16))], axis=1).astype(np.float32)
    vflag = np.full((1, 1), 1.0 if h == 1 else 0.0, dtype=np.float32)
    return dict(xnk=xnk, wqkv=wqkv, rtab=rtab, tri=_tri_consts(), vflag=vflag)


def attn1b_inputs(ins, core, at):
    wout = np.ascontiguousarray(np.asarray(ins["L1_attn_w_out"])[:, P2C])
    return dict(at=at, wout=wout, tri=_tri_consts())


def ffn1_inputs(ins, core, xn1_sh, x1_sh, a1_sh):
    w = np.asarray(ins["L1_ffn_weight_fp8"])
    wt = np.ascontiguousarray(w[np.ix_(P2C, P2C)].T)
    bias = _colify(np.asarray(ins["L1_ffn_bias"])[P2C], 8)
    return dict(xn=xn1_sh, x1=x1_sh, a1=a1_sh, wt=wt, bias=bias,
                isc=np.asarray(ins["L1_ffn_input_scale"], dtype=np.float32).reshape(1, 1),
                wsc=np.asarray(ins["L1_ffn_weight_scale"], dtype=np.float32).reshape(1, 1))


def _gather_rho(shards):
    out = np.empty((B, T, D), dtype=np.float32)
    for core in range(NCORE):
        b, h = core // 2, core % 2
        out[b, h * ROWS:(h + 1) * ROWS, :] = np.asarray(shards[core]).T
    return out


_NCS = {}


def _get_nc(name):
    if name not in _NCS:
        _NCS[name] = globals()["build_" + name]()
    return _NCS[name]


def _run(name, in_maps):
    nc = _get_nc(name)
    return run_bass_kernel_spmd(nc, in_maps, core_ids=list(range(NCORE))).results


def kernel(**inputs):
    ins = inputs
    x0 = np.asarray(ins["L0_attn_x"], dtype=np.float32)
    res = _run("fused", [fused_inputs(ins, c, x0) for c in range(NCORE)])
    return unshard_rows([r["out"] for r in res], P2C)


def kernel_unfused(**inputs):
    ins = inputs
    x0 = np.asarray(ins["L0_attn_x"], dtype=np.float32)

    res = _run("ln0", [{"x": s} for s in shard_rows(x0, P2C)])
    xn0_rho = _gather_rho([r["xn"] for r in res])
    xn0_orig = xn0_rho[:, :, C2P]

    res = _run("attn0", [attn0_inputs(ins, c, xn0_rho) for c in range(NCORE)])
    a0_sh = [np.asarray(r["a0"]) for r in res]

    x = xn0_orig
    for layer in range(6):
        res = _run("ffn0_attn", [ffn0_attn_inputs(ins, layer, c, x) for c in range(NCORE)])
        xa = _gather_rho([r["xa"] for r in res])
        res = _run("ffn0_ffn", [ffn0_ffn_inputs(ins, layer, c, xa) for c in range(NCORE)])
        x = _gather_rho([r["xo"] for r in res])
    f0_rho = x[:, :, P2C]

    x0_sh = shard_rows(x0, P2C)
    f0_sh = shard_rows(f0_rho)
    res = _run("combine", [dict(x0=x0_sh[c], a0=a0_sh[c], f0=f0_sh[c])
                           for c in range(NCORE)])
    x1_sh = [np.asarray(r["x1"]) for r in res]
    xn1_sh = [np.asarray(r["xn"]) for r in res]
    xn1_rho = _gather_rho(xn1_sh)

    res = _run("attn1a", [attn1a_inputs(ins, c, xn1_rho) for c in range(NCORE)])
    ats = [np.asarray(r["at"]) for r in res]
    res = _run("attn1b", [attn1b_inputs(ins, c, ats[c]) for c in range(NCORE)])
    a1_sh = [np.asarray(r["a1"]) for r in res]

    res = _run("ffn1", [ffn1_inputs(ins, c, xn1_sh[c], x1_sh[c], a1_sh[c])
                        for c in range(NCORE)])
    out = unshard_rows([r["out"] for r in res], P2C)
    return out
